# version 05
# baseline (speedup 1.0000x reference)
import numpy as np
import concourse.bass as bass
import concourse.mybir as mybir
from concourse.bass_utils import run_bass_kernel_spmd

F32 = mybir.dt.float32
BF16 = mybir.dt.bfloat16
AF = mybir.ActivationFunctionType
ALU = mybir.AluOpType
AX = mybir.AxisListType

P = 128
T = 4096
D = 1024
D2 = 2048
NST = 32
RK = 64
CH = 512
NCH = T // CH
WIN = 1024
NWIN = T // WIN
CPW = WIN // CH
ND2 = D2 // P
NKD = D // P
NQB = D // NST

ENGS = ("sp", "pq", "pe", "act", "dve", "pool")


class Em:
    def __init__(self):
        self.ops = []
        self.n = {e: 0 for e in ENGS}
        self.writer = {}
        self.readers = {}
        self.serial = False

    def op(self, eng, fn, reads=(), writes=(), ser_tag=None):
        waits = {}
        ser = False
        if self.serial:
            if self.serial is True:
                ser = True
            elif isinstance(self.serial, str) and self.serial.startswith("except:"):
                ser = (ser_tag or "") not in self.serial[7:].split(",")
            elif eng in self.serial or (ser_tag and ser_tag in self.serial):
                ser = True
        if ser:
            for e2 in ENGS:
                ns = self.n[e2] - (1 if e2 == eng else 0)
                if ns > 0:
                    waits[e2] = ns - 1

        def need(dep):
            if dep is None:
                return
            e, s = dep
            if e == eng:
                if e == "pe":
                    return
            if e == "pool" and eng == "pq":
                return
            if waits.get(e, -1) < s:
                waits[e] = s

        for r in reads:
            need(self.writer.get(r))
        for w in writes:
            need(self.writer.get(w))
            for rd in self.readers.get(w, ()):
                need(rd)
        seq = self.n[eng]
        self.ops.append((eng, fn, waits, seq))
        self.n[eng] += 1
        me = (eng, seq)
        for w in writes:
            self.writer[w] = me
            self.readers[w] = []
        for r in reads:
            self.readers.setdefault(r, []).append(me)

    def finalize(self):
        needed = {e: set() for e in ENGS}
        for eng, fn, waits, seq in self.ops:
            for e, s in waits.items():
                needed[e].add(s)
        needed["sp"] = set(range(self.n["sp"]))
        needed["pq"] = set(range(self.n["pq"]))
        self.rank = {}
        self.ninc = {}
        for e in ENGS:
            cnt = 0
            rm = {}
            for s in range(self.n[e]):
                if s in needed[e]:
                    cnt += 1
                rm[s] = cnt
            self.rank[e] = rm
            self.ninc[e] = cnt
        self.needed = needed


def build_core_program(debug=False):
    nc = bass.Bass("TRN2", target_bir_lowering=False)
    spk = "ExternalOutput" if debug else "Internal"
    dumps = (debug is True)

    def din(name, shape, dt=F32):
        return nc.dram_tensor(name, list(shape), dt, kind="ExternalInput").ap()

    x_in = din("x", (T, D))
    w_in_g = din("w_in_g", (32, 8, P, P))
    w_out_g = din("w_out_g", (16, P, D))
    w_xp_g = din("w_xp_g", (16, P, P))
    w_dtp = din("w_dtp", (RK, D2))
    w_ffn = din("w_ffn", (D, D))
    b_ffn = din("b_ffn", (1, D))
    nrmw = din("nrmw", (1, D))
    conv_w_l = din("conv_w_l", (P, ND2 * 4))
    conv_b_l = din("conv_b_l", (P, ND2))
    dtb_l = din("dtb_l", (P, ND2))
    d_l = din("d_l", (P, ND2))
    alog_l = din("alog_l", (P, ND2 * NST))

    out_d = nc.dram_tensor("out", [T, D], F32, kind="ExternalOutput").ap()
    x1_sp = nc.dram_tensor("x1_sp", [T, D], F32, kind=spk).ap() if dumps else None
    xn1_sp = nc.dram_tensor("xn1_sp", [T, D], F32, kind=spk).ap() if dumps else None
    xd_sp = nc.dram_tensor("xd_sp", [T, D], F32, kind=spk).ap() if dumps else None
    wdt_sp = nc.dram_tensor("wdt_sp", [P, 8 * D], F32, kind=spk).ap() if dumps else None
    tb_sp = nc.dram_tensor("tb_sp", [P, 8 * CH], F32, kind=spk).ap() if dumps else None

    win_bf = nc.dram_tensor("win_bf", [32, P, 8 * P], BF16, kind="Internal").ap()
    wout_bf = nc.dram_tensor("wout_bf", [16, P, D], BF16, kind="Internal").ap()
    dt_sp = nc.dram_tensor("dt_sp", [ND2, P, T], F32, kind=spk).ap()
    dtu_sp = nc.dram_tensor("dtu_sp", [ND2, P, T], F32, kind=spk).ap()
    us_sp = nc.dram_tensor("us_sp", [ND2, P, T], F32, kind=spk).ap()
    sz_sp = nc.dram_tensor("sz_sp", [ND2, P, T], F32, kind=spk).ap()
    y3_sp = nc.dram_tensor("y3_sp", [ND2, P, T], BF16, kind=spk).ap()
    b_sp = nc.dram_tensor("b_sp", [NST, T], F32, kind=spk).ap()
    c_sp = nc.dram_tensor("c_sp", [NST, T], F32, kind=spk).ap()

    em = Em()
    SP, PQ, PE, ACT, DVE, POOL = ENGS

    import contextlib
    with contextlib.ExitStack() as _st:
        ident = _st.enter_context(nc.sbuf_tensor("ident", [P, P], F32))
        ones1 = _st.enter_context(nc.sbuf_tensor("ones1", [1, P], F32))
        wdT = _st.enter_context(nc.sbuf_tensor("wdT", [P, 8 * D], BF16))
        xpT = _st.enter_context(nc.sbuf_tensor("xpT", [P, 16 * P], BF16))
        dtpT = _st.enter_context(nc.sbuf_tensor("dtpT", [RK, D2], BF16))
        a_t = _st.enter_context(nc.sbuf_tensor("a_t", [P, ND2 * NST], F32))
        convw = _st.enter_context(nc.sbuf_tensor("convw", [P, ND2 * 4], F32))
        convb = _st.enter_context(nc.sbuf_tensor("convb", [P, ND2], F32))
        dtb = _st.enter_context(nc.sbuf_tensor("dtb", [P, ND2], F32))
        dpar = _st.enter_context(nc.sbuf_tensor("dpar", [P, ND2], F32))
        nwb = _st.enter_context(nc.sbuf_tensor("nwb", [P, D], F32))
        bfb = _st.enter_context(nc.sbuf_tensor("bfb", [P, D], F32))
        kcol = _st.enter_context(nc.sbuf_tensor("kcol", [P, 4], F32))
        lns = _st.enter_context(nc.sbuf_tensor("lns", [P, 8], F32))
        qs = _st.enter_context(nc.sbuf_tensor("qs", [P, 3 * NQB], F32))
        hstate = _st.enter_context(nc.sbuf_tensor("hstate", [P, ND2 * NST], F32))
        tails = _st.enter_context(nc.sbuf_tensor("tails", [P, ND2 * 4], F32))
        wslab = _st.enter_context(nc.sbuf_tensor("wslab", [P, 2 * 8 * P], BF16))
        woslab = _st.enter_context(nc.sbuf_tensor("woslab", [P, 2 * D], BF16))
        xch = _st.enter_context(nc.sbuf_tensor("xch", [P, CPW * 4 * D], F32))
        scr = _st.enter_context(nc.sbuf_tensor("scr", [P, 5 * D], F32))
        tbuf = _st.enter_context(nc.sbuf_tensor("tbuf", [P, 8 * CH], BF16))
        uext = _st.enter_context(nc.sbuf_tensor("uext", [P, CH + 3], F32))
        cvt = _st.enter_context(nc.sbuf_tensor("cvt", [P, CH], F32))
        usbf = _st.enter_context(nc.sbuf_tensor("usbf", [P, CH], BF16))
        ussl = _st.enter_context(nc.sbuf_tensor("ussl", [P, CH], F32))
        dtr_bf = _st.enter_context(nc.sbuf_tensor("dtr_bf", [RK, CH], BF16))
        spl = _st.enter_context(nc.sbuf_tensor("spl", [P, 2 * CH], F32))
        dtsl = _st.enter_context(nc.sbuf_tensor("dtsl", [P, 3 * WIN], F32))
        dtusl = _st.enter_context(nc.sbuf_tensor("dtusl", [P, 3 * WIN], F32))
        dab = _st.enter_context(nc.sbuf_tensor("dab", [P, 2 * WIN], F32))
        xbb = _st.enter_context(nc.sbuf_tensor("xbb", [P, 2 * WIN], F32))
        hbb = _st.enter_context(nc.sbuf_tensor("hbb", [P, 2 * WIN], F32))
        hcb = _st.enter_context(nc.sbuf_tensor("hcb", [P, WIN], F32))
        bbt = _st.enter_context(nc.sbuf_tensor("bbt", [P, WIN], F32))
        cbt = _st.enter_context(nc.sbuf_tensor("cbt", [P, WIN], F32))
        brow = _st.enter_context(nc.sbuf_tensor("brow", [1, WIN], F32))
        crow = _st.enter_context(nc.sbuf_tensor("crow", [1, WIN], F32))
        y3b = _st.enter_context(nc.sbuf_tensor("y3b", [P, WIN], BF16))
        y3ti = _st.enter_context(nc.sbuf_tensor("y3ti", [P, ND2 * P], BF16))
        posts = _st.enter_context(nc.sbuf_tensor("posts", [P, 2 * WIN], F32))
        ps = _st.enter_context(nc.psum_tensor("ps", [P, 4096], F32))
        s_sp = _st.enter_context(nc.semaphore("s_sp"))
        s_pq = _st.enter_context(nc.semaphore("s_pq"))
        s_pe = _st.enter_context(nc.semaphore("s_pe"))
        s_act = _st.enter_context(nc.semaphore("s_act"))
        s_dve = _st.enter_context(nc.semaphore("s_dve"))
        s_pool = _st.enter_context(nc.semaphore("s_pool"))
        block = _st.enter_context(nc.Block())
        sems = {"sp": s_sp, "pq": s_pq, "pe": s_pe, "act": s_act,
                "dve": s_dve, "pool": s_pool}

        def bank(i, w=512):
            return ps[:, i * 512:i * 512 + w]

        def pb(*idx):
            return [f"pb{i}" for i in idx]

        def dma(dst_ap, src_ap, reads=(), writes=(), q=SP):
            em.op(q, lambda e, o=dst_ap, i=src_ap: e.dma_start(out=o, in_=i),
                  reads=reads, writes=writes)

        def s_(i, w=D):
            return scr[:, i * D:i * D + w]

        em.op(POOL, lambda e: e.memset(ident[:], 0.0), writes=["ident"])
        em.op(POOL, lambda e: e.affine_select(
            out=ident[:], in_=ident[:], compare_op=ALU.not_equal, fill=1.0,
            base=0, pattern=[[-1, P]], channel_multiplier=1),
            reads=["ident"], writes=["ident"])
        em.op(DVE, lambda e: e.memset(ones1[:], 1.0), writes=["ones1"])
        em.op(DVE, lambda e: e.memset(hstate[:], 0.0), writes=["hstate"])
        em.op(DVE, lambda e: e.memset(tails[:], 0.0), writes=["tails"])
        em.op(DVE, lambda e: e.memset(kcol[:], 0.0), writes=["kcol"])
        for j, val in ((0, 1e-5), (1, 1e-6), (2, 1.0)):
            em.op(DVE, lambda e, j=j, val=val: e.tensor_scalar(
                kcol[:, j:j + 1], kcol[:, j:j + 1], val, None, ALU.add),
                reads=["kcol"], writes=["kcol"])
        eps1, eps2, onec = kcol[:, 0:1], kcol[:, 1:2], kcol[:, 2:3]

        dma(convw[:], conv_w_l[:, :], writes=["convw"])
        dma(convb[:], conv_b_l[:, :], writes=["convb"])
        dma(dtb[:], dtb_l[:, :], writes=["dtb"])
        dma(dpar[:], d_l[:, :], writes=["dpar"])
        dma(a_t[:], alog_l[:, :], writes=["a_t"])
        em.op(ACT, lambda e: e.activation(a_t[:], a_t[:], AF.Exp),
              reads=["a_t"], writes=["a_t"])
        em.op(DVE, lambda e: e.tensor_scalar(a_t[:], a_t[:], -1.0, None, ALU.mult),
              reads=["a_t"], writes=["a_t"])

        def row_bcast(dst, src_row, tag):
            dma(dst[0:1, :], src_row, writes=[tag])
            p = 1
            while p < P:
                q = min(p, P - p)
                dma(dst[p:p + q, :], dst[0:q, :], reads=[tag], writes=[tag])
                p += q
        row_bcast(nwb, nrmw[:, :], "nwb")
        row_bcast(bfb, b_ffn[:, :], "bfb")

        for k in range(16):
            dma(s_(0, P), w_xp_g[k, :, :], reads=["scr0"], writes=["scr0"])
            em.op(DVE, lambda e, k=k: e.tensor_copy(
                xpT[:, k * P:(k + 1) * P], s_(0, P)),
                reads=["scr0"], writes=["xpT"])
        for h in range(2):
            dma(spl[0:RK, 0:D2 // 2], w_dtp[:, h * D2 // 2:(h + 1) * D2 // 2],
                reads=["spl"], writes=["spl"])
            em.op(DVE, lambda e, h=h: e.tensor_copy(
                dtpT[:, h * D2 // 2:(h + 1) * D2 // 2], spl[0:RK, 0:D2 // 2]),
                reads=["spl"], writes=["dtpT"])

        for m in range(32):
            dma(s_(0).rearrange("p (k c) -> p k c", c=P),
                w_in_g[m].rearrange("k p c -> p k c"),
                reads=["scr0"], writes=["scr0"])
            em.op(DVE, lambda e: e.tensor_copy(tbuf[:, 0:D], s_(0)),
                  reads=["scr0"], writes=["tbuf"])
            dma(win_bf[m, :, :], tbuf[:, 0:D], reads=["tbuf"], writes=["win_bf"])
        for k in range(16):
            dma(s_(0), w_out_g[k, :, :], reads=["scr0"], writes=["scr0"])
            em.op(DVE, lambda e: e.tensor_copy(tbuf[:, 0:D], s_(0)),
                  reads=["scr0"], writes=["tbuf"])
            dma(wout_bf[k, :, :], tbuf[:, 0:D], reads=["tbuf"], writes=["wout_bf"])

        def quantize(src_i, dst_i):
            sv, sa, sm, sw0, swt = (x for x in (src_i, 0, 1, 4, 2))
            v = s_(sv)
            aa = s_(sa)
            m1 = s_(sm)
            sgn = s_(dst_i)
            w0t = s_(sw0)
            wt = s_(swt)
            amax = qs[:, 0:NQB]
            sc = qs[:, NQB:2 * NQB]
            sinv = qs[:, 2 * NQB:3 * NQB]
            rd_src = [f"scr{src_i}"]
            em.op(DVE, lambda e: e.tensor_reduce(
                amax, v.rearrange("p (b k) -> p b k", k=NST), AX.X, ALU.max,
                apply_absolute_value=True), reads=rd_src, writes=["qs"])
            em.op(DVE, lambda e: e.tensor_scalar(
                sc, amax, 1.0 / 7.5, 1e-30, ALU.mult, ALU.max),
                reads=["qs"], writes=["qs"])
            em.op(DVE, lambda e: e.reciprocal(sinv, sc), reads=["qs"], writes=["qs"])
            em.op(DVE, lambda e: e.tensor_tensor(
                v.rearrange("p (b k) -> p b k", k=NST),
                v.rearrange("p (b k) -> p b k", k=NST),
                sinv.unsqueeze(-1).broadcast_to([P, NQB, NST]), ALU.mult),
                reads=rd_src + ["qs"], writes=rd_src)
            em.op(ACT, lambda e: e.activation(sgn, v, AF.Sign),
                  reads=rd_src, writes=[f"scr{dst_i}"])
            em.op(ACT, lambda e: e.activation(aa, v, AF.Abs),
                  reads=rd_src, writes=["scr0"])
            em.op(DVE, lambda e: e.tensor_scalar(
                m1, aa, 1.9375, -4.0, ALU.is_gt, ALU.mult),
                reads=["scr0"], writes=["scr1"])
            em.op(DVE, lambda e: e.tensor_scalar(
                v, aa, 3.875, -2.0, ALU.is_gt, ALU.mult),
                reads=["scr0"], writes=rd_src)
            em.op(DVE, lambda e: e.scalar_tensor_tensor(
                w0t, m1, -0.25, v, ALU.mult, ALU.subtract),
                reads=["scr1"] + rd_src, writes=["scr4"])
            em.op(DVE, lambda e: e.tensor_scalar(
                w0t, w0t, 0.125, 0.125, ALU.mult, ALU.add),
                reads=["scr4"], writes=["scr4"])
            em.op(DVE, lambda e: e.tensor_tensor(
                w0t.rearrange("p (b k) -> p b k", k=NST),
                w0t.rearrange("p (b k) -> p b k", k=NST),
                sc.unsqueeze(-1).broadcast_to([P, NQB, NST]), ALU.mult),
                reads=["scr4", "qs"], writes=["scr4"])
            em.op(DVE, lambda e: e.tensor_tensor(m1, m1, v, ALU.add),
                  reads=["scr1"] + rd_src, writes=["scr1"])
            em.op(DVE, lambda e: e.tensor_scalar(
                m1, m1, 0.125, 1.0, ALU.mult, ALU.add),
                reads=["scr1"], writes=["scr1"])
            em.op(DVE, lambda e: e.tensor_tensor(wt, aa, m1, ALU.mult),
                  reads=["scr0", "scr1"], writes=["scr2"])
            em.op(DVE, lambda e: e.tensor_scalar(
                wt, wt, 8.0, 0.5, ALU.mult, ALU.add),
                reads=["scr2"], writes=["scr2"])
            em.op(DVE, lambda e: e.tensor_copy(aa, wt),
                  reads=["scr2"], writes=["scr0"])
            for K in (32.0, 16.0, 8.0, 4.0, 2.0, 1.0):
                em.op(DVE, lambda e, K=K: e.tensor_scalar(
                    m1, aa, K, K, ALU.is_ge, ALU.mult),
                    reads=["scr0"], writes=["scr1"])
                em.op(DVE, lambda e: e.tensor_tensor(aa, aa, m1, ALU.subtract),
                      reads=["scr0", "scr1"], writes=["scr0"])
            em.op(DVE, lambda e: e.tensor_tensor(wt, wt, aa, ALU.subtract),
                  reads=["scr2", "scr0"], writes=["scr2"])
            em.op(DVE, lambda e: e.tensor_tensor(wt, wt, w0t, ALU.mult),
                  reads=["scr2", "scr4"], writes=["scr2"])
            em.op(DVE, lambda e: e.tensor_tensor(s_(dst_i), wt, sgn, ALU.mult),
                  reads=["scr2", f"scr{dst_i}"], writes=[f"scr{dst_i}"])

        for r in range(8):
            dma(s_(2), w_ffn[r * P:(r + 1) * P, :], reads=["scr2"], writes=["scr2"])
            quantize(2, 3)
            for kb in range(8):
                em.op(PE, lambda e, kb=kb: e.transpose(
                    bank(5, P), s_(3)[:, kb * P:(kb + 1) * P], ident[:]),
                    reads=["scr3", "ident"], writes=["pb5"])
                em.op(DVE, lambda e, kb=kb, r=r: e.tensor_copy(
                    wdT[:, kb * D + r * P:kb * D + (r + 1) * P], bank(5, P)),
                    reads=["pb5"], writes=["wdT"])

        if dumps:
            for kb in range(8):
                em.op(DVE, lambda e, kb=kb: e.tensor_copy(
                    s_(0), wdT[:, kb * D:(kb + 1) * D]),
                    reads=["wdT"], writes=["scr0"])
                dma(wdt_sp[:, kb * D:(kb + 1) * D], s_(0), reads=["scr0"])

        def layernorm(src_tag, src_ap, dst_ap, dst_tag, eps_col):
            sm, ssq, mcol, rstd = (lns[:, j:j + 1] for j in range(4))
            em.op(DVE, lambda e: e.reduce_sum(sm, src_ap, AX.X),
                  reads=[src_tag], writes=["lns"])
            em.op(ACT, lambda e: e.activation(
                s_(4), src_ap, AF.Square),
                reads=[src_tag], writes=["scr4"])
            em.op(DVE, lambda e: e.reduce_sum(ssq, s_(4), AX.X),
                  reads=["scr4"], writes=["lns"])
            em.op(DVE, lambda e: e.tensor_scalar(
                mcol, sm, 1.0 / D, None, ALU.mult), reads=["lns"], writes=["lns"])
            em.op(DVE, lambda e: e.scalar_tensor_tensor(
                sm, mcol, 1.0, mcol, ALU.mult, ALU.mult),
                reads=["lns"], writes=["lns"])
            em.op(DVE, lambda e: e.tensor_scalar(
                ssq, ssq, 1.0 / D, None, ALU.mult), reads=["lns"], writes=["lns"])
            em.op(DVE, lambda e: e.tensor_tensor(ssq, ssq, sm, ALU.subtract),
                  reads=["lns"], writes=["lns"])
            em.op(ACT, lambda e: e.activation(rstd, ssq, AF.Ln, bias=eps_col),
                  reads=["lns", "kcol"], writes=["lns"])
            em.op(ACT, lambda e: e.activation(rstd, rstd, AF.Exp, scale=-0.5),
                  reads=["lns"], writes=["lns"])
            em.op(DVE, lambda e: e.tensor_scalar(
                dst_ap, src_ap, mcol, rstd, ALU.subtract, ALU.mult),
                reads=[src_tag, "lns"], writes=[dst_tag])

        def transpose_in(src_i, dst, dst_tag, i, width=CH):
            for k in range(NKD):
                slot = k % 4
                em.op(PE, lambda e, k=k, slot=slot: e.transpose(
                    bank(5)[:, slot * P:(slot + 1) * P],
                    s_(src_i)[:, k * P:(k + 1) * P], ident[:]),
                    reads=[f"scr{src_i}", "ident"], writes=["pb5"])
                em.op(DVE, lambda e, k=k, slot=slot: e.tensor_copy(
                    dst[:, k * width + i * P:k * width + (i + 1) * P],
                    bank(5)[:, slot * P:(slot + 1) * P]),
                    reads=["pb5"], writes=[dst_tag])

        def ffn_block(ci, store_out=None, wi=0):
            for i in range(4):
                xw_ap = xch[:, (ci * 4 + i) * D:(ci * 4 + i + 1) * D]
                xwt = f"xch{ci}_{i}"
                layernorm(xwt, xw_ap, s_(2), "scr2", eps1)
                if dumps and store_out is None:
                    T0d = (wi * CPW + ci) * CH
                    dma(xn1_sp[T0d + i * P:T0d + (i + 1) * P, :], s_(2),
                        reads=["scr2"])
                quantize(2, 3)
                if dumps and store_out is None:
                    dma(xd_sp[T0d + i * P:T0d + (i + 1) * P, :], s_(3),
                        reads=["scr3"])
                transpose_in(3, tbuf, "tbuf", i)
            for i in range(4):
                xw_ap = xch[:, (ci * 4 + i) * D:(ci * 4 + i + 1) * D]
                xwt = f"xch{ci}_{i}"
                for k in range(NKD):
                    for h2 in range(2):
                        em.op(PE, lambda e, k=k, i=i, h2=h2: e.matmul(
                            bank(3 + h2),
                            tbuf[:, k * CH + i * P:k * CH + (i + 1) * P],
                            wdT[:, k * D + h2 * 512:k * D + (h2 + 1) * 512],
                            start=(k == 0), stop=(k == NKD - 1),
                            skip_group_check=True),
                            reads=["tbuf", "wdT"], writes=[f"pb{3 + h2}"])
                em.op(DVE, lambda e, xw_ap=xw_ap: e.scalar_tensor_tensor(
                    xw_ap, ps[:, 3 * 512:5 * 512], 0.5, xw_ap, ALU.mult, ALU.add),
                    reads=pb(3, 4) + [xwt], writes=[xwt])
                em.op(DVE, lambda e, xw_ap=xw_ap: e.scalar_tensor_tensor(
                    xw_ap, bfb[:], 0.5, xw_ap, ALU.mult, ALU.add),
                    reads=["bfb", xwt], writes=[xwt])
                if store_out is not None:
                    T0 = (wi * CPW + ci) * CH
                    dma(out_d[T0 + i * P:T0 + (i + 1) * P, :], xw_ap, reads=[xwt])

        for wi in range(NWIN):
            for ci in range(CPW):
                c = wi * CPW + ci
                T0 = c * CH

                for i in range(4):
                    xw_ap = xch[:, (ci * 4 + i) * D:(ci * 4 + i + 1) * D]
                    dma(xw_ap, x_in[T0 + i * P:T0 + (i + 1) * P, :],
                        reads=[f"xch{ci}_{i}"], writes=[f"xch{ci}_{i}"])

                ffn_block(ci)
                if dumps and c == 0:
                    for kk in range(8):
                        em.op(DVE, lambda e, kk=kk: e.tensor_copy(
                            s_(0, CH), tbuf[:, kk * CH:(kk + 1) * CH]),
                            reads=["tbuf"], writes=["scr0"])
                        dma(tb_sp[:, kk * CH:(kk + 1) * CH], s_(0, CH),
                            reads=["scr0"])
                if dumps:
                    for i in range(4):
                        dma(x1_sp[T0 + i * P:T0 + (i + 1) * P, :],
                            xch[:, (ci * 4 + i) * D:(ci * 4 + i + 1) * D],
                            reads=[f"xch{ci}_{i}"])

                for i in range(4):
                    xw_ap = xch[:, (ci * 4 + i) * D:(ci * 4 + i + 1) * D]
                    xwt = f"xch{ci}_{i}"
                    layernorm(xwt, xw_ap, s_(2), "scr2", eps1)
                    ssq2 = lns[:, 4:5]
                    rstd2 = lns[:, 5:6]
                    em.op(ACT, lambda e: e.activation(
                        s_(4), s_(2), AF.Square),
                        reads=["scr2"], writes=["scr4"])
                    em.op(DVE, lambda e, ssq2=ssq2: e.reduce_sum(
                        ssq2, s_(4), AX.X),
                        reads=["scr4"], writes=["lns"])
                    em.op(DVE, lambda e, ssq2=ssq2: e.tensor_scalar(
                        ssq2, ssq2, 1.0 / D, None, ALU.mult),
                        reads=["lns"], writes=["lns"])
                    em.op(ACT, lambda e, ssq2=ssq2, rstd2=rstd2: e.activation(
                        rstd2, ssq2, AF.Ln, bias=eps2),
                        reads=["lns", "kcol"], writes=["lns"])
                    em.op(ACT, lambda e, rstd2=rstd2: e.activation(
                        rstd2, rstd2, AF.Exp, scale=-0.5),
                        reads=["lns"], writes=["lns"])
                    em.op(DVE, lambda e, rstd2=rstd2: e.tensor_scalar(
                        s_(2), s_(2), rstd2, None, ALU.mult),
                        reads=["scr2", "lns"], writes=["scr2"])
                    em.op(DVE, lambda e: e.tensor_tensor(
                        s_(2), s_(2), nwb[:], ALU.mult),
                        reads=["scr2", "nwb"], writes=["scr2"])
                    transpose_in(2, tbuf, "tbuf", i)

                for m in range(32):
                    sl = m % 2
                    wsl = wslab[:, sl * 8 * P:(sl + 1) * 8 * P]
                    dma(wsl, win_bf[m, :, :], reads=[f"wsl{sl}", "win_bf"],
                        writes=[f"wsl{sl}"])
                    for k in range(NKD):
                        em.op(PE, lambda e, k=k, wsl=wsl, sl=sl: e.matmul(
                            bank(sl), wsl[:, k * P:(k + 1) * P],
                            tbuf[:, k * CH:(k + 1) * CH],
                            start=(k == 0), stop=(k == NKD - 1),
                            skip_group_check=True),
                            reads=[f"wsl{sl}", "tbuf"], writes=[f"pb{sl}"])
                    if m >= ND2:
                        em.op(ACT, lambda e, sl=sl: e.activation(
                            ussl[:], bank(sl), AF.Silu),
                            reads=[f"pb{sl}"], writes=["ussl"])
                        dma(sz_sp[m - ND2, :, T0:T0 + CH], ussl[:],
                            reads=["ussl"], writes=[f"szsp{m - ND2}"])
                        continue
                    em.op(DVE, lambda e, m=m: e.tensor_copy(
                        uext[:, 0:3], tails[:, m * 4:m * 4 + 3]),
                        reads=["tails"], writes=["uext"])
                    em.op(DVE, lambda e, sl=sl: e.tensor_copy(
                        uext[:, 3:3 + CH], bank(sl)),
                        reads=[f"pb{sl}"], writes=["uext"])
                    em.op(DVE, lambda e, m=m: e.tensor_copy(
                        tails[:, m * 4:m * 4 + 3], uext[:, CH:CH + 3]),
                        reads=["uext"], writes=["tails"])
                    em.op(DVE, lambda e, m=m: e.tensor_scalar(
                        cvt[:], uext[:, 0:CH], convw[:, m * 4:m * 4 + 1], None,
                        ALU.mult), reads=["uext", "convw"], writes=["cvt"])
                    for j in range(1, 4):
                        em.op(DVE, lambda e, m=m, j=j: e.scalar_tensor_tensor(
                            cvt[:], uext[:, j:j + CH],
                            convw[:, m * 4 + j:m * 4 + j + 1],
                            cvt[:], ALU.mult, ALU.add),
                            reads=["uext", "convw", "cvt"], writes=["cvt"])
                    em.op(ACT, lambda e, m=m: e.activation(
                        ussl[:], cvt[:], AF.Silu, bias=convb[:, m:m + 1]),
                        reads=["cvt", "convb"], writes=["ussl"])
                    dma(us_sp[m, :, T0:T0 + CH], ussl[:], reads=["ussl"],
                        writes=[f"ussp{m}"])
                    em.op(DVE, lambda e: e.tensor_copy(usbf[:], ussl[:]),
                          reads=["ussl"], writes=["usbf"])
                    em.op(PE, lambda e, m=m: e.matmul(
                        bank(2), xpT[:, m * P:(m + 1) * P], usbf[:],
                        start=(m == 0), stop=(m == ND2 - 1),
                        skip_group_check=True),
                        reads=["usbf", "xpT"], writes=["pb2"])

                em.op(DVE, lambda e: e.tensor_copy(spl[:, 0:CH], bank(2)),
                      reads=["pb2", "spl"], writes=["spl"])
                dma(b_sp[:, T0:T0 + CH], spl[0:NST, 0:CH], reads=["spl"],
                    writes=[f"bsp{c}"])
                dma(c_sp[:, T0:T0 + CH], spl[NST:2 * NST, 0:CH],
                    reads=["spl"], writes=[f"csp{c}"])
                dma(ussl[0:RK, :], spl[RK:P, 0:CH], reads=["spl", "ussl"],
                    writes=["ussl"])
                em.op(DVE, lambda e: e.tensor_copy(dtr_bf[:], ussl[0:RK, :]),
                      reads=["ussl"], writes=["dtrb"])

                for m in range(ND2):
                    sl = m % 2
                    em.op(PE, lambda e, m=m, sl=sl: e.matmul(
                        bank(sl), dtpT[:, m * P:(m + 1) * P], dtr_bf[:],
                        start=True, stop=True, skip_group_check=True),
                        reads=["dtrb", "dtpT"], writes=[f"pb{sl}"])
                    aa = spl[:, 0:CH]
                    ll = spl[:, CH:2 * CH]
                    em.op(ACT, lambda e, m=m, sl=sl: e.activation(
                        aa, bank(sl), AF.Abs, bias=dtb[:, m:m + 1]),
                        reads=[f"pb{sl}", "dtb"], writes=["spl"])
                    em.op(ACT, lambda e, aa=aa, ll=ll: e.activation(
                        ll, aa, AF.Exp, scale=-1.0), reads=["spl"], writes=["spl"])
                    em.op(ACT, lambda e, ll=ll: e.activation(
                        ll, ll, AF.Ln, bias=onec),
                        reads=["spl", "kcol"], writes=["spl"])
                    em.op(DVE, lambda e, m=m, sl=sl, aa=aa: e.tensor_scalar(
                        aa, bank(sl), dtb[:, m:m + 1], 0.0, ALU.add, ALU.max),
                        reads=[f"pb{sl}", "dtb"], writes=["spl"])
                    em.op(DVE, lambda e, aa=aa, ll=ll: e.tensor_tensor(
                        aa, aa, ll, ALU.add), reads=["spl"], writes=["spl"])
                    dma(dt_sp[m, :, T0:T0 + CH], aa, reads=["spl"], writes=[f"dtsp{m}"])
                    dma(ussl[:], us_sp[m, :, T0:T0 + CH],
                        reads=["ussl", f"ussp{m}"], writes=["ussl"])
                    em.op(DVE, lambda e, aa=aa: e.tensor_tensor(
                        cvt[:], aa, ussl[:], ALU.mult),
                        reads=["spl", "ussl"], writes=["cvt"])
                    dma(dtu_sp[m, :, T0:T0 + CH], cvt[:], reads=["cvt"],
                        writes=[f"dtusp{m}"])

            import os as _os
            _ks = _os.environ.get("KSERIAL", "1")
            if _ks == "1":
                em.serial = True
            elif _ks == "0":
                em.serial = False
            elif _ks.startswith("except:"):
                em.serial = _ks
            else:
                em.serial = _ks.split(",")
            W0 = wi * WIN
            blocks = [(0, 3), (3, 3), (6, 3), (9, 3), (12, 3), (15, 1)]
            for (m0, bsz) in blocks:
                for j in range(bsz):
                    m = m0 + j
                    dma(dtsl[:, j * WIN:(j + 1) * WIN], dt_sp[m, :, W0:W0 + WIN],
                        reads=[f"dtsl{j}", f"dtsp{m}"], writes=[f"dtsl{j}"])
                    dma(dtusl[:, j * WIN:(j + 1) * WIN],
                        dtu_sp[m, :, W0:W0 + WIN],
                        reads=[f"dtusl{j}", f"dtusp{m}"],
                        writes=[f"dtusl{j}"])
                for n in range(NST):
                    br = brow[:, :]
                    cr = crow[:, :]
                    em.op(SP, lambda e, br=br, n=n: e.dma_start(
                        out=br, in_=b_sp[n:n + 1, W0:W0 + WIN]),
                        reads=["brow", f"bsp{2 * wi}", f"bsp{2 * wi + 1}"],
                        writes=["brow"], ser_tag="srow")
                    em.op(SP, lambda e, cr=cr, n=n: e.dma_start(
                        out=cr, in_=c_sp[n:n + 1, W0:W0 + WIN]),
                        reads=["crow", f"csp{2 * wi}", f"csp{2 * wi + 1}"],
                        writes=["crow"], ser_tag="srow")
                    off = 0
                    for h2 in range(2):
                        em.op(PE, lambda e, br=br, off=off, h2=h2: e.matmul(
                            bank(6), ones1[:],
                            br[:, off + h2 * 512:off + (h2 + 1) * 512],
                            start=True, stop=True, skip_group_check=True),
                            reads=["brow", "ones1"],
                            writes=["pb6"], ser_tag="sbmm")
                        em.op(ACT, lambda e, h2=h2: e.activation(
                            bbt[:, h2 * 512:(h2 + 1) * 512], bank(6), AF.Copy),
                            reads=["pb6"], writes=["bbt"], ser_tag="sevac")
                        em.op(PE, lambda e, cr=cr, off=off, h2=h2: e.matmul(
                            bank(7), ones1[:],
                            cr[:, off + h2 * 512:off + (h2 + 1) * 512],
                            start=True, stop=True, skip_group_check=True),
                            reads=["crow", "ones1"],
                            writes=["pb7"], ser_tag="sbmm")
                        em.op(ACT, lambda e, h2=h2: e.activation(
                            cbt[:, h2 * 512:(h2 + 1) * 512], bank(7), AF.Copy),
                            reads=["pb7"], writes=["cbt"], ser_tag="sevac")
                    for j in range(bsz):
                        m = m0 + j
                        ii = (n * bsz + j) % 2
                        da = dab[:, ii * WIN:(ii + 1) * WIN]
                        xx = xbb[:, ii * WIN:(ii + 1) * WIN]
                        hh = hbb[:, ii * WIN:(ii + 1) * WIN]
                        hc = hcb[:, 0:WIN]
                        scol = m * NST + n
                        em.op(ACT, lambda e, j=j, m=m, n=n, da=da: e.activation(
                            da, dtsl[:, j * WIN:(j + 1) * WIN], AF.Exp,
                            scale=a_t[:, m * NST + n:m * NST + n + 1]),
                            reads=[f"dtsl{j}", "a_t"], writes=[f"da{ii}"],
                            ser_tag="sda")
                        em.op(POOL, lambda e, j=j, xx=xx: e.tensor_tensor(
                            xx, dtusl[:, j * WIN:(j + 1) * WIN], bbt[:], ALU.mult),
                            reads=[f"dtusl{j}", "bbt"], writes=[f"xb{ii}"],
                            ser_tag="sxb")
                        em.op(DVE, lambda e, da=da, xx=xx, hh=hh, scol=scol:
                              e.tensor_tensor_scan(
                                  hh, da, xx, hstate[:, scol:scol + 1],
                                  ALU.mult, ALU.add),
                              reads=[f"da{ii}", f"xb{ii}", "hstate"],
                              writes=[f"hb{ii}"], ser_tag="sscan")
                        em.op(DVE, lambda e, hh=hh, scol=scol: e.tensor_copy(
                            hstate[:, scol:scol + 1], hh[:, WIN - 1:WIN]),
                            reads=[f"hb{ii}"], writes=["hstate"],
                            ser_tag="shend")
                        hc_eng = DVE if j == 0 else POOL
                        em.op(hc_eng, lambda e, hh=hh, hc=hc: e.tensor_tensor(
                            hc, hh, cbt[:], ALU.mult),
                            reads=[f"hb{ii}", "cbt"], writes=["hc0"],
                            ser_tag="shc")
                        for h2 in range(2):
                            em.op(PE, lambda e, j=j, hc=hc, h2=h2, n=n: e.matmul(
                                ps[:, (2 * j + h2) * 512:(2 * j + h2 + 1) * 512],
                                ident[:], hc[:, h2 * 512:(h2 + 1) * 512],
                                start=(n == 0), stop=(n == NST - 1),
                                skip_group_check=True),
                                reads=["hc0", "ident"],
                                writes=[f"pb{2 * j + h2}"], ser_tag="symm")
                for j in range(bsz):
                    m = m0 + j
                    uD = posts[:, 0:WIN]
                    sz = posts[:, WIN:2 * WIN]
                    dma(uD, us_sp[m, :, W0:W0 + WIN],
                        reads=["posts", f"ussp{m}"], writes=["posts"])
                    dma(sz, sz_sp[m, :, W0:W0 + WIN],
                        reads=["posts2", f"szsp{m}"], writes=["posts2"])
                    em.op(DVE, lambda e, uD=uD, m=m: e.tensor_scalar(
                        uD, uD, dpar[:, m:m + 1], None, ALU.mult),
                        reads=["posts", "dpar"], writes=["posts"])
                    em.op(DVE, lambda e, j=j, uD=uD: e.tensor_tensor(
                        uD, ps[:, 2 * j * 512:(2 * j + 2) * 512], uD, ALU.add),
                        reads=pb(2 * j, 2 * j + 1) + ["posts"], writes=["posts"])
                    em.op(DVE, lambda e, uD=uD, sz=sz: e.tensor_tensor(
                        y3b[:], uD, sz, ALU.mult),
                        reads=["posts", "posts2"], writes=["y3b"])
                    dma(y3_sp[m, :, W0:W0 + WIN], y3b[:], reads=["y3b"],
                        writes=[f"y3sp{m}"])

            em.serial = False
            for ti in range(8):
                ci, i = divmod(ti, 4)
                tsl = 0
                yti = y3ti[:, :]
                dma(yti.rearrange("p (m c) -> p m c", c=P),
                    y3_sp[:, :, W0 + ti * P:W0 + (ti + 1) * P].rearrange(
                        "m p c -> p m c"),
                    reads=[f"y3ti{tsl}"] + [f"y3sp{mm}" for mm in range(ND2)],
                    writes=[f"y3ti{tsl}"])
                for k in range(16):
                    sl = k % 2
                    wsl_ap = woslab[:, sl * D:(sl + 1) * D]
                    dma(wsl_ap, wout_bf[k, :, :], reads=[f"wo{sl}", "wout_bf"],
                        writes=[f"wo{sl}"])
                    for h2 in range(2):
                        em.op(PE, lambda e, k=k, h2=h2, yti=yti, wsl_ap=wsl_ap,
                              tsl=tsl: e.matmul(
                                  bank(3 + h2),
                                  yti[:, k * P:(k + 1) * P],
                                  wsl_ap[:, h2 * 512:(h2 + 1) * 512],
                                  start=(k == 0), stop=(k == 15),
                                  skip_group_check=True),
                              reads=[f"y3ti{tsl}", f"wo{sl}"],
                              writes=[f"pb{3 + h2}"])
                xw_ap = xch[:, ti * D:(ti + 1) * D]
                xwt = f"xch{ci}_{i}"
                layernorm(xwt, xw_ap, s_(2), "scr2", eps1)
                em.op(DVE, lambda e, xw_ap=xw_ap: e.tensor_tensor(
                    xw_ap, s_(2), xw_ap, ALU.add),
                    reads=["scr2", xwt], writes=[xwt])
                em.op(DVE, lambda e, xw_ap=xw_ap: e.tensor_tensor(
                    xw_ap, ps[:, 3 * 512:5 * 512], xw_ap, ALU.add),
                    reads=pb(3, 4) + [xwt], writes=[xwt])

            for ci in range(CPW):
                ffn_block(ci, store_out=True, wi=wi)


        em.finalize()
        streams = {"sp": [], "pe": [], "act": [], "dve": [], "pool": []}
        for eng, fn, waits, seq in em.ops:
            stream = "pool" if eng in ("pool", "pq") else eng
            streams[stream].append((eng, fn, waits, seq))

        def play(stream_name, handle):
            waited = {}
            for eng, fn, waits, seq in streams[stream_name]:
                for we, ws in waits.items():
                    val = em.rank[we][ws]
                    if we in ("sp", "pq"):
                        val *= 16
                    if val > 0 and waited.get(we, -1) < val:
                        handle.wait_ge(sems[we], val)
                        waited[we] = val
                inst = fn(handle)
                if seq in em.needed[eng]:
                    inst.then_inc(sems[eng], 16 if eng in ("sp", "pq") else 1)
            if stream_name == "sp":
                handle.wait_ge(sems["sp"], 16 * em.ninc["sp"])
                if em.ninc["pq"]:
                    handle.wait_ge(sems["pq"], 16 * em.ninc["pq"])

        @block.sync
        def _(sync):
            play("sp", sync)

        @block.tensor
        def _(t):
            play("pe", t)

        @block.scalar
        def _(s):
            play("act", s)

        @block.vector
        def _(v):
            play("dve", v)

        @block.gpsimd
        def _(g):
            play("pool", g)

    return nc


_NC_CACHE = {}


def kernel(L0_attn_x, L0_attn_norm_w, L0_attn_in_proj_w, L0_attn_conv_w,
           L0_attn_conv_b, L0_attn_x_proj_w, L0_attn_dt_proj_w, L0_attn_dt_proj_b,
           L0_attn_A_log, L0_attn_D_param, L0_attn_out_proj_w, L0_ffn_weight,
           L0_ffn_bias):
    B = L0_attn_x.shape[0]
    f32 = np.float32

    def lay(a):
        return np.ascontiguousarray(np.asarray(a), dtype=f32)

    w_in_g = lay(np.asarray(L0_attn_in_proj_w).T
                 .reshape(8, 128, 32, 128).transpose(2, 0, 1, 3))
    w_out_g = lay(np.asarray(L0_attn_out_proj_w).T.reshape(16, 128, 1024))
    perm = list(range(64, 96)) + list(range(96, 128)) + list(range(0, 64))
    w_xp_g = lay(np.asarray(L0_attn_x_proj_w)[perm].T.reshape(16, 128, 128))
    w_dtp = lay(np.asarray(L0_attn_dt_proj_w).T)
    w_ffn = lay(L0_ffn_weight)
    b_ffn = lay(np.asarray(L0_ffn_bias).reshape(1, 1024))
    nrmw = lay(np.asarray(L0_attn_norm_w).reshape(1, 1024))
    conv_w_l = lay(np.asarray(L0_attn_conv_w).reshape(16, 128, 4)
                   .transpose(1, 0, 2).reshape(128, 64))
    conv_b_l = lay(np.asarray(L0_attn_conv_b).reshape(16, 128).T)
    dtb_l = lay(np.asarray(L0_attn_dt_proj_b).reshape(16, 128).T)
    d_l = lay(np.asarray(L0_attn_D_param).reshape(16, 128).T)
    alog_l = lay(np.asarray(L0_attn_A_log).reshape(16, 128, 32)
                 .transpose(1, 0, 2).reshape(128, 512))

    if "nc" not in _NC_CACHE:
        _NC_CACHE["nc"] = build_core_program()
    nc = _NC_CACHE["nc"]

    shared = {
        "w_in_g": w_in_g, "w_out_g": w_out_g, "w_xp_g": w_xp_g, "w_dtp": w_dtp,
        "w_ffn": w_ffn, "b_ffn": b_ffn, "nrmw": nrmw, "conv_w_l": conv_w_l,
        "conv_b_l": conv_b_l, "dtb_l": dtb_l, "d_l": d_l, "alog_l": alog_l,
    }
    in_maps = []
    for b in range(B):
        m = dict(shared)
        m["x"] = lay(L0_attn_x[b])
        in_maps.append(m)

    res = run_bass_kernel_spmd(nc, in_maps, core_ids=list(range(B)))
    out = np.stack([res.results[b]["out"] for b in range(B)], axis=0)
    return out.astype(np.float32)


# version 06
# speedup vs baseline: 1.4985x; 1.4985x over previous
import numpy as np
import concourse.bass as bass
import concourse.mybir as mybir
from concourse.bass_utils import run_bass_kernel_spmd

F32 = mybir.dt.float32
BF16 = mybir.dt.bfloat16
AF = mybir.ActivationFunctionType
ALU = mybir.AluOpType
AX = mybir.AxisListType

P = 128
T = 4096
D = 1024
D2 = 2048
NST = 32
RK = 64
CH = 512
NCH = T // CH
WIN = 1024
NWIN = T // WIN
CPW = WIN // CH
ND2 = D2 // P
NKD = D // P
NQB = D // NST

ENGS = ("sp", "pq", "pe", "act", "dve", "pool")


class Em:
    def __init__(self):
        self.ops = []
        self.n = {e: 0 for e in ENGS}
        self.writer = {}
        self.readers = {}
        self.serial = False
        self.barrier = None

    def op(self, eng, fn, reads=(), writes=(), ser_tag=None):
        waits = {}
        if self.barrier:
            for e2, s2 in self.barrier.items():
                if e2 != eng and waits.get(e2, -1) < s2:
                    waits[e2] = s2
                if e2 == eng and eng != "pe" and waits.get(e2, -1) < s2:
                    waits[e2] = s2
        ser = False
        if self.serial:
            if self.serial is True:
                ser = True
            elif isinstance(self.serial, str) and self.serial.startswith("except:"):
                ser = (ser_tag or "") not in self.serial[7:].split(",")
            elif eng in self.serial or (ser_tag and ser_tag in self.serial):
                ser = True
        if ser:
            for e2 in ENGS:
                ns = self.n[e2] - (1 if e2 == eng else 0)
                if ns > 0:
                    waits[e2] = ns - 1

        def need(dep):
            if dep is None:
                return
            e, s = dep
            if e == eng:
                if e == "pe":
                    return
            if e == "pool" and eng == "pq":
                return
            if waits.get(e, -1) < s:
                waits[e] = s

        for r in reads:
            need(self.writer.get(r))
        for w in writes:
            need(self.writer.get(w))
            for rd in self.readers.get(w, ()):
                need(rd)
        seq = self.n[eng]
        self.ops.append((eng, fn, waits, seq))
        self.n[eng] += 1
        me = (eng, seq)
        for w in writes:
            self.writer[w] = me
            self.readers[w] = []
        for r in reads:
            self.readers.setdefault(r, []).append(me)

    def finalize(self):
        needed = {e: set() for e in ENGS}
        for eng, fn, waits, seq in self.ops:
            for e, s in waits.items():
                needed[e].add(s)
        needed["sp"] = set(range(self.n["sp"]))
        needed["pq"] = set(range(self.n["pq"]))
        self.rank = {}
        self.ninc = {}
        for e in ENGS:
            cnt = 0
            rm = {}
            for s in range(self.n[e]):
                if s in needed[e]:
                    cnt += 1
                rm[s] = cnt
            self.rank[e] = rm
            self.ninc[e] = cnt
        self.needed = needed


def build_core_program(debug=False):
    nc = bass.Bass("TRN2", target_bir_lowering=False)
    spk = "ExternalOutput" if debug else "Internal"
    dumps = (debug is True)

    def din(name, shape, dt=F32):
        return nc.dram_tensor(name, list(shape), dt, kind="ExternalInput").ap()

    x_in = din("x", (T, D))
    w_in_g = din("w_in_g", (32, 8, P, P))
    w_out_g = din("w_out_g", (16, P, D))
    w_xp_g = din("w_xp_g", (16, P, P))
    w_dtp = din("w_dtp", (RK, D2))
    w_ffn = din("w_ffn", (D, D))
    b_ffn = din("b_ffn", (1, D))
    nrmw = din("nrmw", (1, D))
    conv_w_l = din("conv_w_l", (P, ND2 * 4))
    conv_b_l = din("conv_b_l", (P, ND2))
    dtb_l = din("dtb_l", (P, ND2))
    d_l = din("d_l", (P, ND2))
    alog_l = din("alog_l", (P, ND2 * NST))

    out_d = nc.dram_tensor("out", [T, D], F32, kind="ExternalOutput").ap()
    x1_sp = nc.dram_tensor("x1_sp", [T, D], F32, kind=spk).ap() if dumps else None
    xn1_sp = nc.dram_tensor("xn1_sp", [T, D], F32, kind=spk).ap() if dumps else None
    xd_sp = nc.dram_tensor("xd_sp", [T, D], F32, kind=spk).ap() if dumps else None
    wdt_sp = nc.dram_tensor("wdt_sp", [P, 8 * D], F32, kind=spk).ap() if dumps else None
    tb_sp = nc.dram_tensor("tb_sp", [P, 8 * CH], F32, kind=spk).ap() if dumps else None

    win_bf = nc.dram_tensor("win_bf", [32, P, 8 * P], BF16, kind="Internal").ap()
    wout_bf = nc.dram_tensor("wout_bf", [16, P, D], BF16, kind="Internal").ap()
    dt_sp = nc.dram_tensor("dt_sp", [ND2, P, T], F32, kind=spk).ap()
    dtu_sp = nc.dram_tensor("dtu_sp", [ND2, P, T], F32, kind=spk).ap()
    us_sp = nc.dram_tensor("us_sp", [ND2, P, T], F32, kind=spk).ap()
    sz_sp = nc.dram_tensor("sz_sp", [ND2, P, T], F32, kind=spk).ap()
    y3_sp = nc.dram_tensor("y3_sp", [ND2, P, T], BF16, kind=spk).ap()
    b_sp = nc.dram_tensor("b_sp", [NST, T], F32, kind=spk).ap()
    c_sp = nc.dram_tensor("c_sp", [NST, T], F32, kind=spk).ap()

    em = Em()
    SP, PQ, PE, ACT, DVE, POOL = ENGS

    import contextlib
    with contextlib.ExitStack() as _st:
        ident = _st.enter_context(nc.sbuf_tensor("ident", [P, P], F32))
        ones1 = _st.enter_context(nc.sbuf_tensor("ones1", [1, P], F32))
        wdT = _st.enter_context(nc.sbuf_tensor("wdT", [P, 8 * D], BF16))
        xpT = _st.enter_context(nc.sbuf_tensor("xpT", [P, 16 * P], BF16))
        dtpT = _st.enter_context(nc.sbuf_tensor("dtpT", [RK, D2], BF16))
        a_t = _st.enter_context(nc.sbuf_tensor("a_t", [P, ND2 * NST], F32))
        convw = _st.enter_context(nc.sbuf_tensor("convw", [P, ND2 * 4], F32))
        convb = _st.enter_context(nc.sbuf_tensor("convb", [P, ND2], F32))
        dtb = _st.enter_context(nc.sbuf_tensor("dtb", [P, ND2], F32))
        dpar = _st.enter_context(nc.sbuf_tensor("dpar", [P, ND2], F32))
        nwb = _st.enter_context(nc.sbuf_tensor("nwb", [P, D], F32))
        bfb = _st.enter_context(nc.sbuf_tensor("bfb", [P, D], F32))
        kcol = _st.enter_context(nc.sbuf_tensor("kcol", [P, 4], F32))
        lns = _st.enter_context(nc.sbuf_tensor("lns", [P, 8], F32))
        qs = _st.enter_context(nc.sbuf_tensor("qs", [P, 3 * NQB], F32))
        hstate = _st.enter_context(nc.sbuf_tensor("hstate", [P, ND2 * NST], F32))
        tails = _st.enter_context(nc.sbuf_tensor("tails", [P, ND2 * 4], F32))
        wslab = _st.enter_context(nc.sbuf_tensor("wslab", [P, 2 * 8 * P], BF16))
        woslab = _st.enter_context(nc.sbuf_tensor("woslab", [P, 2 * D], BF16))
        xch = _st.enter_context(nc.sbuf_tensor("xch", [P, CPW * 4 * D], F32))
        scr = _st.enter_context(nc.sbuf_tensor("scr", [P, 5 * D], F32))
        tbuf = _st.enter_context(nc.sbuf_tensor("tbuf", [P, 8 * CH], BF16))
        uext = _st.enter_context(nc.sbuf_tensor("uext", [P, CH + 3], F32))
        cvt = _st.enter_context(nc.sbuf_tensor("cvt", [P, CH], F32))
        usbf = _st.enter_context(nc.sbuf_tensor("usbf", [P, CH], BF16))
        ussl = _st.enter_context(nc.sbuf_tensor("ussl", [P, CH], F32))
        dtr_bf = _st.enter_context(nc.sbuf_tensor("dtr_bf", [RK, CH], BF16))
        spl = _st.enter_context(nc.sbuf_tensor("spl", [P, 2 * CH], F32))
        dtsl = _st.enter_context(nc.sbuf_tensor("dtsl", [P, 3 * WIN], F32))
        dtusl = _st.enter_context(nc.sbuf_tensor("dtusl", [P, 3 * WIN], F32))
        dab = _st.enter_context(nc.sbuf_tensor("dab", [P, 2 * WIN], F32))
        xbb = _st.enter_context(nc.sbuf_tensor("xbb", [P, 2 * WIN], F32))
        hbb = _st.enter_context(nc.sbuf_tensor("hbb", [P, 2 * WIN], F32))
        hcb = _st.enter_context(nc.sbuf_tensor("hcb", [P, WIN], F32))
        bbt = _st.enter_context(nc.sbuf_tensor("bbt", [P, WIN], F32))
        cbt = _st.enter_context(nc.sbuf_tensor("cbt", [P, WIN], F32))
        brow = _st.enter_context(nc.sbuf_tensor("brow", [1, WIN], F32))
        crow = _st.enter_context(nc.sbuf_tensor("crow", [1, WIN], F32))
        y3b = _st.enter_context(nc.sbuf_tensor("y3b", [P, WIN], BF16))
        y3ti = _st.enter_context(nc.sbuf_tensor("y3ti", [P, ND2 * P], BF16))
        posts = _st.enter_context(nc.sbuf_tensor("posts", [P, 2 * WIN], F32))
        ps = _st.enter_context(nc.psum_tensor("ps", [P, 4096], F32))
        s_sp = _st.enter_context(nc.semaphore("s_sp"))
        s_pq = _st.enter_context(nc.semaphore("s_pq"))
        s_pe = _st.enter_context(nc.semaphore("s_pe"))
        s_act = _st.enter_context(nc.semaphore("s_act"))
        s_dve = _st.enter_context(nc.semaphore("s_dve"))
        s_pool = _st.enter_context(nc.semaphore("s_pool"))
        block = _st.enter_context(nc.Block())
        sems = {"sp": s_sp, "pq": s_pq, "pe": s_pe, "act": s_act,
                "dve": s_dve, "pool": s_pool}

        def bank(i, w=512):
            return ps[:, i * 512:i * 512 + w]

        def pb(*idx):
            return [f"pb{i}" for i in idx]

        def dma(dst_ap, src_ap, reads=(), writes=(), q=SP):
            em.op(q, lambda e, o=dst_ap, i=src_ap: e.dma_start(out=o, in_=i),
                  reads=reads, writes=writes)

        def s_(i, w=D):
            return scr[:, i * D:i * D + w]

        em.op(POOL, lambda e: e.memset(ident[:], 0.0), writes=["ident"])
        em.op(POOL, lambda e: e.affine_select(
            out=ident[:], in_=ident[:], compare_op=ALU.not_equal, fill=1.0,
            base=0, pattern=[[-1, P]], channel_multiplier=1),
            reads=["ident"], writes=["ident"])
        em.op(DVE, lambda e: e.memset(ones1[:], 1.0), writes=["ones1"])
        em.op(DVE, lambda e: e.memset(hstate[:], 0.0), writes=["hstate"])
        em.op(DVE, lambda e: e.memset(tails[:], 0.0), writes=["tails"])
        em.op(DVE, lambda e: e.memset(kcol[:], 0.0), writes=["kcol"])
        for j, val in ((0, 1e-5), (1, 1e-6), (2, 1.0)):
            em.op(DVE, lambda e, j=j, val=val: e.tensor_scalar(
                kcol[:, j:j + 1], kcol[:, j:j + 1], val, None, ALU.add),
                reads=["kcol"], writes=["kcol"])
        eps1, eps2, onec = kcol[:, 0:1], kcol[:, 1:2], kcol[:, 2:3]

        dma(convw[:], conv_w_l[:, :], writes=["convw"])
        dma(convb[:], conv_b_l[:, :], writes=["convb"])
        dma(dtb[:], dtb_l[:, :], writes=["dtb"])
        dma(dpar[:], d_l[:, :], writes=["dpar"])
        dma(a_t[:], alog_l[:, :], writes=["a_t"])
        em.op(ACT, lambda e: e.activation(a_t[:], a_t[:], AF.Exp),
              reads=["a_t"], writes=["a_t"])
        em.op(DVE, lambda e: e.tensor_scalar(a_t[:], a_t[:], -1.0, None, ALU.mult),
              reads=["a_t"], writes=["a_t"])

        def row_bcast(dst, src_row, tag):
            dma(dst[0:1, :], src_row, writes=[tag])
            p = 1
            while p < P:
                q = min(p, P - p)
                dma(dst[p:p + q, :], dst[0:q, :], reads=[tag], writes=[tag])
                p += q
        row_bcast(nwb, nrmw[:, :], "nwb")
        row_bcast(bfb, b_ffn[:, :], "bfb")

        for k in range(16):
            dma(s_(0, P), w_xp_g[k, :, :], reads=["scr0"], writes=["scr0"])
            em.op(DVE, lambda e, k=k: e.tensor_copy(
                xpT[:, k * P:(k + 1) * P], s_(0, P)),
                reads=["scr0"], writes=["xpT"])
        for h in range(2):
            dma(spl[0:RK, 0:D2 // 2], w_dtp[:, h * D2 // 2:(h + 1) * D2 // 2],
                reads=["spl"], writes=["spl"])
            em.op(DVE, lambda e, h=h: e.tensor_copy(
                dtpT[:, h * D2 // 2:(h + 1) * D2 // 2], spl[0:RK, 0:D2 // 2]),
                reads=["spl"], writes=["dtpT"])

        for m in range(32):
            dma(s_(0).rearrange("p (k c) -> p k c", c=P),
                w_in_g[m].rearrange("k p c -> p k c"),
                reads=["scr0"], writes=["scr0"])
            em.op(DVE, lambda e: e.tensor_copy(tbuf[:, 0:D], s_(0)),
                  reads=["scr0"], writes=["tbuf"])
            dma(win_bf[m, :, :], tbuf[:, 0:D], reads=["tbuf"], writes=["win_bf"])
        for k in range(16):
            dma(s_(0), w_out_g[k, :, :], reads=["scr0"], writes=["scr0"])
            em.op(DVE, lambda e: e.tensor_copy(tbuf[:, 0:D], s_(0)),
                  reads=["scr0"], writes=["tbuf"])
            dma(wout_bf[k, :, :], tbuf[:, 0:D], reads=["tbuf"], writes=["wout_bf"])

        def quantize(src_i, dst_i):
            sv, sa, sm, sw0, swt = (x for x in (src_i, 0, 1, 4, 2))
            v = s_(sv)
            aa = s_(sa)
            m1 = s_(sm)
            sgn = s_(dst_i)
            w0t = s_(sw0)
            wt = s_(swt)
            amax = qs[:, 0:NQB]
            sc = qs[:, NQB:2 * NQB]
            sinv = qs[:, 2 * NQB:3 * NQB]
            rd_src = [f"scr{src_i}"]
            em.op(DVE, lambda e: e.tensor_reduce(
                amax, v.rearrange("p (b k) -> p b k", k=NST), AX.X, ALU.max,
                apply_absolute_value=True), reads=rd_src, writes=["qs"])
            em.op(DVE, lambda e: e.tensor_scalar(
                sc, amax, 1.0 / 7.5, 1e-30, ALU.mult, ALU.max),
                reads=["qs"], writes=["qs"])
            em.op(DVE, lambda e: e.reciprocal(sinv, sc), reads=["qs"], writes=["qs"])
            em.op(DVE, lambda e: e.tensor_tensor(
                v.rearrange("p (b k) -> p b k", k=NST),
                v.rearrange("p (b k) -> p b k", k=NST),
                sinv.unsqueeze(-1).broadcast_to([P, NQB, NST]), ALU.mult),
                reads=rd_src + ["qs"], writes=rd_src)
            em.op(ACT, lambda e: e.activation(sgn, v, AF.Sign),
                  reads=rd_src, writes=[f"scr{dst_i}"])
            em.op(ACT, lambda e: e.activation(aa, v, AF.Abs),
                  reads=rd_src, writes=["scr0"])
            em.op(DVE, lambda e: e.tensor_scalar(
                m1, aa, 1.9375, -4.0, ALU.is_gt, ALU.mult),
                reads=["scr0"], writes=["scr1"])
            em.op(DVE, lambda e: e.tensor_scalar(
                v, aa, 3.875, -2.0, ALU.is_gt, ALU.mult),
                reads=["scr0"], writes=rd_src)
            em.op(DVE, lambda e: e.scalar_tensor_tensor(
                w0t, m1, -0.25, v, ALU.mult, ALU.subtract),
                reads=["scr1"] + rd_src, writes=["scr4"])
            em.op(DVE, lambda e: e.tensor_scalar(
                w0t, w0t, 0.125, 0.125, ALU.mult, ALU.add),
                reads=["scr4"], writes=["scr4"])
            em.op(DVE, lambda e: e.tensor_tensor(
                w0t.rearrange("p (b k) -> p b k", k=NST),
                w0t.rearrange("p (b k) -> p b k", k=NST),
                sc.unsqueeze(-1).broadcast_to([P, NQB, NST]), ALU.mult),
                reads=["scr4", "qs"], writes=["scr4"])
            em.op(DVE, lambda e: e.tensor_tensor(m1, m1, v, ALU.add),
                  reads=["scr1"] + rd_src, writes=["scr1"])
            em.op(DVE, lambda e: e.tensor_scalar(
                m1, m1, 0.125, 1.0, ALU.mult, ALU.add),
                reads=["scr1"], writes=["scr1"])
            em.op(DVE, lambda e: e.tensor_tensor(wt, aa, m1, ALU.mult),
                  reads=["scr0", "scr1"], writes=["scr2"])
            em.op(DVE, lambda e: e.tensor_scalar(
                wt, wt, 8.0, 0.5, ALU.mult, ALU.add),
                reads=["scr2"], writes=["scr2"])
            em.op(DVE, lambda e: e.tensor_copy(aa, wt),
                  reads=["scr2"], writes=["scr0"])
            for K in (32.0, 16.0, 8.0, 4.0, 2.0, 1.0):
                em.op(DVE, lambda e, K=K: e.tensor_scalar(
                    m1, aa, K, K, ALU.is_ge, ALU.mult),
                    reads=["scr0"], writes=["scr1"])
                em.op(DVE, lambda e: e.tensor_tensor(aa, aa, m1, ALU.subtract),
                      reads=["scr0", "scr1"], writes=["scr0"])
            em.op(DVE, lambda e: e.tensor_tensor(wt, wt, aa, ALU.subtract),
                  reads=["scr2", "scr0"], writes=["scr2"])
            em.op(DVE, lambda e: e.tensor_tensor(wt, wt, w0t, ALU.mult),
                  reads=["scr2", "scr4"], writes=["scr2"])
            em.op(DVE, lambda e: e.tensor_tensor(s_(dst_i), wt, sgn, ALU.mult),
                  reads=["scr2", f"scr{dst_i}"], writes=[f"scr{dst_i}"])

        for r in range(8):
            dma(s_(2), w_ffn[r * P:(r + 1) * P, :], reads=["scr2"], writes=["scr2"])
            quantize(2, 3)
            for kb in range(8):
                em.op(PE, lambda e, kb=kb: e.transpose(
                    bank(5, P), s_(3)[:, kb * P:(kb + 1) * P], ident[:]),
                    reads=["scr3", "ident"], writes=["pb5"])
                em.op(DVE, lambda e, kb=kb, r=r: e.tensor_copy(
                    wdT[:, kb * D + r * P:kb * D + (r + 1) * P], bank(5, P)),
                    reads=["pb5"], writes=["wdT"])

        if dumps:
            for kb in range(8):
                em.op(DVE, lambda e, kb=kb: e.tensor_copy(
                    s_(0), wdT[:, kb * D:(kb + 1) * D]),
                    reads=["wdT"], writes=["scr0"])
                dma(wdt_sp[:, kb * D:(kb + 1) * D], s_(0), reads=["scr0"])

        def layernorm(src_tag, src_ap, dst_ap, dst_tag, eps_col):
            sm, ssq, mcol, rstd = (lns[:, j:j + 1] for j in range(4))
            em.op(DVE, lambda e: e.reduce_sum(sm, src_ap, AX.X),
                  reads=[src_tag], writes=["lns"])
            em.op(ACT, lambda e: e.activation(
                s_(4), src_ap, AF.Square),
                reads=[src_tag], writes=["scr4"])
            em.op(DVE, lambda e: e.reduce_sum(ssq, s_(4), AX.X),
                  reads=["scr4"], writes=["lns"])
            em.op(DVE, lambda e: e.tensor_scalar(
                mcol, sm, 1.0 / D, None, ALU.mult), reads=["lns"], writes=["lns"])
            em.op(DVE, lambda e: e.scalar_tensor_tensor(
                sm, mcol, 1.0, mcol, ALU.mult, ALU.mult),
                reads=["lns"], writes=["lns"])
            em.op(DVE, lambda e: e.tensor_scalar(
                ssq, ssq, 1.0 / D, None, ALU.mult), reads=["lns"], writes=["lns"])
            em.op(DVE, lambda e: e.tensor_tensor(ssq, ssq, sm, ALU.subtract),
                  reads=["lns"], writes=["lns"])
            em.op(ACT, lambda e: e.activation(rstd, ssq, AF.Ln, bias=eps_col),
                  reads=["lns", "kcol"], writes=["lns"])
            em.op(ACT, lambda e: e.activation(rstd, rstd, AF.Exp, scale=-0.5),
                  reads=["lns"], writes=["lns"])
            em.op(DVE, lambda e: e.tensor_scalar(
                dst_ap, src_ap, mcol, rstd, ALU.subtract, ALU.mult),
                reads=[src_tag, "lns"], writes=[dst_tag])

        def transpose_in(src_i, dst, dst_tag, i, width=CH):
            for k in range(NKD):
                slot = k % 4
                em.op(PE, lambda e, k=k, slot=slot: e.transpose(
                    bank(5)[:, slot * P:(slot + 1) * P],
                    s_(src_i)[:, k * P:(k + 1) * P], ident[:]),
                    reads=[f"scr{src_i}", "ident"], writes=["pb5"])
                em.op(DVE, lambda e, k=k, slot=slot: e.tensor_copy(
                    dst[:, k * width + i * P:k * width + (i + 1) * P],
                    bank(5)[:, slot * P:(slot + 1) * P]),
                    reads=["pb5"], writes=[dst_tag])

        def ffn_block(ci, store_out=None, wi=0):
            for i in range(4):
                xw_ap = xch[:, (ci * 4 + i) * D:(ci * 4 + i + 1) * D]
                xwt = f"xch{ci}_{i}"
                layernorm(xwt, xw_ap, s_(2), "scr2", eps1)
                if dumps and store_out is None:
                    T0d = (wi * CPW + ci) * CH
                    dma(xn1_sp[T0d + i * P:T0d + (i + 1) * P, :], s_(2),
                        reads=["scr2"])
                quantize(2, 3)
                if dumps and store_out is None:
                    dma(xd_sp[T0d + i * P:T0d + (i + 1) * P, :], s_(3),
                        reads=["scr3"])
                transpose_in(3, tbuf, "tbuf", i)
            for i in range(4):
                xw_ap = xch[:, (ci * 4 + i) * D:(ci * 4 + i + 1) * D]
                xwt = f"xch{ci}_{i}"
                for k in range(NKD):
                    for h2 in range(2):
                        em.op(PE, lambda e, k=k, i=i, h2=h2: e.matmul(
                            bank(3 + h2),
                            tbuf[:, k * CH + i * P:k * CH + (i + 1) * P],
                            wdT[:, k * D + h2 * 512:k * D + (h2 + 1) * 512],
                            start=(k == 0), stop=(k == NKD - 1),
                            skip_group_check=True),
                            reads=["tbuf", "wdT"], writes=[f"pb{3 + h2}"])
                em.op(DVE, lambda e, xw_ap=xw_ap: e.scalar_tensor_tensor(
                    xw_ap, ps[:, 3 * 512:5 * 512], 0.5, xw_ap, ALU.mult, ALU.add),
                    reads=pb(3, 4) + [xwt], writes=[xwt])
                em.op(DVE, lambda e, xw_ap=xw_ap: e.scalar_tensor_tensor(
                    xw_ap, bfb[:], 0.5, xw_ap, ALU.mult, ALU.add),
                    reads=["bfb", xwt], writes=[xwt])
                if store_out is not None:
                    T0 = (wi * CPW + ci) * CH
                    dma(out_d[T0 + i * P:T0 + (i + 1) * P, :], xw_ap, reads=[xwt])

        for wi in range(NWIN):
            for ci in range(CPW):
                c = wi * CPW + ci
                T0 = c * CH

                for i in range(4):
                    xw_ap = xch[:, (ci * 4 + i) * D:(ci * 4 + i + 1) * D]
                    dma(xw_ap, x_in[T0 + i * P:T0 + (i + 1) * P, :],
                        reads=[f"xch{ci}_{i}"], writes=[f"xch{ci}_{i}"])

                ffn_block(ci)
                if dumps and c == 0:
                    for kk in range(8):
                        em.op(DVE, lambda e, kk=kk: e.tensor_copy(
                            s_(0, CH), tbuf[:, kk * CH:(kk + 1) * CH]),
                            reads=["tbuf"], writes=["scr0"])
                        dma(tb_sp[:, kk * CH:(kk + 1) * CH], s_(0, CH),
                            reads=["scr0"])
                if dumps:
                    for i in range(4):
                        dma(x1_sp[T0 + i * P:T0 + (i + 1) * P, :],
                            xch[:, (ci * 4 + i) * D:(ci * 4 + i + 1) * D],
                            reads=[f"xch{ci}_{i}"])

                for i in range(4):
                    xw_ap = xch[:, (ci * 4 + i) * D:(ci * 4 + i + 1) * D]
                    xwt = f"xch{ci}_{i}"
                    layernorm(xwt, xw_ap, s_(2), "scr2", eps1)
                    ssq2 = lns[:, 4:5]
                    rstd2 = lns[:, 5:6]
                    em.op(ACT, lambda e: e.activation(
                        s_(4), s_(2), AF.Square),
                        reads=["scr2"], writes=["scr4"])
                    em.op(DVE, lambda e, ssq2=ssq2: e.reduce_sum(
                        ssq2, s_(4), AX.X),
                        reads=["scr4"], writes=["lns"])
                    em.op(DVE, lambda e, ssq2=ssq2: e.tensor_scalar(
                        ssq2, ssq2, 1.0 / D, None, ALU.mult),
                        reads=["lns"], writes=["lns"])
                    em.op(ACT, lambda e, ssq2=ssq2, rstd2=rstd2: e.activation(
                        rstd2, ssq2, AF.Ln, bias=eps2),
                        reads=["lns", "kcol"], writes=["lns"])
                    em.op(ACT, lambda e, rstd2=rstd2: e.activation(
                        rstd2, rstd2, AF.Exp, scale=-0.5),
                        reads=["lns"], writes=["lns"])
                    em.op(DVE, lambda e, rstd2=rstd2: e.tensor_scalar(
                        s_(2), s_(2), rstd2, None, ALU.mult),
                        reads=["scr2", "lns"], writes=["scr2"])
                    em.op(DVE, lambda e: e.tensor_tensor(
                        s_(2), s_(2), nwb[:], ALU.mult),
                        reads=["scr2", "nwb"], writes=["scr2"])
                    transpose_in(2, tbuf, "tbuf", i)

                for m in range(32):
                    sl = m % 2
                    wsl = wslab[:, sl * 8 * P:(sl + 1) * 8 * P]
                    dma(wsl, win_bf[m, :, :], reads=[f"wsl{sl}", "win_bf"],
                        writes=[f"wsl{sl}"])
                    for k in range(NKD):
                        em.op(PE, lambda e, k=k, wsl=wsl, sl=sl: e.matmul(
                            bank(sl), wsl[:, k * P:(k + 1) * P],
                            tbuf[:, k * CH:(k + 1) * CH],
                            start=(k == 0), stop=(k == NKD - 1),
                            skip_group_check=True),
                            reads=[f"wsl{sl}", "tbuf"], writes=[f"pb{sl}"])
                    if m >= ND2:
                        em.op(ACT, lambda e, sl=sl: e.activation(
                            ussl[:], bank(sl), AF.Silu),
                            reads=[f"pb{sl}"], writes=["ussl"])
                        dma(sz_sp[m - ND2, :, T0:T0 + CH], ussl[:],
                            reads=["ussl"], writes=[f"szsp{m - ND2}"])
                        continue
                    em.op(DVE, lambda e, m=m: e.tensor_copy(
                        uext[:, 0:3], tails[:, m * 4:m * 4 + 3]),
                        reads=["tails"], writes=["uext"])
                    em.op(DVE, lambda e, sl=sl: e.tensor_copy(
                        uext[:, 3:3 + CH], bank(sl)),
                        reads=[f"pb{sl}"], writes=["uext"])
                    em.op(DVE, lambda e, m=m: e.tensor_copy(
                        tails[:, m * 4:m * 4 + 3], uext[:, CH:CH + 3]),
                        reads=["uext"], writes=["tails"])
                    em.op(DVE, lambda e, m=m: e.tensor_scalar(
                        cvt[:], uext[:, 0:CH], convw[:, m * 4:m * 4 + 1], None,
                        ALU.mult), reads=["uext", "convw"], writes=["cvt"])
                    for j in range(1, 4):
                        em.op(DVE, lambda e, m=m, j=j: e.scalar_tensor_tensor(
                            cvt[:], uext[:, j:j + CH],
                            convw[:, m * 4 + j:m * 4 + j + 1],
                            cvt[:], ALU.mult, ALU.add),
                            reads=["uext", "convw", "cvt"], writes=["cvt"])
                    em.op(ACT, lambda e, m=m: e.activation(
                        ussl[:], cvt[:], AF.Silu, bias=convb[:, m:m + 1]),
                        reads=["cvt", "convb"], writes=["ussl"])
                    dma(us_sp[m, :, T0:T0 + CH], ussl[:], reads=["ussl"],
                        writes=[f"ussp{m}"])
                    em.op(DVE, lambda e: e.tensor_copy(usbf[:], ussl[:]),
                          reads=["ussl"], writes=["usbf"])
                    em.op(PE, lambda e, m=m: e.matmul(
                        bank(2), xpT[:, m * P:(m + 1) * P], usbf[:],
                        start=(m == 0), stop=(m == ND2 - 1),
                        skip_group_check=True),
                        reads=["usbf", "xpT"], writes=["pb2"])

                em.op(DVE, lambda e: e.tensor_copy(spl[:, 0:CH], bank(2)),
                      reads=["pb2", "spl"], writes=["spl"])
                dma(b_sp[:, T0:T0 + CH], spl[0:NST, 0:CH], reads=["spl"],
                    writes=[f"bsp{c}"])
                dma(c_sp[:, T0:T0 + CH], spl[NST:2 * NST, 0:CH],
                    reads=["spl"], writes=[f"csp{c}"])
                dma(ussl[0:RK, :], spl[RK:P, 0:CH], reads=["spl", "ussl"],
                    writes=["ussl"])
                em.op(DVE, lambda e: e.tensor_copy(dtr_bf[:], ussl[0:RK, :]),
                      reads=["ussl"], writes=["dtrb"])

                for m in range(ND2):
                    sl = m % 2
                    em.op(PE, lambda e, m=m, sl=sl: e.matmul(
                        bank(sl), dtpT[:, m * P:(m + 1) * P], dtr_bf[:],
                        start=True, stop=True, skip_group_check=True),
                        reads=["dtrb", "dtpT"], writes=[f"pb{sl}"])
                    aa = spl[:, 0:CH]
                    ll = spl[:, CH:2 * CH]
                    em.op(ACT, lambda e, m=m, sl=sl: e.activation(
                        aa, bank(sl), AF.Abs, bias=dtb[:, m:m + 1]),
                        reads=[f"pb{sl}", "dtb"], writes=["spl"])
                    em.op(ACT, lambda e, aa=aa, ll=ll: e.activation(
                        ll, aa, AF.Exp, scale=-1.0), reads=["spl"], writes=["spl"])
                    em.op(ACT, lambda e, ll=ll: e.activation(
                        ll, ll, AF.Ln, bias=onec),
                        reads=["spl", "kcol"], writes=["spl"])
                    em.op(DVE, lambda e, m=m, sl=sl, aa=aa: e.tensor_scalar(
                        aa, bank(sl), dtb[:, m:m + 1], 0.0, ALU.add, ALU.max),
                        reads=[f"pb{sl}", "dtb"], writes=["spl"])
                    em.op(DVE, lambda e, aa=aa, ll=ll: e.tensor_tensor(
                        aa, aa, ll, ALU.add), reads=["spl"], writes=["spl"])
                    dma(dt_sp[m, :, T0:T0 + CH], aa, reads=["spl"], writes=[f"dtsp{m}"])
                    dma(ussl[:], us_sp[m, :, T0:T0 + CH],
                        reads=["ussl", f"ussp{m}"], writes=["ussl"])
                    em.op(DVE, lambda e, aa=aa: e.tensor_tensor(
                        cvt[:], aa, ussl[:], ALU.mult),
                        reads=["spl", "ussl"], writes=["cvt"])
                    dma(dtu_sp[m, :, T0:T0 + CH], cvt[:], reads=["cvt"],
                        writes=[f"dtusp{m}"])

            import os as _os
            _ks = _os.environ.get("KSERIAL", "0")
            if _ks == "1":
                em.serial = True
            elif _ks == "0":
                em.serial = False
            elif _ks.startswith("except:"):
                em.serial = _ks
            else:
                em.serial = _ks.split(",")
            W0 = wi * WIN
            blocks = [(0, 3), (3, 3), (6, 3), (9, 3), (12, 3), (15, 1)]
            for (m0, bsz) in blocks:
                for j in range(bsz):
                    m = m0 + j
                    dma(dtsl[:, j * WIN:(j + 1) * WIN], dt_sp[m, :, W0:W0 + WIN],
                        reads=[f"dtsl{j}", f"dtsp{m}"], writes=[f"dtsl{j}"])
                    dma(dtusl[:, j * WIN:(j + 1) * WIN],
                        dtu_sp[m, :, W0:W0 + WIN],
                        reads=[f"dtusl{j}", f"dtusp{m}"],
                        writes=[f"dtusl{j}"])
                fences = []
                for n in range(NST):
                    fences.append({e: em.n[e] - 1 for e in ENGS if em.n[e]})
                    if len(fences) >= 3:
                        em.barrier = fences[-3]
                    br = brow[:, :]
                    cr = crow[:, :]
                    em.op(SP, lambda e, br=br, n=n: e.dma_start(
                        out=br, in_=b_sp[n:n + 1, W0:W0 + WIN]),
                        reads=["brow", f"bsp{2 * wi}", f"bsp{2 * wi + 1}"],
                        writes=["brow"], ser_tag="srow")
                    em.op(SP, lambda e, cr=cr, n=n: e.dma_start(
                        out=cr, in_=c_sp[n:n + 1, W0:W0 + WIN]),
                        reads=["crow", f"csp{2 * wi}", f"csp{2 * wi + 1}"],
                        writes=["crow"], ser_tag="srow")
                    off = 0
                    for h2 in range(2):
                        em.op(PE, lambda e, br=br, off=off, h2=h2: e.matmul(
                            bank(6), ones1[:],
                            br[:, off + h2 * 512:off + (h2 + 1) * 512],
                            start=True, stop=True, skip_group_check=True),
                            reads=["brow", "ones1"],
                            writes=["pb6"], ser_tag="sbmm")
                        em.op(ACT, lambda e, h2=h2: e.activation(
                            bbt[:, h2 * 512:(h2 + 1) * 512], bank(6), AF.Copy),
                            reads=["pb6"], writes=["bbt"], ser_tag="sevac")
                        em.op(PE, lambda e, cr=cr, off=off, h2=h2: e.matmul(
                            bank(7), ones1[:],
                            cr[:, off + h2 * 512:off + (h2 + 1) * 512],
                            start=True, stop=True, skip_group_check=True),
                            reads=["crow", "ones1"],
                            writes=["pb7"], ser_tag="sbmm")
                        em.op(ACT, lambda e, h2=h2: e.activation(
                            cbt[:, h2 * 512:(h2 + 1) * 512], bank(7), AF.Copy),
                            reads=["pb7"], writes=["cbt"], ser_tag="sevac")
                    for j in range(bsz):
                        m = m0 + j
                        ii = (n * bsz + j) % 2
                        da = dab[:, ii * WIN:(ii + 1) * WIN]
                        xx = xbb[:, ii * WIN:(ii + 1) * WIN]
                        hh = hbb[:, ii * WIN:(ii + 1) * WIN]
                        hc = hcb[:, 0:WIN]
                        scol = m * NST + n
                        em.op(ACT, lambda e, j=j, m=m, n=n, da=da: e.activation(
                            da, dtsl[:, j * WIN:(j + 1) * WIN], AF.Exp,
                            scale=a_t[:, m * NST + n:m * NST + n + 1]),
                            reads=[f"dtsl{j}", "a_t"], writes=[f"da{ii}"],
                            ser_tag="sda")
                        em.op(POOL, lambda e, j=j, xx=xx: e.tensor_tensor(
                            xx, dtusl[:, j * WIN:(j + 1) * WIN], bbt[:], ALU.mult),
                            reads=[f"dtusl{j}", "bbt"], writes=[f"xb{ii}"],
                            ser_tag="sxb")
                        em.op(DVE, lambda e, da=da, xx=xx, hh=hh, scol=scol:
                              e.tensor_tensor_scan(
                                  hh, da, xx, hstate[:, scol:scol + 1],
                                  ALU.mult, ALU.add),
                              reads=[f"da{ii}", f"xb{ii}", "hstate"],
                              writes=[f"hb{ii}"], ser_tag="sscan")
                        em.op(DVE, lambda e, hh=hh, scol=scol: e.tensor_copy(
                            hstate[:, scol:scol + 1], hh[:, WIN - 1:WIN]),
                            reads=[f"hb{ii}"], writes=["hstate"],
                            ser_tag="shend")
                        hc_eng = DVE if j == 0 else POOL
                        em.op(hc_eng, lambda e, hh=hh, hc=hc: e.tensor_tensor(
                            hc, hh, cbt[:], ALU.mult),
                            reads=[f"hb{ii}", "cbt"], writes=["hc0"],
                            ser_tag="shc")
                        for h2 in range(2):
                            em.op(PE, lambda e, j=j, hc=hc, h2=h2, n=n: e.matmul(
                                ps[:, (2 * j + h2) * 512:(2 * j + h2 + 1) * 512],
                                ident[:], hc[:, h2 * 512:(h2 + 1) * 512],
                                start=(n == 0), stop=(n == NST - 1),
                                skip_group_check=True),
                                reads=["hc0", "ident"],
                                writes=[f"pb{2 * j + h2}"], ser_tag="symm")
                em.barrier = None
                for j in range(bsz):
                    m = m0 + j
                    uD = posts[:, 0:WIN]
                    sz = posts[:, WIN:2 * WIN]
                    dma(uD, us_sp[m, :, W0:W0 + WIN],
                        reads=["posts", f"ussp{m}"], writes=["posts"])
                    dma(sz, sz_sp[m, :, W0:W0 + WIN],
                        reads=["posts2", f"szsp{m}"], writes=["posts2"])
                    em.op(DVE, lambda e, uD=uD, m=m: e.tensor_scalar(
                        uD, uD, dpar[:, m:m + 1], None, ALU.mult),
                        reads=["posts", "dpar"], writes=["posts"])
                    em.op(DVE, lambda e, j=j, uD=uD: e.tensor_tensor(
                        uD, ps[:, 2 * j * 512:(2 * j + 2) * 512], uD, ALU.add),
                        reads=pb(2 * j, 2 * j + 1) + ["posts"], writes=["posts"])
                    em.op(DVE, lambda e, uD=uD, sz=sz: e.tensor_tensor(
                        y3b[:], uD, sz, ALU.mult),
                        reads=["posts", "posts2"], writes=["y3b"])
                    dma(y3_sp[m, :, W0:W0 + WIN], y3b[:], reads=["y3b"],
                        writes=[f"y3sp{m}"])

            em.serial = False
            for ti in range(8):
                ci, i = divmod(ti, 4)
                tsl = 0
                yti = y3ti[:, :]
                dma(yti.rearrange("p (m c) -> p m c", c=P),
                    y3_sp[:, :, W0 + ti * P:W0 + (ti + 1) * P].rearrange(
                        "m p c -> p m c"),
                    reads=[f"y3ti{tsl}"] + [f"y3sp{mm}" for mm in range(ND2)],
                    writes=[f"y3ti{tsl}"])
                for k in range(16):
                    sl = k % 2
                    wsl_ap = woslab[:, sl * D:(sl + 1) * D]
                    dma(wsl_ap, wout_bf[k, :, :], reads=[f"wo{sl}", "wout_bf"],
                        writes=[f"wo{sl}"])
                    for h2 in range(2):
                        em.op(PE, lambda e, k=k, h2=h2, yti=yti, wsl_ap=wsl_ap,
                              tsl=tsl: e.matmul(
                                  bank(3 + h2),
                                  yti[:, k * P:(k + 1) * P],
                                  wsl_ap[:, h2 * 512:(h2 + 1) * 512],
                                  start=(k == 0), stop=(k == 15),
                                  skip_group_check=True),
                              reads=[f"y3ti{tsl}", f"wo{sl}"],
                              writes=[f"pb{3 + h2}"])
                xw_ap = xch[:, ti * D:(ti + 1) * D]
                xwt = f"xch{ci}_{i}"
                layernorm(xwt, xw_ap, s_(2), "scr2", eps1)
                em.op(DVE, lambda e, xw_ap=xw_ap: e.tensor_tensor(
                    xw_ap, s_(2), xw_ap, ALU.add),
                    reads=["scr2", xwt], writes=[xwt])
                em.op(DVE, lambda e, xw_ap=xw_ap: e.tensor_tensor(
                    xw_ap, ps[:, 3 * 512:5 * 512], xw_ap, ALU.add),
                    reads=pb(3, 4) + [xwt], writes=[xwt])

            for ci in range(CPW):
                ffn_block(ci, store_out=True, wi=wi)


        em.finalize()
        streams = {"sp": [], "pe": [], "act": [], "dve": [], "pool": []}
        for eng, fn, waits, seq in em.ops:
            stream = "pool" if eng in ("pool", "pq") else eng
            streams[stream].append((eng, fn, waits, seq))

        def play(stream_name, handle):
            waited = {}
            for eng, fn, waits, seq in streams[stream_name]:
                for we, ws in waits.items():
                    val = em.rank[we][ws]
                    if we in ("sp", "pq"):
                        val *= 16
                    if val > 0 and waited.get(we, -1) < val:
                        handle.wait_ge(sems[we], val)
                        waited[we] = val
                inst = fn(handle)
                if seq in em.needed[eng]:
                    inst.then_inc(sems[eng], 16 if eng in ("sp", "pq") else 1)
            if stream_name == "sp":
                handle.wait_ge(sems["sp"], 16 * em.ninc["sp"])
                if em.ninc["pq"]:
                    handle.wait_ge(sems["pq"], 16 * em.ninc["pq"])

        @block.sync
        def _(sync):
            play("sp", sync)

        @block.tensor
        def _(t):
            play("pe", t)

        @block.scalar
        def _(s):
            play("act", s)

        @block.vector
        def _(v):
            play("dve", v)

        @block.gpsimd
        def _(g):
            play("pool", g)

    return nc


_NC_CACHE = {}


def kernel(L0_attn_x, L0_attn_norm_w, L0_attn_in_proj_w, L0_attn_conv_w,
           L0_attn_conv_b, L0_attn_x_proj_w, L0_attn_dt_proj_w, L0_attn_dt_proj_b,
           L0_attn_A_log, L0_attn_D_param, L0_attn_out_proj_w, L0_ffn_weight,
           L0_ffn_bias):
    B = L0_attn_x.shape[0]
    f32 = np.float32

    def lay(a):
        return np.ascontiguousarray(np.asarray(a), dtype=f32)

    w_in_g = lay(np.asarray(L0_attn_in_proj_w).T
                 .reshape(8, 128, 32, 128).transpose(2, 0, 1, 3))
    w_out_g = lay(np.asarray(L0_attn_out_proj_w).T.reshape(16, 128, 1024))
    perm = list(range(64, 96)) + list(range(96, 128)) + list(range(0, 64))
    w_xp_g = lay(np.asarray(L0_attn_x_proj_w)[perm].T.reshape(16, 128, 128))
    w_dtp = lay(np.asarray(L0_attn_dt_proj_w).T)
    w_ffn = lay(L0_ffn_weight)
    b_ffn = lay(np.asarray(L0_ffn_bias).reshape(1, 1024))
    nrmw = lay(np.asarray(L0_attn_norm_w).reshape(1, 1024))
    conv_w_l = lay(np.asarray(L0_attn_conv_w).reshape(16, 128, 4)
                   .transpose(1, 0, 2).reshape(128, 64))
    conv_b_l = lay(np.asarray(L0_attn_conv_b).reshape(16, 128).T)
    dtb_l = lay(np.asarray(L0_attn_dt_proj_b).reshape(16, 128).T)
    d_l = lay(np.asarray(L0_attn_D_param).reshape(16, 128).T)
    alog_l = lay(np.asarray(L0_attn_A_log).reshape(16, 128, 32)
                 .transpose(1, 0, 2).reshape(128, 512))

    if "nc" not in _NC_CACHE:
        _NC_CACHE["nc"] = build_core_program()
    nc = _NC_CACHE["nc"]

    shared = {
        "w_in_g": w_in_g, "w_out_g": w_out_g, "w_xp_g": w_xp_g, "w_dtp": w_dtp,
        "w_ffn": w_ffn, "b_ffn": b_ffn, "nrmw": nrmw, "conv_w_l": conv_w_l,
        "conv_b_l": conv_b_l, "dtb_l": dtb_l, "d_l": d_l, "alog_l": alog_l,
    }
    in_maps = []
    for b in range(B):
        m = dict(shared)
        m["x"] = lay(L0_attn_x[b])
        in_maps.append(m)

    res = run_bass_kernel_spmd(nc, in_maps, core_ids=list(range(B)))
    out = np.stack([res.results[b]["out"] for b in range(B)], axis=0)
    return out.astype(np.float32)


# version 12
# speedup vs baseline: 1.9535x; 1.3037x over previous
import numpy as np
import concourse.bass as bass
import concourse.mybir as mybir
from concourse.bass_utils import run_bass_kernel_spmd

F32 = mybir.dt.float32
BF16 = mybir.dt.bfloat16
AF = mybir.ActivationFunctionType
ALU = mybir.AluOpType
AX = mybir.AxisListType

P = 128
T = 4096
D = 1024
D2 = 2048
NST = 32
RK = 64
CH = 512
NCH = T // CH
WIN = 1024
NWIN = T // WIN
CPW = WIN // CH
ND2 = D2 // P
NKD = D // P
NQB = D // NST

ENGS = ("sp", "pq", "pe", "act", "dve", "pool")


class Em:
    def __init__(self):
        self.ops = []
        self.n = {e: 0 for e in ENGS}
        self.writer = {}
        self.readers = {}
        self.serial = False
        self.barrier = None

    def op(self, eng, fn, reads=(), writes=(), ser_tag=None):
        waits = {}
        if self.barrier:
            for e2, s2 in self.barrier.items():
                if e2 != eng and waits.get(e2, -1) < s2:
                    waits[e2] = s2
                if e2 == eng and eng != "pe" and waits.get(e2, -1) < s2:
                    waits[e2] = s2
        ser = False
        if self.serial:
            if self.serial is True:
                ser = True
            elif isinstance(self.serial, str) and self.serial.startswith("except:"):
                ser = (ser_tag or "") not in self.serial[7:].split(",")
            elif eng in self.serial or (ser_tag and ser_tag in self.serial):
                ser = True
        if ser:
            for e2 in ENGS:
                ns = self.n[e2] - (1 if e2 == eng else 0)
                if ns > 0:
                    waits[e2] = ns - 1

        def need(dep):
            if dep is None:
                return
            e, s = dep
            if e == eng:
                if e == "pe":
                    return
            if e == "pool" and eng == "pq":
                return
            if waits.get(e, -1) < s:
                waits[e] = s

        for r in reads:
            need(self.writer.get(r))
        for w in writes:
            need(self.writer.get(w))
            for rd in self.readers.get(w, ()):
                need(rd)
        seq = self.n[eng]
        if eng in ("sp", "pq") and seq > 0 and waits.get(eng, -1) < seq - 1:
            waits[eng] = seq - 1
        self.ops.append((eng, fn, waits, seq))
        self.n[eng] += 1
        me = (eng, seq)
        for w in writes:
            self.writer[w] = me
            self.readers[w] = []
        for r in reads:
            self.readers.setdefault(r, []).append(me)

    def finalize(self):
        needed = {e: set() for e in ENGS}
        for eng, fn, waits, seq in self.ops:
            for e, s in waits.items():
                needed[e].add(s)
        needed["sp"] = set(range(self.n["sp"]))
        needed["pq"] = set(range(self.n["pq"]))
        self.rank = {}
        self.ninc = {}
        for e in ENGS:
            cnt = 0
            rm = {}
            for s in range(self.n[e]):
                if s in needed[e]:
                    cnt += 1
                rm[s] = cnt
            self.rank[e] = rm
            self.ninc[e] = cnt
        self.needed = needed


def build_core_program(debug=False):
    nc = bass.Bass("TRN2", target_bir_lowering=False)
    spk = "ExternalOutput" if debug else "Internal"
    dumps = (debug is True)

    def din(name, shape, dt=F32):
        return nc.dram_tensor(name, list(shape), dt, kind="ExternalInput").ap()

    x_in = din("x", (T, D))
    w_in_g = din("w_in_g", (32, 8, P, P))
    w_out_g = din("w_out_g", (16, P, D))
    w_xp_g = din("w_xp_g", (16, P, P))
    w_dtp = din("w_dtp", (RK, D2))
    w_ffn = din("w_ffn", (D, D))
    b_ffn = din("b_ffn", (1, D))
    nrmw = din("nrmw", (1, D))
    conv_w_l = din("conv_w_l", (P, ND2 * 4))
    conv_b_l = din("conv_b_l", (P, ND2))
    dtb_l = din("dtb_l", (P, ND2))
    d_l = din("d_l", (P, ND2))
    alog_l = din("alog_l", (P, ND2 * NST))

    out_d = nc.dram_tensor("out", [T, D], F32, kind="ExternalOutput").ap()
    x1_sp = nc.dram_tensor("x1_sp", [T, D], F32, kind=spk).ap() if dumps else None
    xn1_sp = nc.dram_tensor("xn1_sp", [T, D], F32, kind=spk).ap() if dumps else None
    xd_sp = nc.dram_tensor("xd_sp", [T, D], F32, kind=spk).ap() if dumps else None
    wdt_sp = nc.dram_tensor("wdt_sp", [P, 8 * D], F32, kind=spk).ap() if dumps else None
    tb_sp = nc.dram_tensor("tb_sp", [P, 8 * CH], F32, kind=spk).ap() if dumps else None

    win_bf = nc.dram_tensor("win_bf", [32, P, 8 * P], BF16, kind="Internal").ap()
    wout_bf = nc.dram_tensor("wout_bf", [16, P, D], BF16, kind="Internal").ap()
    dt_sp = nc.dram_tensor("dt_sp", [ND2, P, T], F32, kind=spk).ap()
    dtu_sp = nc.dram_tensor("dtu_sp", [ND2, P, T], F32, kind=spk).ap()
    us_sp = nc.dram_tensor("us_sp", [ND2, P, T], F32, kind=spk).ap()
    sz_sp = nc.dram_tensor("sz_sp", [ND2, P, T], F32, kind=spk).ap()
    y3_sp = nc.dram_tensor("y3_sp", [ND2, P, T], BF16, kind=spk).ap()
    b_sp = nc.dram_tensor("b_sp", [NST, T], F32, kind=spk).ap()
    c_sp = nc.dram_tensor("c_sp", [NST, T], F32, kind=spk).ap()

    em = Em()
    SP, PQ, PE, ACT, DVE, POOL = ENGS

    import contextlib
    with contextlib.ExitStack() as _st:
        ident = _st.enter_context(nc.sbuf_tensor("ident", [P, P], F32))
        identb = _st.enter_context(nc.sbuf_tensor("identb", [P, P], BF16))
        ones1 = _st.enter_context(nc.sbuf_tensor("ones1", [1, P], F32))
        wdT = _st.enter_context(nc.sbuf_tensor("wdT", [P, 8 * D], BF16))
        xpT = _st.enter_context(nc.sbuf_tensor("xpT", [P, 16 * P], BF16))
        dtpT = _st.enter_context(nc.sbuf_tensor("dtpT", [RK, D2], BF16))
        a_t = _st.enter_context(nc.sbuf_tensor("a_t", [P, ND2 * NST], F32))
        convw = _st.enter_context(nc.sbuf_tensor("convw", [P, ND2 * 4], F32))
        convb = _st.enter_context(nc.sbuf_tensor("convb", [P, ND2], F32))
        dtb = _st.enter_context(nc.sbuf_tensor("dtb", [P, ND2], F32))
        dpar = _st.enter_context(nc.sbuf_tensor("dpar", [P, ND2], F32))
        nwb = _st.enter_context(nc.sbuf_tensor("nwb", [P, D], F32))
        bfb = _st.enter_context(nc.sbuf_tensor("bfb", [P, D], F32))
        kcol = _st.enter_context(nc.sbuf_tensor("kcol", [P, 4], F32))
        lns = _st.enter_context(nc.sbuf_tensor("lns", [P, 8], F32))
        qs = _st.enter_context(nc.sbuf_tensor("qs", [P, 3 * NQB], F32))
        hstate = _st.enter_context(nc.sbuf_tensor("hstate", [P, ND2 * NST], F32))
        tails = _st.enter_context(nc.sbuf_tensor("tails", [P, ND2 * 4], F32))
        wslab = _st.enter_context(nc.sbuf_tensor("wslab", [P, 2 * 8 * P], BF16))
        woslab = _st.enter_context(nc.sbuf_tensor("woslab", [P, 2 * D], BF16))
        xch = _st.enter_context(nc.sbuf_tensor("xch", [P, CPW * 4 * D], F32))
        scr = _st.enter_context(nc.sbuf_tensor("scr", [P, 5 * D], F32))
        tbuf = _st.enter_context(nc.sbuf_tensor("tbuf", [P, 8 * CH], BF16))
        uext = _st.enter_context(nc.sbuf_tensor("uext", [P, CH + 3], F32))
        cvt = _st.enter_context(nc.sbuf_tensor("cvt", [P, CH], F32))
        usbf = _st.enter_context(nc.sbuf_tensor("usbf", [P, CH], BF16))
        ussl = _st.enter_context(nc.sbuf_tensor("ussl", [P, CH], F32))
        dtr_bf = _st.enter_context(nc.sbuf_tensor("dtr_bf", [RK, CH], BF16))
        spl = _st.enter_context(nc.sbuf_tensor("spl", [P, 2 * CH], F32))
        dtsl = _st.enter_context(nc.sbuf_tensor("dtsl", [P, 2 * WIN], F32))
        dtusl = _st.enter_context(nc.sbuf_tensor("dtusl", [P, 2 * WIN], F32))
        dab = _st.enter_context(nc.sbuf_tensor("dab", [P, 2 * WIN], F32))
        xbb = _st.enter_context(nc.sbuf_tensor("xbb", [P, 2 * WIN], F32))
        hbb = _st.enter_context(nc.sbuf_tensor("hbb", [P, 2 * WIN], F32))
        hcb = _st.enter_context(nc.sbuf_tensor("hcb", [P, 2 * WIN], BF16))
        bbt = _st.enter_context(nc.sbuf_tensor("bbt", [P, WIN], F32))
        cbt = _st.enter_context(nc.sbuf_tensor("cbt", [P, WIN], F32))
        y3b = _st.enter_context(nc.sbuf_tensor("y3b", [P, WIN], BF16))
        y3ti = _st.enter_context(nc.sbuf_tensor("y3ti", [P, ND2 * P], BF16))
        posts = _st.enter_context(nc.sbuf_tensor("posts", [P, 2 * WIN], F32))
        ps = _st.enter_context(nc.psum_tensor("ps", [P, 4096], F32))
        s_sp = _st.enter_context(nc.semaphore("s_sp"))
        s_pq = _st.enter_context(nc.semaphore("s_pq"))
        s_pe = _st.enter_context(nc.semaphore("s_pe"))
        s_act = _st.enter_context(nc.semaphore("s_act"))
        s_dve = _st.enter_context(nc.semaphore("s_dve"))
        s_pool = _st.enter_context(nc.semaphore("s_pool"))
        block = _st.enter_context(nc.Block())
        sems = {"sp": s_sp, "pq": s_pq, "pe": s_pe, "act": s_act,
                "dve": s_dve, "pool": s_pool}

        def bank(i, w=512):
            return ps[:, i * 512:i * 512 + w]

        def pb(*idx):
            return [f"pb{i}" for i in idx]

        def dma(dst_ap, src_ap, reads=(), writes=(), q=SP, ser_tag=None):
            em.op(q, lambda e, o=dst_ap, i=src_ap: e.dma_start(out=o, in_=i),
                  reads=reads, writes=writes, ser_tag=ser_tag)

        def s_(i, w=D):
            return scr[:, i * D:i * D + w]

        em.op(POOL, lambda e: e.memset(ident[:], 0.0), writes=["ident"])
        em.op(POOL, lambda e: e.affine_select(
            out=ident[:], in_=ident[:], compare_op=ALU.not_equal, fill=1.0,
            base=0, pattern=[[-1, P]], channel_multiplier=1),
            reads=["ident"], writes=["ident"])
        em.op(DVE, lambda e: e.tensor_copy(identb[:], ident[:]),
              reads=["ident"], writes=["identb"])
        em.op(DVE, lambda e: e.memset(ones1[:], 1.0), writes=["ones1"])
        em.op(DVE, lambda e: e.memset(hstate[:], 0.0), writes=["hstate"])
        em.op(DVE, lambda e: e.memset(tails[:], 0.0), writes=["tails"])
        em.op(DVE, lambda e: e.memset(kcol[:], 0.0), writes=["kcol"])
        for j, val in ((0, 1e-5), (1, 1e-6), (2, 1.0)):
            em.op(DVE, lambda e, j=j, val=val: e.tensor_scalar(
                kcol[:, j:j + 1], kcol[:, j:j + 1], val, None, ALU.add),
                reads=["kcol"], writes=["kcol"])
        eps1, eps2, onec = kcol[:, 0:1], kcol[:, 1:2], kcol[:, 2:3]

        dma(convw[:], conv_w_l[:, :], writes=["convw"])
        dma(convb[:], conv_b_l[:, :], writes=["convb"])
        dma(dtb[:], dtb_l[:, :], writes=["dtb"])
        dma(dpar[:], d_l[:, :], writes=["dpar"])
        dma(a_t[:], alog_l[:, :], writes=["a_t"])
        em.op(ACT, lambda e: e.activation(a_t[:], a_t[:], AF.Exp),
              reads=["a_t"], writes=["a_t"])
        em.op(DVE, lambda e: e.tensor_scalar(a_t[:], a_t[:], -1.0, None, ALU.mult),
              reads=["a_t"], writes=["a_t"])

        def row_bcast(dst, src_row, tag):
            dma(dst[0:1, :], src_row, writes=[tag])
            p = 1
            while p < P:
                q = min(p, P - p)
                dma(dst[p:p + q, :], dst[0:q, :], reads=[tag], writes=[tag])
                p += q
        row_bcast(nwb, nrmw[:, :], "nwb")
        row_bcast(bfb, b_ffn[:, :], "bfb")

        for k in range(16):
            dma(s_(0, P), w_xp_g[k, :, :], reads=["scr0"], writes=["scr0"])
            em.op(DVE, lambda e, k=k: e.tensor_copy(
                xpT[:, k * P:(k + 1) * P], s_(0, P)),
                reads=["scr0"], writes=["xpT"])
        for h in range(2):
            dma(spl[0:RK, 0:D2 // 2], w_dtp[:, h * D2 // 2:(h + 1) * D2 // 2],
                reads=["spl"], writes=["spl"])
            em.op(DVE, lambda e, h=h: e.tensor_copy(
                dtpT[:, h * D2 // 2:(h + 1) * D2 // 2], spl[0:RK, 0:D2 // 2]),
                reads=["spl"], writes=["dtpT"])

        for m in range(32):
            dma(s_(0).rearrange("p (k c) -> p k c", c=P),
                w_in_g[m].rearrange("k p c -> p k c"),
                reads=["scr0"], writes=["scr0"])
            em.op(DVE, lambda e: e.tensor_copy(tbuf[:, 0:D], s_(0)),
                  reads=["scr0"], writes=["tbuf"])
            dma(win_bf[m, :, :], tbuf[:, 0:D], reads=["tbuf"], writes=["win_bf"])
        for k in range(16):
            dma(s_(0), w_out_g[k, :, :], reads=["scr0"], writes=["scr0"])
            em.op(DVE, lambda e: e.tensor_copy(tbuf[:, 0:D], s_(0)),
                  reads=["scr0"], writes=["tbuf"])
            dma(wout_bf[k, :, :], tbuf[:, 0:D], reads=["tbuf"], writes=["wout_bf"])

        def quantize(src_i, dst_i):
            sv, sa, sm, sw0, swt = (x for x in (src_i, 0, 1, 4, 2))
            v = s_(sv)
            aa = s_(sa)
            m1 = s_(sm)
            sgn = s_(dst_i)
            w0t = s_(sw0)
            wt = s_(swt)
            amax = qs[:, 0:NQB]
            sc = qs[:, NQB:2 * NQB]
            sinv = qs[:, 2 * NQB:3 * NQB]
            rd_src = [f"scr{src_i}"]
            em.op(DVE, lambda e: e.tensor_reduce(
                amax, v.rearrange("p (b k) -> p b k", k=NST), AX.X, ALU.max,
                apply_absolute_value=True), reads=rd_src, writes=["qs"])
            em.op(DVE, lambda e: e.tensor_scalar(
                sc, amax, 1.0 / 7.5, 1e-30, ALU.mult, ALU.max),
                reads=["qs"], writes=["qs"])
            em.op(DVE, lambda e: e.reciprocal(sinv, sc), reads=["qs"], writes=["qs"])
            em.op(DVE, lambda e: e.tensor_tensor(
                v.rearrange("p (b k) -> p b k", k=NST),
                v.rearrange("p (b k) -> p b k", k=NST),
                sinv.unsqueeze(-1).broadcast_to([P, NQB, NST]), ALU.mult),
                reads=rd_src + ["qs"], writes=rd_src)
            em.op(ACT, lambda e: e.activation(sgn, v, AF.Sign),
                  reads=rd_src, writes=[f"scr{dst_i}"])
            em.op(ACT, lambda e: e.activation(aa, v, AF.Abs),
                  reads=rd_src, writes=["scr0"])
            em.op(DVE, lambda e: e.tensor_scalar(
                m1, aa, 1.9375, -4.0, ALU.is_gt, ALU.mult),
                reads=["scr0"], writes=["scr1"])
            em.op(DVE, lambda e: e.tensor_scalar(
                v, aa, 3.875, -2.0, ALU.is_gt, ALU.mult),
                reads=["scr0"], writes=rd_src)
            em.op(DVE, lambda e: e.scalar_tensor_tensor(
                w0t, m1, -0.25, v, ALU.mult, ALU.subtract),
                reads=["scr1"] + rd_src, writes=["scr4"])
            em.op(DVE, lambda e: e.tensor_scalar(
                w0t, w0t, 0.125, 0.125, ALU.mult, ALU.add),
                reads=["scr4"], writes=["scr4"])
            em.op(DVE, lambda e: e.tensor_tensor(
                w0t.rearrange("p (b k) -> p b k", k=NST),
                w0t.rearrange("p (b k) -> p b k", k=NST),
                sc.unsqueeze(-1).broadcast_to([P, NQB, NST]), ALU.mult),
                reads=["scr4", "qs"], writes=["scr4"])
            em.op(DVE, lambda e: e.tensor_tensor(m1, m1, v, ALU.add),
                  reads=["scr1"] + rd_src, writes=["scr1"])
            em.op(DVE, lambda e: e.tensor_scalar(
                m1, m1, 0.125, 1.0, ALU.mult, ALU.add),
                reads=["scr1"], writes=["scr1"])
            em.op(DVE, lambda e: e.tensor_tensor(wt, aa, m1, ALU.mult),
                  reads=["scr0", "scr1"], writes=["scr2"])
            em.op(DVE, lambda e: e.tensor_scalar(
                wt, wt, 8.0, 0.5, ALU.mult, ALU.add),
                reads=["scr2"], writes=["scr2"])
            em.op(DVE, lambda e: e.tensor_copy(aa, wt),
                  reads=["scr2"], writes=["scr0"])
            for K in (32.0, 16.0, 8.0, 4.0, 2.0, 1.0):
                em.op(DVE, lambda e, K=K: e.tensor_scalar(
                    m1, aa, K, K, ALU.is_ge, ALU.mult),
                    reads=["scr0"], writes=["scr1"])
                em.op(DVE, lambda e: e.tensor_tensor(aa, aa, m1, ALU.subtract),
                      reads=["scr0", "scr1"], writes=["scr0"])
            em.op(DVE, lambda e: e.tensor_tensor(wt, wt, aa, ALU.subtract),
                  reads=["scr2", "scr0"], writes=["scr2"])
            em.op(DVE, lambda e: e.tensor_tensor(wt, wt, w0t, ALU.mult),
                  reads=["scr2", "scr4"], writes=["scr2"])
            em.op(DVE, lambda e: e.tensor_tensor(s_(dst_i), wt, sgn, ALU.mult),
                  reads=["scr2", f"scr{dst_i}"], writes=[f"scr{dst_i}"])

        for r in range(8):
            dma(s_(2), w_ffn[r * P:(r + 1) * P, :], reads=["scr2"], writes=["scr2"])
            quantize(2, 3)
            for kb in range(8):
                em.op(PE, lambda e, kb=kb: e.transpose(
                    bank(5, P), s_(3)[:, kb * P:(kb + 1) * P], ident[:]),
                    reads=["scr3", "ident"], writes=["pb5"])
                em.op(DVE, lambda e, kb=kb, r=r: e.tensor_copy(
                    wdT[:, kb * D + r * P:kb * D + (r + 1) * P], bank(5, P)),
                    reads=["pb5"], writes=["wdT"])

        if dumps:
            for kb in range(8):
                em.op(DVE, lambda e, kb=kb: e.tensor_copy(
                    s_(0), wdT[:, kb * D:(kb + 1) * D]),
                    reads=["wdT"], writes=["scr0"])
                dma(wdt_sp[:, kb * D:(kb + 1) * D], s_(0), reads=["scr0"])

        def layernorm(src_tag, src_ap, dst_ap, dst_tag, eps_col):
            sm, ssq, mcol, rstd = (lns[:, j:j + 1] for j in range(4))
            em.op(DVE, lambda e: e.reduce_sum(sm, src_ap, AX.X),
                  reads=[src_tag], writes=["lns"])
            em.op(ACT, lambda e: e.activation(
                s_(4), src_ap, AF.Square),
                reads=[src_tag], writes=["scr4"])
            em.op(DVE, lambda e: e.reduce_sum(ssq, s_(4), AX.X),
                  reads=["scr4"], writes=["lns"])
            em.op(DVE, lambda e: e.tensor_scalar(
                mcol, sm, 1.0 / D, None, ALU.mult), reads=["lns"], writes=["lns"])
            em.op(DVE, lambda e: e.scalar_tensor_tensor(
                sm, mcol, 1.0, mcol, ALU.mult, ALU.mult),
                reads=["lns"], writes=["lns"])
            em.op(DVE, lambda e: e.tensor_scalar(
                ssq, ssq, 1.0 / D, None, ALU.mult), reads=["lns"], writes=["lns"])
            em.op(DVE, lambda e: e.tensor_tensor(ssq, ssq, sm, ALU.subtract),
                  reads=["lns"], writes=["lns"])
            em.op(ACT, lambda e: e.activation(rstd, ssq, AF.Ln, bias=eps_col),
                  reads=["lns", "kcol"], writes=["lns"])
            em.op(ACT, lambda e: e.activation(rstd, rstd, AF.Exp, scale=-0.5),
                  reads=["lns"], writes=["lns"])
            em.op(DVE, lambda e: e.tensor_scalar(
                dst_ap, src_ap, mcol, rstd, ALU.subtract, ALU.mult),
                reads=[src_tag, "lns"], writes=[dst_tag])

        def transpose_in(src_i, dst, dst_tag, i, width=CH):
            for k in range(NKD):
                slot = k % 4
                em.op(PE, lambda e, k=k, slot=slot: e.transpose(
                    bank(5)[:, slot * P:(slot + 1) * P],
                    s_(src_i)[:, k * P:(k + 1) * P], ident[:]),
                    reads=[f"scr{src_i}", "ident"], writes=["pb5"])
                em.op(DVE, lambda e, k=k, slot=slot: e.tensor_copy(
                    dst[:, k * width + i * P:k * width + (i + 1) * P],
                    bank(5)[:, slot * P:(slot + 1) * P]),
                    reads=["pb5"], writes=[dst_tag])

        def ffn_block(ci, store_out=None, wi=0):
            for i in range(4):
                xw_ap = xch[:, (ci * 4 + i) * D:(ci * 4 + i + 1) * D]
                xwt = f"xch{ci}_{i}"
                layernorm(xwt, xw_ap, s_(2), "scr2", eps1)
                if dumps and store_out is None:
                    T0d = (wi * CPW + ci) * CH
                    dma(xn1_sp[T0d + i * P:T0d + (i + 1) * P, :], s_(2),
                        reads=["scr2"])
                quantize(2, 3)
                if dumps and store_out is None:
                    dma(xd_sp[T0d + i * P:T0d + (i + 1) * P, :], s_(3),
                        reads=["scr3"])
                transpose_in(3, tbuf, "tbuf", i)
            for i in range(4):
                xw_ap = xch[:, (ci * 4 + i) * D:(ci * 4 + i + 1) * D]
                xwt = f"xch{ci}_{i}"
                for k in range(NKD):
                    for h2 in range(2):
                        em.op(PE, lambda e, k=k, i=i, h2=h2: e.matmul(
                            bank(3 + h2),
                            tbuf[:, k * CH + i * P:k * CH + (i + 1) * P],
                            wdT[:, k * D + h2 * 512:k * D + (h2 + 1) * 512],
                            start=(k == 0), stop=(k == NKD - 1),
                            skip_group_check=True),
                            reads=["tbuf", "wdT"], writes=[f"pb{3 + h2}"])
                em.op(DVE, lambda e, xw_ap=xw_ap: e.scalar_tensor_tensor(
                    xw_ap, ps[:, 3 * 512:5 * 512], 0.5, xw_ap, ALU.mult, ALU.add),
                    reads=pb(3, 4) + [xwt], writes=[xwt])
                em.op(DVE, lambda e, xw_ap=xw_ap: e.scalar_tensor_tensor(
                    xw_ap, bfb[:], 0.5, xw_ap, ALU.mult, ALU.add),
                    reads=["bfb", xwt], writes=[xwt])
                if store_out is not None:
                    T0 = (wi * CPW + ci) * CH
                    dma(out_d[T0 + i * P:T0 + (i + 1) * P, :], xw_ap, reads=[xwt])

        for wi in range(NWIN):
            for ci in range(CPW):
                c = wi * CPW + ci
                T0 = c * CH

                for i in range(4):
                    xw_ap = xch[:, (ci * 4 + i) * D:(ci * 4 + i + 1) * D]
                    dma(xw_ap, x_in[T0 + i * P:T0 + (i + 1) * P, :],
                        reads=[f"xch{ci}_{i}"], writes=[f"xch{ci}_{i}"])

                ffn_block(ci)
                if dumps and c == 0:
                    for kk in range(8):
                        em.op(DVE, lambda e, kk=kk: e.tensor_copy(
                            s_(0, CH), tbuf[:, kk * CH:(kk + 1) * CH]),
                            reads=["tbuf"], writes=["scr0"])
                        dma(tb_sp[:, kk * CH:(kk + 1) * CH], s_(0, CH),
                            reads=["scr0"])
                if dumps:
                    for i in range(4):
                        dma(x1_sp[T0 + i * P:T0 + (i + 1) * P, :],
                            xch[:, (ci * 4 + i) * D:(ci * 4 + i + 1) * D],
                            reads=[f"xch{ci}_{i}"])

                for i in range(4):
                    xw_ap = xch[:, (ci * 4 + i) * D:(ci * 4 + i + 1) * D]
                    xwt = f"xch{ci}_{i}"
                    layernorm(xwt, xw_ap, s_(2), "scr2", eps1)
                    ssq2 = lns[:, 4:5]
                    rstd2 = lns[:, 5:6]
                    em.op(ACT, lambda e: e.activation(
                        s_(4), s_(2), AF.Square),
                        reads=["scr2"], writes=["scr4"])
                    em.op(DVE, lambda e, ssq2=ssq2: e.reduce_sum(
                        ssq2, s_(4), AX.X),
                        reads=["scr4"], writes=["lns"])
                    em.op(DVE, lambda e, ssq2=ssq2: e.tensor_scalar(
                        ssq2, ssq2, 1.0 / D, None, ALU.mult),
                        reads=["lns"], writes=["lns"])
                    em.op(ACT, lambda e, ssq2=ssq2, rstd2=rstd2: e.activation(
                        rstd2, ssq2, AF.Ln, bias=eps2),
                        reads=["lns", "kcol"], writes=["lns"])
                    em.op(ACT, lambda e, rstd2=rstd2: e.activation(
                        rstd2, rstd2, AF.Exp, scale=-0.5),
                        reads=["lns"], writes=["lns"])
                    em.op(DVE, lambda e, rstd2=rstd2: e.tensor_scalar(
                        s_(2), s_(2), rstd2, None, ALU.mult),
                        reads=["scr2", "lns"], writes=["scr2"])
                    em.op(DVE, lambda e: e.tensor_tensor(
                        s_(2), s_(2), nwb[:], ALU.mult),
                        reads=["scr2", "nwb"], writes=["scr2"])
                    transpose_in(2, tbuf, "tbuf", i)

                for m in range(32):
                    sl = m % 2
                    wsl = wslab[:, sl * 8 * P:(sl + 1) * 8 * P]
                    dma(wsl, win_bf[m, :, :], reads=[f"wsl{sl}", "win_bf"],
                        writes=[f"wsl{sl}"])
                    for k in range(NKD):
                        em.op(PE, lambda e, k=k, wsl=wsl, sl=sl: e.matmul(
                            bank(sl), wsl[:, k * P:(k + 1) * P],
                            tbuf[:, k * CH:(k + 1) * CH],
                            start=(k == 0), stop=(k == NKD - 1),
                            skip_group_check=True),
                            reads=[f"wsl{sl}", "tbuf"], writes=[f"pb{sl}"])
                    if m >= ND2:
                        em.op(ACT, lambda e, sl=sl: e.activation(
                            ussl[:], bank(sl), AF.Silu),
                            reads=[f"pb{sl}"], writes=["ussl"])
                        dma(sz_sp[m - ND2, :, T0:T0 + CH], ussl[:],
                            reads=["ussl"], writes=[f"szsp{m - ND2}"])
                        continue
                    em.op(DVE, lambda e, m=m: e.tensor_copy(
                        uext[:, 0:3], tails[:, m * 4:m * 4 + 3]),
                        reads=["tails"], writes=["uext"])
                    em.op(DVE, lambda e, sl=sl: e.tensor_copy(
                        uext[:, 3:3 + CH], bank(sl)),
                        reads=[f"pb{sl}"], writes=["uext"])
                    em.op(DVE, lambda e, m=m: e.tensor_copy(
                        tails[:, m * 4:m * 4 + 3], uext[:, CH:CH + 3]),
                        reads=["uext"], writes=["tails"])
                    em.op(DVE, lambda e, m=m: e.tensor_scalar(
                        cvt[:], uext[:, 0:CH], convw[:, m * 4:m * 4 + 1], None,
                        ALU.mult), reads=["uext", "convw"], writes=["cvt"])
                    for j in range(1, 4):
                        em.op(DVE, lambda e, m=m, j=j: e.scalar_tensor_tensor(
                            cvt[:], uext[:, j:j + CH],
                            convw[:, m * 4 + j:m * 4 + j + 1],
                            cvt[:], ALU.mult, ALU.add),
                            reads=["uext", "convw", "cvt"], writes=["cvt"])
                    em.op(ACT, lambda e, m=m: e.activation(
                        ussl[:], cvt[:], AF.Silu, bias=convb[:, m:m + 1]),
                        reads=["cvt", "convb"], writes=["ussl"])
                    dma(us_sp[m, :, T0:T0 + CH], ussl[:], reads=["ussl"],
                        writes=[f"ussp{m}"])
                    em.op(DVE, lambda e: e.tensor_copy(usbf[:], ussl[:]),
                          reads=["ussl"], writes=["usbf"])
                    em.op(PE, lambda e, m=m: e.matmul(
                        bank(2), xpT[:, m * P:(m + 1) * P], usbf[:],
                        start=(m == 0), stop=(m == ND2 - 1),
                        skip_group_check=True),
                        reads=["usbf", "xpT"], writes=["pb2"])

                em.op(DVE, lambda e: e.tensor_copy(spl[:, 0:CH], bank(2)),
                      reads=["pb2", "spl"], writes=["spl"])
                dma(b_sp[:, T0:T0 + CH], spl[0:NST, 0:CH], reads=["spl"],
                    writes=[f"bsp{c}"])
                dma(c_sp[:, T0:T0 + CH], spl[NST:2 * NST, 0:CH],
                    reads=["spl"], writes=[f"csp{c}"])
                dma(ussl[0:RK, :], spl[RK:P, 0:CH], reads=["spl", "ussl"],
                    writes=["ussl"])
                em.op(DVE, lambda e: e.tensor_copy(dtr_bf[:], ussl[0:RK, :]),
                      reads=["ussl"], writes=["dtrb"])

                for m in range(ND2):
                    sl = m % 2
                    em.op(PE, lambda e, m=m, sl=sl: e.matmul(
                        bank(sl), dtpT[:, m * P:(m + 1) * P], dtr_bf[:],
                        start=True, stop=True, skip_group_check=True),
                        reads=["dtrb", "dtpT"], writes=[f"pb{sl}"])
                    aa = spl[:, 0:CH]
                    ll = spl[:, CH:2 * CH]
                    em.op(ACT, lambda e, m=m, sl=sl: e.activation(
                        aa, bank(sl), AF.Abs, bias=dtb[:, m:m + 1]),
                        reads=[f"pb{sl}", "dtb"], writes=["spl"])
                    em.op(ACT, lambda e, aa=aa, ll=ll: e.activation(
                        ll, aa, AF.Exp, scale=-1.0), reads=["spl"], writes=["spl"])
                    em.op(ACT, lambda e, ll=ll: e.activation(
                        ll, ll, AF.Ln, bias=onec),
                        reads=["spl", "kcol"], writes=["spl"])
                    em.op(DVE, lambda e, m=m, sl=sl, aa=aa: e.tensor_scalar(
                        aa, bank(sl), dtb[:, m:m + 1], 0.0, ALU.add, ALU.max),
                        reads=[f"pb{sl}", "dtb"], writes=["spl"])
                    em.op(DVE, lambda e, aa=aa, ll=ll: e.tensor_tensor(
                        aa, aa, ll, ALU.add), reads=["spl"], writes=["spl"])
                    dma(dt_sp[m, :, T0:T0 + CH], aa, reads=["spl"], writes=[f"dtsp{m}"])
                    dma(ussl[:], us_sp[m, :, T0:T0 + CH],
                        reads=["ussl", f"ussp{m}"], writes=["ussl"])
                    em.op(DVE, lambda e, aa=aa: e.tensor_tensor(
                        cvt[:], aa, ussl[:], ALU.mult),
                        reads=["spl", "ussl"], writes=["cvt"])
                    dma(dtu_sp[m, :, T0:T0 + CH], cvt[:], reads=["cvt"],
                        writes=[f"dtusp{m}"])

            import os as _os
            _ks = _os.environ.get("KSERIAL", "0")
            if _ks == "1":
                em.serial = True
            elif _ks == "0":
                em.serial = False
            elif _ks.startswith("except:"):
                em.serial = _ks
            else:
                em.serial = _ks.split(",")
            W0 = wi * WIN
            blocks = [(m0, 2) for m0 in range(0, 16, 2)]
            for bi, (m0, bsz) in enumerate(blocks):
                yb = (bi % 2) * 4
                for j in range(bsz):
                    m = m0 + j
                    dma(dtsl[:, j * WIN:(j + 1) * WIN], dt_sp[m, :, W0:W0 + WIN],
                        reads=[f"dtsl{j}", f"dtsp{m}"], writes=[f"dtsl{j}"],
                        ser_tag="sblk")
                    dma(dtusl[:, j * WIN:(j + 1) * WIN],
                        dtu_sp[m, :, W0:W0 + WIN],
                        reads=[f"dtusl{j}", f"dtusp{m}"],
                        writes=[f"dtusl{j}"], ser_tag="sblk")
                fences = []
                for n in range(NST):
                    fences.append({e: em.n[e] - 1 for e in ENGS if em.n[e]})
                    _fd = int(_os.environ.get("KFENCE", "3"))
                    if _fd > 0 and len(fences) >= _fd:
                        em.barrier = fences[-_fd]
                    em.op(SP, lambda e, n=n: e.dma_start(
                        out=bbt[:], in_=b_sp[n:n + 1, W0:W0 + WIN]
                        .broadcast_to([P, WIN])),
                        reads=["bbt", f"bsp{2 * wi}", f"bsp{2 * wi + 1}"],
                        writes=["bbt"], ser_tag="srow")
                    em.op(SP, lambda e, n=n: e.dma_start(
                        out=cbt[:], in_=c_sp[n:n + 1, W0:W0 + WIN]
                        .broadcast_to([P, WIN])),
                        reads=["cbt", f"csp{2 * wi}", f"csp{2 * wi + 1}"],
                        writes=["cbt"], ser_tag="srow")
                    for j in range(bsz):
                        m = m0 + j
                        ii = (n * bsz + j) % 2
                        da = dab[:, ii * WIN:(ii + 1) * WIN]
                        xx = xbb[:, ii * WIN:(ii + 1) * WIN]
                        hh = hbb[:, ii * WIN:(ii + 1) * WIN]
                        hc = hcb[:, j * WIN:(j + 1) * WIN]
                        scol = m * NST + n
                        em.op(ACT, lambda e, j=j, m=m, n=n, da=da: e.activation(
                            da, dtsl[:, j * WIN:(j + 1) * WIN], AF.Exp,
                            scale=a_t[:, m * NST + n:m * NST + n + 1]),
                            reads=[f"dtsl{j}", "a_t"], writes=[f"da{ii}"],
                            ser_tag="sda")
                        xeng = DVE if j == 0 else POOL
                        em.op(xeng, lambda e, j=j, xx=xx: e.tensor_tensor(
                            xx, dtusl[:, j * WIN:(j + 1) * WIN], bbt[:], ALU.mult),
                            reads=[f"dtusl{j}", "bbt"], writes=[f"xb{ii}"],
                            ser_tag="sxb")
                        em.op(DVE, lambda e, da=da, xx=xx, hh=hh, scol=scol:
                              e.tensor_tensor_scan(
                                  hh, da, xx, hstate[:, scol:scol + 1],
                                  ALU.mult, ALU.add),
                              reads=[f"da{ii}", f"xb{ii}", "hstate"],
                              writes=[f"hb{ii}"], ser_tag="sscan")
                        em.op(DVE, lambda e, hh=hh, scol=scol: e.tensor_copy(
                            hstate[:, scol:scol + 1], hh[:, WIN - 1:WIN]),
                            reads=[f"hb{ii}"], writes=["hstate"],
                            ser_tag="shend")
                        hc_eng = DVE if j == 0 else POOL
                        em.op(hc_eng, lambda e, hh=hh, hc=hc: e.tensor_tensor(
                            hc, hh, cbt[:], ALU.mult),
                            reads=[f"hb{ii}", "cbt"], writes=[f"hc{j}"],
                            ser_tag="shc")
                        for h2 in range(2):
                            em.op(PE, lambda e, j=j, hc=hc, h2=h2, n=n, yb=yb:
                                  e.matmul(
                                      ps[:, (yb + 2 * j + h2) * 512:
                                         (yb + 2 * j + h2 + 1) * 512],
                                      identb[:], hc[:, h2 * 512:(h2 + 1) * 512],
                                      start=(n == 0), stop=(n == NST - 1),
                                      skip_group_check=True),
                                  reads=[f"hc{j}", "identb"],
                                  writes=[f"pb{yb + 2 * j + h2}"], ser_tag="symm")
                em.barrier = None
                for j in range(bsz):
                    m = m0 + j
                    uD = posts[:, 0:WIN]
                    sz = posts[:, WIN:2 * WIN]
                    dma(uD, us_sp[m, :, W0:W0 + WIN],
                        reads=["posts", f"ussp{m}"], writes=["posts"],
                        ser_tag="spost")
                    dma(sz, sz_sp[m, :, W0:W0 + WIN],
                        reads=["posts2", f"szsp{m}"], writes=["posts2"],
                        ser_tag="spost")
                    em.op(DVE, lambda e, uD=uD, m=m: e.tensor_scalar(
                        uD, uD, dpar[:, m:m + 1], None, ALU.mult),
                        reads=["posts", "dpar"], writes=["posts"],
                        ser_tag="spost")
                    em.op(DVE, lambda e, j=j, uD=uD, yb=yb: e.tensor_tensor(
                        uD, ps[:, (yb + 2 * j) * 512:(yb + 2 * j + 2) * 512],
                        uD, ALU.add),
                        reads=pb(yb + 2 * j, yb + 2 * j + 1) + ["posts"],
                        writes=["posts"], ser_tag="spost")
                    em.op(DVE, lambda e, uD=uD, sz=sz: e.tensor_tensor(
                        y3b[:], uD, sz, ALU.mult),
                        reads=["posts", "posts2"], writes=["y3b"],
                        ser_tag="spost")
                    dma(y3_sp[m, :, W0:W0 + WIN], y3b[:], reads=["y3b"],
                        writes=[f"y3sp{m}"], ser_tag="spost")

            em.serial = False
            if _os.environ.get("KSTAGEB", "0") == "1":
                em.barrier = {e: em.n[e] - 1 for e in ENGS if em.n[e]}
            for ti in range(8):
                ci, i = divmod(ti, 4)
                tsl = 0
                yti = y3ti[:, :]
                dma(yti.rearrange("p (m c) -> p m c", c=P),
                    y3_sp[:, :, W0 + ti * P:W0 + (ti + 1) * P].rearrange(
                        "m p c -> p m c"),
                    reads=[f"y3ti{tsl}"] + [f"y3sp{mm}" for mm in range(ND2)],
                    writes=[f"y3ti{tsl}"])
                for k in range(16):
                    sl = k % 2
                    wsl_ap = woslab[:, sl * D:(sl + 1) * D]
                    dma(wsl_ap, wout_bf[k, :, :], reads=[f"wo{sl}", "wout_bf"],
                        writes=[f"wo{sl}"])
                    for h2 in range(2):
                        em.op(PE, lambda e, k=k, h2=h2, yti=yti, wsl_ap=wsl_ap,
                              tsl=tsl: e.matmul(
                                  bank(3 + h2),
                                  yti[:, k * P:(k + 1) * P],
                                  wsl_ap[:, h2 * 512:(h2 + 1) * 512],
                                  start=(k == 0), stop=(k == 15),
                                  skip_group_check=True),
                              reads=[f"y3ti{tsl}", f"wo{sl}"],
                              writes=[f"pb{3 + h2}"])
                xw_ap = xch[:, ti * D:(ti + 1) * D]
                xwt = f"xch{ci}_{i}"
                layernorm(xwt, xw_ap, s_(2), "scr2", eps1)
                em.op(DVE, lambda e, xw_ap=xw_ap: e.tensor_tensor(
                    xw_ap, s_(2), xw_ap, ALU.add),
                    reads=["scr2", xwt], writes=[xwt])
                em.op(DVE, lambda e, xw_ap=xw_ap: e.tensor_tensor(
                    xw_ap, ps[:, 3 * 512:5 * 512], xw_ap, ALU.add),
                    reads=pb(3, 4) + [xwt], writes=[xwt])

            for ci in range(CPW):
                ffn_block(ci, store_out=True, wi=wi)


        em.finalize()
        streams = {"sp": [], "pe": [], "act": [], "dve": [], "pool": []}
        for eng, fn, waits, seq in em.ops:
            stream = "pool" if eng in ("pool", "pq") else eng
            streams[stream].append((eng, fn, waits, seq))

        def play(stream_name, handle):
            waited = {}
            for eng, fn, waits, seq in streams[stream_name]:
                for we, ws in waits.items():
                    val = em.rank[we][ws]
                    if we in ("sp", "pq"):
                        val *= 16
                    if val > 0 and waited.get(we, -1) < val:
                        handle.wait_ge(sems[we], val)
                        waited[we] = val
                inst = fn(handle)
                if seq in em.needed[eng]:
                    inst.then_inc(sems[eng], 16 if eng in ("sp", "pq") else 1)
            if stream_name == "sp":
                handle.wait_ge(sems["sp"], 16 * em.ninc["sp"])
                if em.ninc["pq"]:
                    handle.wait_ge(sems["pq"], 16 * em.ninc["pq"])

        @block.sync
        def _(sync):
            play("sp", sync)

        @block.tensor
        def _(t):
            play("pe", t)

        @block.scalar
        def _(s):
            play("act", s)

        @block.vector
        def _(v):
            play("dve", v)

        @block.gpsimd
        def _(g):
            play("pool", g)

    return nc


_NC_CACHE = {}


def kernel(L0_attn_x, L0_attn_norm_w, L0_attn_in_proj_w, L0_attn_conv_w,
           L0_attn_conv_b, L0_attn_x_proj_w, L0_attn_dt_proj_w, L0_attn_dt_proj_b,
           L0_attn_A_log, L0_attn_D_param, L0_attn_out_proj_w, L0_ffn_weight,
           L0_ffn_bias):
    B = L0_attn_x.shape[0]
    f32 = np.float32

    def lay(a):
        return np.ascontiguousarray(np.asarray(a), dtype=f32)

    w_in_g = lay(np.asarray(L0_attn_in_proj_w).T
                 .reshape(8, 128, 32, 128).transpose(2, 0, 1, 3))
    w_out_g = lay(np.asarray(L0_attn_out_proj_w).T.reshape(16, 128, 1024))
    perm = list(range(64, 96)) + list(range(96, 128)) + list(range(0, 64))
    w_xp_g = lay(np.asarray(L0_attn_x_proj_w)[perm].T.reshape(16, 128, 128))
    w_dtp = lay(np.asarray(L0_attn_dt_proj_w).T)
    w_ffn = lay(L0_ffn_weight)
    b_ffn = lay(np.asarray(L0_ffn_bias).reshape(1, 1024))
    nrmw = lay(np.asarray(L0_attn_norm_w).reshape(1, 1024))
    conv_w_l = lay(np.asarray(L0_attn_conv_w).reshape(16, 128, 4)
                   .transpose(1, 0, 2).reshape(128, 64))
    conv_b_l = lay(np.asarray(L0_attn_conv_b).reshape(16, 128).T)
    dtb_l = lay(np.asarray(L0_attn_dt_proj_b).reshape(16, 128).T)
    d_l = lay(np.asarray(L0_attn_D_param).reshape(16, 128).T)
    alog_l = lay(np.asarray(L0_attn_A_log).reshape(16, 128, 32)
                 .transpose(1, 0, 2).reshape(128, 512))

    if "nc" not in _NC_CACHE:
        _NC_CACHE["nc"] = build_core_program()
    nc = _NC_CACHE["nc"]

    shared = {
        "w_in_g": w_in_g, "w_out_g": w_out_g, "w_xp_g": w_xp_g, "w_dtp": w_dtp,
        "w_ffn": w_ffn, "b_ffn": b_ffn, "nrmw": nrmw, "conv_w_l": conv_w_l,
        "conv_b_l": conv_b_l, "dtb_l": dtb_l, "d_l": d_l, "alog_l": alog_l,
    }
    in_maps = []
    for b in range(B):
        m = dict(shared)
        m["x"] = lay(L0_attn_x[b])
        in_maps.append(m)

    res = run_bass_kernel_spmd(nc, in_maps, core_ids=list(range(B)))
    out = np.stack([res.results[b]["out"] for b in range(B)], axis=0)
    return out.astype(np.float32)


# version 13
# speedup vs baseline: 1.9609x; 1.0038x over previous
import numpy as np
import concourse.bass as bass
import concourse.mybir as mybir
from concourse.bass_utils import run_bass_kernel_spmd

F32 = mybir.dt.float32
BF16 = mybir.dt.bfloat16
AF = mybir.ActivationFunctionType
ALU = mybir.AluOpType
AX = mybir.AxisListType

P = 128
T = 4096
D = 1024
D2 = 2048
NST = 32
RK = 64
CH = 512
NCH = T // CH
WIN = 1024
NWIN = T // WIN
CPW = WIN // CH
ND2 = D2 // P
NKD = D // P
NQB = D // NST

ENGS = ("sp", "pq", "pe", "act", "dve", "pool")


class Em:
    def __init__(self):
        self.ops = []
        self.n = {e: 0 for e in ENGS}
        self.writer = {}
        self.readers = {}
        self.serial = False
        self.barrier = None

    def op(self, eng, fn, reads=(), writes=(), ser_tag=None):
        waits = {}
        if self.barrier:
            for e2, s2 in self.barrier.items():
                if e2 != eng and waits.get(e2, -1) < s2:
                    waits[e2] = s2
                if e2 == eng and eng != "pe" and waits.get(e2, -1) < s2:
                    waits[e2] = s2
        ser = False
        if self.serial:
            if self.serial is True:
                ser = True
            elif isinstance(self.serial, str) and self.serial.startswith("except:"):
                ser = (ser_tag or "") not in self.serial[7:].split(",")
            elif eng in self.serial or (ser_tag and ser_tag in self.serial):
                ser = True
        if ser:
            for e2 in ENGS:
                ns = self.n[e2] - (1 if e2 == eng else 0)
                if ns > 0:
                    waits[e2] = ns - 1

        def need(dep):
            if dep is None:
                return
            e, s = dep
            if e == eng:
                if e == "pe":
                    return
            if e == "pool" and eng == "pq":
                return
            if waits.get(e, -1) < s:
                waits[e] = s

        for r in reads:
            need(self.writer.get(r))
        for w in writes:
            need(self.writer.get(w))
            for rd in self.readers.get(w, ()):
                need(rd)
        seq = self.n[eng]
        if eng in ("sp", "pq") and seq > 0 and waits.get(eng, -1) < seq - 1:
            waits[eng] = seq - 1
        self.ops.append((eng, fn, waits, seq))
        self.n[eng] += 1
        me = (eng, seq)
        for w in writes:
            self.writer[w] = me
            self.readers[w] = []
        for r in reads:
            self.readers.setdefault(r, []).append(me)

    def finalize(self):
        needed = {e: set() for e in ENGS}
        for eng, fn, waits, seq in self.ops:
            for e, s in waits.items():
                needed[e].add(s)
        needed["sp"] = set(range(self.n["sp"]))
        needed["pq"] = set(range(self.n["pq"]))
        self.rank = {}
        self.ninc = {}
        for e in ENGS:
            cnt = 0
            rm = {}
            for s in range(self.n[e]):
                if s in needed[e]:
                    cnt += 1
                rm[s] = cnt
            self.rank[e] = rm
            self.ninc[e] = cnt
        self.needed = needed


def build_core_program(debug=False):
    nc = bass.Bass("TRN2", target_bir_lowering=False)
    spk = "ExternalOutput" if debug else "Internal"
    dumps = (debug is True)

    def din(name, shape, dt=F32):
        return nc.dram_tensor(name, list(shape), dt, kind="ExternalInput").ap()

    x_in = din("x", (T, D))
    w_in_g = din("w_in_g", (32, 8, P, P))
    w_out_g = din("w_out_g", (16, P, D))
    w_xp_g = din("w_xp_g", (16, P, P))
    w_dtp = din("w_dtp", (RK, D2))
    w_ffn = din("w_ffn", (D, D))
    b_ffn = din("b_ffn", (1, D))
    nrmw = din("nrmw", (1, D))
    conv_w_l = din("conv_w_l", (P, ND2 * 4))
    conv_b_l = din("conv_b_l", (P, ND2))
    dtb_l = din("dtb_l", (P, ND2))
    d_l = din("d_l", (P, ND2))
    alog_l = din("alog_l", (P, ND2 * NST))

    out_d = nc.dram_tensor("out", [T, D], F32, kind="ExternalOutput").ap()
    x1_sp = nc.dram_tensor("x1_sp", [T, D], F32, kind=spk).ap() if dumps else None
    xn1_sp = nc.dram_tensor("xn1_sp", [T, D], F32, kind=spk).ap() if dumps else None
    xd_sp = nc.dram_tensor("xd_sp", [T, D], F32, kind=spk).ap() if dumps else None
    wdt_sp = nc.dram_tensor("wdt_sp", [P, 8 * D], F32, kind=spk).ap() if dumps else None
    tb_sp = nc.dram_tensor("tb_sp", [P, 8 * CH], F32, kind=spk).ap() if dumps else None

    win_bf = nc.dram_tensor("win_bf", [32, P, 8 * P], BF16, kind="Internal").ap()
    wout_bf = nc.dram_tensor("wout_bf", [16, P, D], BF16, kind="Internal").ap()
    dt_sp = nc.dram_tensor("dt_sp", [ND2, P, T], F32, kind=spk).ap()
    dtu_sp = nc.dram_tensor("dtu_sp", [ND2, P, T], F32, kind=spk).ap()
    us_sp = nc.dram_tensor("us_sp", [ND2, P, T], F32, kind=spk).ap()
    sz_sp = nc.dram_tensor("sz_sp", [ND2, P, T], F32, kind=spk).ap()
    y3_sp = nc.dram_tensor("y3_sp", [ND2, P, T], BF16, kind=spk).ap()
    b_sp = nc.dram_tensor("b_sp", [NST, T], F32, kind=spk).ap()
    c_sp = nc.dram_tensor("c_sp", [NST, T], F32, kind=spk).ap()

    em = Em()
    SP, PQ, PE, ACT, DVE, POOL = ENGS

    import contextlib
    with contextlib.ExitStack() as _st:
        ident = _st.enter_context(nc.sbuf_tensor("ident", [P, P], F32))
        identb = _st.enter_context(nc.sbuf_tensor("identb", [P, P], BF16))
        ones1 = _st.enter_context(nc.sbuf_tensor("ones1", [1, P], F32))
        wdT = _st.enter_context(nc.sbuf_tensor("wdT", [P, 8 * D], BF16))
        xpT = _st.enter_context(nc.sbuf_tensor("xpT", [P, 16 * P], BF16))
        dtpT = _st.enter_context(nc.sbuf_tensor("dtpT", [RK, D2], BF16))
        a_t = _st.enter_context(nc.sbuf_tensor("a_t", [P, ND2 * NST], F32))
        convw = _st.enter_context(nc.sbuf_tensor("convw", [P, ND2 * 4], F32))
        convb = _st.enter_context(nc.sbuf_tensor("convb", [P, ND2], F32))
        dtb = _st.enter_context(nc.sbuf_tensor("dtb", [P, ND2], F32))
        dpar = _st.enter_context(nc.sbuf_tensor("dpar", [P, ND2], F32))
        nwb = _st.enter_context(nc.sbuf_tensor("nwb", [P, D], F32))
        bfb = _st.enter_context(nc.sbuf_tensor("bfb", [P, D], F32))
        kcol = _st.enter_context(nc.sbuf_tensor("kcol", [P, 4], F32))
        lns = _st.enter_context(nc.sbuf_tensor("lns", [P, 8], F32))
        qs = _st.enter_context(nc.sbuf_tensor("qs", [P, 3 * NQB], F32))
        hstate = _st.enter_context(nc.sbuf_tensor("hstate", [P, ND2 * NST], F32))
        tails = _st.enter_context(nc.sbuf_tensor("tails", [P, ND2 * 4], F32))
        wslab = _st.enter_context(nc.sbuf_tensor("wslab", [P, 2 * 8 * P], BF16))
        woslab = _st.enter_context(nc.sbuf_tensor("woslab", [P, 2 * D], BF16))
        xch = _st.enter_context(nc.sbuf_tensor("xch", [P, CPW * 4 * D], F32))
        scr = _st.enter_context(nc.sbuf_tensor("scr", [P, 5 * D], F32))
        tbuf = _st.enter_context(nc.sbuf_tensor("tbuf", [P, 8 * CH], BF16))
        uext = _st.enter_context(nc.sbuf_tensor("uext", [P, CH + 3], F32))
        cvt = _st.enter_context(nc.sbuf_tensor("cvt", [P, CH], F32))
        usbf = _st.enter_context(nc.sbuf_tensor("usbf", [P, CH], BF16))
        ussl = _st.enter_context(nc.sbuf_tensor("ussl", [P, CH], F32))
        dtr_bf = _st.enter_context(nc.sbuf_tensor("dtr_bf", [RK, CH], BF16))
        spl = _st.enter_context(nc.sbuf_tensor("spl", [P, 2 * CH], F32))
        dtsl = _st.enter_context(nc.sbuf_tensor("dtsl", [P, 2 * WIN], F32))
        dtusl = _st.enter_context(nc.sbuf_tensor("dtusl", [P, 2 * WIN], F32))
        dab = _st.enter_context(nc.sbuf_tensor("dab", [P, 2 * WIN], F32))
        xbb = _st.enter_context(nc.sbuf_tensor("xbb", [P, 2 * WIN], F32))
        hbb = _st.enter_context(nc.sbuf_tensor("hbb", [P, 2 * WIN], F32))
        hcb = _st.enter_context(nc.sbuf_tensor("hcb", [P, 2 * WIN], BF16))
        bbt = _st.enter_context(nc.sbuf_tensor("bbt", [P, WIN], F32))
        cbt = _st.enter_context(nc.sbuf_tensor("cbt", [P, WIN], F32))
        y3b = _st.enter_context(nc.sbuf_tensor("y3b", [P, WIN], BF16))
        y3ti = _st.enter_context(nc.sbuf_tensor("y3ti", [P, ND2 * P], BF16))
        posts = _st.enter_context(nc.sbuf_tensor("posts", [P, 2 * WIN], F32))
        ps = _st.enter_context(nc.psum_tensor("ps", [P, 4096], F32))
        s_sp = _st.enter_context(nc.semaphore("s_sp"))
        s_pq = _st.enter_context(nc.semaphore("s_pq"))
        s_pe = _st.enter_context(nc.semaphore("s_pe"))
        s_act = _st.enter_context(nc.semaphore("s_act"))
        s_dve = _st.enter_context(nc.semaphore("s_dve"))
        s_pool = _st.enter_context(nc.semaphore("s_pool"))
        block = _st.enter_context(nc.Block())
        sems = {"sp": s_sp, "pq": s_pq, "pe": s_pe, "act": s_act,
                "dve": s_dve, "pool": s_pool}

        def bank(i, w=512):
            return ps[:, i * 512:i * 512 + w]

        def pb(*idx):
            return [f"pb{i}" for i in idx]

        def dma(dst_ap, src_ap, reads=(), writes=(), q=SP, ser_tag=None):
            em.op(q, lambda e, o=dst_ap, i=src_ap: e.dma_start(out=o, in_=i),
                  reads=reads, writes=writes, ser_tag=ser_tag)

        def s_(i, w=D):
            return scr[:, i * D:i * D + w]

        em.op(POOL, lambda e: e.memset(ident[:], 0.0), writes=["ident"])
        em.op(POOL, lambda e: e.affine_select(
            out=ident[:], in_=ident[:], compare_op=ALU.not_equal, fill=1.0,
            base=0, pattern=[[-1, P]], channel_multiplier=1),
            reads=["ident"], writes=["ident"])
        em.op(DVE, lambda e: e.tensor_copy(identb[:], ident[:]),
              reads=["ident"], writes=["identb"])
        em.op(DVE, lambda e: e.memset(ones1[:], 1.0), writes=["ones1"])
        em.op(DVE, lambda e: e.memset(hstate[:], 0.0), writes=["hstate"])
        em.op(DVE, lambda e: e.memset(tails[:], 0.0), writes=["tails"])
        em.op(DVE, lambda e: e.memset(kcol[:], 0.0), writes=["kcol"])
        for j, val in ((0, 1e-5), (1, 1e-6), (2, 1.0)):
            em.op(DVE, lambda e, j=j, val=val: e.tensor_scalar(
                kcol[:, j:j + 1], kcol[:, j:j + 1], val, None, ALU.add),
                reads=["kcol"], writes=["kcol"])
        eps1, eps2, onec = kcol[:, 0:1], kcol[:, 1:2], kcol[:, 2:3]

        dma(convw[:], conv_w_l[:, :], writes=["convw"])
        dma(convb[:], conv_b_l[:, :], writes=["convb"])
        dma(dtb[:], dtb_l[:, :], writes=["dtb"])
        dma(dpar[:], d_l[:, :], writes=["dpar"])
        dma(a_t[:], alog_l[:, :], writes=["a_t"])
        em.op(ACT, lambda e: e.activation(a_t[:], a_t[:], AF.Exp),
              reads=["a_t"], writes=["a_t"])
        em.op(DVE, lambda e: e.tensor_scalar(a_t[:], a_t[:], -1.0, None, ALU.mult),
              reads=["a_t"], writes=["a_t"])

        def row_bcast(dst, src_row, tag):
            dma(dst[0:1, :], src_row, writes=[tag])
            p = 1
            while p < P:
                q = min(p, P - p)
                dma(dst[p:p + q, :], dst[0:q, :], reads=[tag], writes=[tag])
                p += q
        row_bcast(nwb, nrmw[:, :], "nwb")
        row_bcast(bfb, b_ffn[:, :], "bfb")

        for k in range(16):
            dma(s_(0, P), w_xp_g[k, :, :], reads=["scr0"], writes=["scr0"])
            em.op(DVE, lambda e, k=k: e.tensor_copy(
                xpT[:, k * P:(k + 1) * P], s_(0, P)),
                reads=["scr0"], writes=["xpT"])
        for h in range(2):
            dma(spl[0:RK, 0:D2 // 2], w_dtp[:, h * D2 // 2:(h + 1) * D2 // 2],
                reads=["spl"], writes=["spl"])
            em.op(DVE, lambda e, h=h: e.tensor_copy(
                dtpT[:, h * D2 // 2:(h + 1) * D2 // 2], spl[0:RK, 0:D2 // 2]),
                reads=["spl"], writes=["dtpT"])

        for m in range(32):
            dma(s_(0).rearrange("p (k c) -> p k c", c=P),
                w_in_g[m].rearrange("k p c -> p k c"),
                reads=["scr0"], writes=["scr0"])
            em.op(DVE, lambda e: e.tensor_copy(tbuf[:, 0:D], s_(0)),
                  reads=["scr0"], writes=["tbuf"])
            dma(win_bf[m, :, :], tbuf[:, 0:D], reads=["tbuf"], writes=["win_bf"])
        for k in range(16):
            dma(s_(0), w_out_g[k, :, :], reads=["scr0"], writes=["scr0"])
            em.op(DVE, lambda e: e.tensor_copy(tbuf[:, 0:D], s_(0)),
                  reads=["scr0"], writes=["tbuf"])
            dma(wout_bf[k, :, :], tbuf[:, 0:D], reads=["tbuf"], writes=["wout_bf"])

        def quantize(src_i, dst_i):
            sv, sa, sm, sw0, swt = (x for x in (src_i, 0, 1, 4, 2))
            v = s_(sv)
            aa = s_(sa)
            m1 = s_(sm)
            sgn = s_(dst_i)
            w0t = s_(sw0)
            wt = s_(swt)
            amax = qs[:, 0:NQB]
            sc = qs[:, NQB:2 * NQB]
            sinv = qs[:, 2 * NQB:3 * NQB]
            rd_src = [f"scr{src_i}"]
            em.op(DVE, lambda e: e.tensor_reduce(
                amax, v.rearrange("p (b k) -> p b k", k=NST), AX.X, ALU.max,
                apply_absolute_value=True), reads=rd_src, writes=["qs"])
            em.op(DVE, lambda e: e.tensor_scalar(
                sc, amax, 1.0 / 7.5, 1e-30, ALU.mult, ALU.max),
                reads=["qs"], writes=["qs"])
            em.op(DVE, lambda e: e.reciprocal(sinv, sc), reads=["qs"], writes=["qs"])
            em.op(DVE, lambda e: e.tensor_tensor(
                v.rearrange("p (b k) -> p b k", k=NST),
                v.rearrange("p (b k) -> p b k", k=NST),
                sinv.unsqueeze(-1).broadcast_to([P, NQB, NST]), ALU.mult),
                reads=rd_src + ["qs"], writes=rd_src)
            em.op(ACT, lambda e: e.activation(sgn, v, AF.Sign),
                  reads=rd_src, writes=[f"scr{dst_i}"])
            em.op(ACT, lambda e: e.activation(aa, v, AF.Abs),
                  reads=rd_src, writes=["scr0"])
            em.op(DVE, lambda e: e.tensor_scalar(
                m1, aa, 1.9375, -4.0, ALU.is_gt, ALU.mult),
                reads=["scr0"], writes=["scr1"])
            em.op(DVE, lambda e: e.tensor_scalar(
                v, aa, 3.875, -2.0, ALU.is_gt, ALU.mult),
                reads=["scr0"], writes=rd_src)
            em.op(DVE, lambda e: e.scalar_tensor_tensor(
                w0t, m1, -0.25, v, ALU.mult, ALU.subtract),
                reads=["scr1"] + rd_src, writes=["scr4"])
            em.op(DVE, lambda e: e.tensor_scalar(
                w0t, w0t, 0.125, 0.125, ALU.mult, ALU.add),
                reads=["scr4"], writes=["scr4"])
            em.op(DVE, lambda e: e.tensor_tensor(
                w0t.rearrange("p (b k) -> p b k", k=NST),
                w0t.rearrange("p (b k) -> p b k", k=NST),
                sc.unsqueeze(-1).broadcast_to([P, NQB, NST]), ALU.mult),
                reads=["scr4", "qs"], writes=["scr4"])
            em.op(DVE, lambda e: e.tensor_tensor(m1, m1, v, ALU.add),
                  reads=["scr1"] + rd_src, writes=["scr1"])
            em.op(DVE, lambda e: e.tensor_scalar(
                m1, m1, 0.125, 1.0, ALU.mult, ALU.add),
                reads=["scr1"], writes=["scr1"])
            em.op(DVE, lambda e: e.tensor_tensor(wt, aa, m1, ALU.mult),
                  reads=["scr0", "scr1"], writes=["scr2"])
            em.op(DVE, lambda e: e.tensor_scalar(
                wt, wt, 8.0, 0.5, ALU.mult, ALU.add),
                reads=["scr2"], writes=["scr2"])
            em.op(DVE, lambda e: e.tensor_copy(aa, wt),
                  reads=["scr2"], writes=["scr0"])
            for K in (32.0, 16.0, 8.0, 4.0, 2.0, 1.0):
                em.op(DVE, lambda e, K=K: e.tensor_scalar(
                    m1, aa, K, K, ALU.is_ge, ALU.mult),
                    reads=["scr0"], writes=["scr1"])
                em.op(DVE, lambda e: e.tensor_tensor(aa, aa, m1, ALU.subtract),
                      reads=["scr0", "scr1"], writes=["scr0"])
            em.op(DVE, lambda e: e.tensor_tensor(wt, wt, aa, ALU.subtract),
                  reads=["scr2", "scr0"], writes=["scr2"])
            em.op(DVE, lambda e: e.tensor_tensor(wt, wt, w0t, ALU.mult),
                  reads=["scr2", "scr4"], writes=["scr2"])
            em.op(DVE, lambda e: e.tensor_tensor(s_(dst_i), wt, sgn, ALU.mult),
                  reads=["scr2", f"scr{dst_i}"], writes=[f"scr{dst_i}"])

        for r in range(8):
            dma(s_(2), w_ffn[r * P:(r + 1) * P, :], reads=["scr2"], writes=["scr2"])
            quantize(2, 3)
            for kb in range(8):
                em.op(PE, lambda e, kb=kb: e.transpose(
                    bank(5, P), s_(3)[:, kb * P:(kb + 1) * P], ident[:]),
                    reads=["scr3", "ident"], writes=["pb5"])
                em.op(DVE, lambda e, kb=kb, r=r: e.tensor_copy(
                    wdT[:, kb * D + r * P:kb * D + (r + 1) * P], bank(5, P)),
                    reads=["pb5"], writes=["wdT"])

        if dumps:
            for kb in range(8):
                em.op(DVE, lambda e, kb=kb: e.tensor_copy(
                    s_(0), wdT[:, kb * D:(kb + 1) * D]),
                    reads=["wdT"], writes=["scr0"])
                dma(wdt_sp[:, kb * D:(kb + 1) * D], s_(0), reads=["scr0"])

        def layernorm(src_tag, src_ap, dst_ap, dst_tag, eps_col):
            sm, ssq, mcol, rstd = (lns[:, j:j + 1] for j in range(4))
            em.op(DVE, lambda e: e.reduce_sum(sm, src_ap, AX.X),
                  reads=[src_tag], writes=["lns"])
            em.op(ACT, lambda e: e.activation(
                s_(4), src_ap, AF.Square),
                reads=[src_tag], writes=["scr4"])
            em.op(DVE, lambda e: e.reduce_sum(ssq, s_(4), AX.X),
                  reads=["scr4"], writes=["lns"])
            em.op(DVE, lambda e: e.tensor_scalar(
                mcol, sm, 1.0 / D, None, ALU.mult), reads=["lns"], writes=["lns"])
            em.op(DVE, lambda e: e.scalar_tensor_tensor(
                sm, mcol, 1.0, mcol, ALU.mult, ALU.mult),
                reads=["lns"], writes=["lns"])
            em.op(DVE, lambda e: e.tensor_scalar(
                ssq, ssq, 1.0 / D, None, ALU.mult), reads=["lns"], writes=["lns"])
            em.op(DVE, lambda e: e.tensor_tensor(ssq, ssq, sm, ALU.subtract),
                  reads=["lns"], writes=["lns"])
            em.op(ACT, lambda e: e.activation(rstd, ssq, AF.Ln, bias=eps_col),
                  reads=["lns", "kcol"], writes=["lns"])
            em.op(ACT, lambda e: e.activation(rstd, rstd, AF.Exp, scale=-0.5),
                  reads=["lns"], writes=["lns"])
            em.op(DVE, lambda e: e.tensor_scalar(
                dst_ap, src_ap, mcol, rstd, ALU.subtract, ALU.mult),
                reads=[src_tag, "lns"], writes=[dst_tag])

        def transpose_in(src_i, dst, dst_tag, i, width=CH):
            for k in range(NKD):
                slot = k % 4
                em.op(PE, lambda e, k=k, slot=slot: e.transpose(
                    bank(5)[:, slot * P:(slot + 1) * P],
                    s_(src_i)[:, k * P:(k + 1) * P], ident[:]),
                    reads=[f"scr{src_i}", "ident"], writes=["pb5"])
                em.op(DVE, lambda e, k=k, slot=slot: e.tensor_copy(
                    dst[:, k * width + i * P:k * width + (i + 1) * P],
                    bank(5)[:, slot * P:(slot + 1) * P]),
                    reads=["pb5"], writes=[dst_tag])

        def ffn_block(ci, store_out=None, wi=0):
            for i in range(4):
                xw_ap = xch[:, (ci * 4 + i) * D:(ci * 4 + i + 1) * D]
                xwt = f"xch{ci}_{i}"
                layernorm(xwt, xw_ap, s_(2), "scr2", eps1)
                if dumps and store_out is None:
                    T0d = (wi * CPW + ci) * CH
                    dma(xn1_sp[T0d + i * P:T0d + (i + 1) * P, :], s_(2),
                        reads=["scr2"])
                quantize(2, 3)
                if dumps and store_out is None:
                    dma(xd_sp[T0d + i * P:T0d + (i + 1) * P, :], s_(3),
                        reads=["scr3"])
                transpose_in(3, tbuf, "tbuf", i)
            for i in range(4):
                xw_ap = xch[:, (ci * 4 + i) * D:(ci * 4 + i + 1) * D]
                xwt = f"xch{ci}_{i}"
                for k in range(NKD):
                    for h2 in range(2):
                        em.op(PE, lambda e, k=k, i=i, h2=h2: e.matmul(
                            bank(3 + h2),
                            tbuf[:, k * CH + i * P:k * CH + (i + 1) * P],
                            wdT[:, k * D + h2 * 512:k * D + (h2 + 1) * 512],
                            start=(k == 0), stop=(k == NKD - 1),
                            skip_group_check=True),
                            reads=["tbuf", "wdT"], writes=[f"pb{3 + h2}"])
                em.op(DVE, lambda e, xw_ap=xw_ap: e.scalar_tensor_tensor(
                    xw_ap, ps[:, 3 * 512:5 * 512], 0.5, xw_ap, ALU.mult, ALU.add),
                    reads=pb(3, 4) + [xwt], writes=[xwt])
                em.op(DVE, lambda e, xw_ap=xw_ap: e.scalar_tensor_tensor(
                    xw_ap, bfb[:], 0.5, xw_ap, ALU.mult, ALU.add),
                    reads=["bfb", xwt], writes=[xwt])
                if store_out is not None:
                    T0 = (wi * CPW + ci) * CH
                    dma(out_d[T0 + i * P:T0 + (i + 1) * P, :], xw_ap, reads=[xwt])

        for wi in range(NWIN):
            for ci in range(CPW):
                c = wi * CPW + ci
                T0 = c * CH

                for i in range(4):
                    xw_ap = xch[:, (ci * 4 + i) * D:(ci * 4 + i + 1) * D]
                    dma(xw_ap, x_in[T0 + i * P:T0 + (i + 1) * P, :],
                        reads=[f"xch{ci}_{i}"], writes=[f"xch{ci}_{i}"])

                ffn_block(ci)
                if dumps and c == 0:
                    for kk in range(8):
                        em.op(DVE, lambda e, kk=kk: e.tensor_copy(
                            s_(0, CH), tbuf[:, kk * CH:(kk + 1) * CH]),
                            reads=["tbuf"], writes=["scr0"])
                        dma(tb_sp[:, kk * CH:(kk + 1) * CH], s_(0, CH),
                            reads=["scr0"])
                if dumps:
                    for i in range(4):
                        dma(x1_sp[T0 + i * P:T0 + (i + 1) * P, :],
                            xch[:, (ci * 4 + i) * D:(ci * 4 + i + 1) * D],
                            reads=[f"xch{ci}_{i}"])

                for i in range(4):
                    xw_ap = xch[:, (ci * 4 + i) * D:(ci * 4 + i + 1) * D]
                    xwt = f"xch{ci}_{i}"
                    layernorm(xwt, xw_ap, s_(2), "scr2", eps1)
                    ssq2 = lns[:, 4:5]
                    rstd2 = lns[:, 5:6]
                    em.op(ACT, lambda e: e.activation(
                        s_(4), s_(2), AF.Square),
                        reads=["scr2"], writes=["scr4"])
                    em.op(DVE, lambda e, ssq2=ssq2: e.reduce_sum(
                        ssq2, s_(4), AX.X),
                        reads=["scr4"], writes=["lns"])
                    em.op(DVE, lambda e, ssq2=ssq2: e.tensor_scalar(
                        ssq2, ssq2, 1.0 / D, None, ALU.mult),
                        reads=["lns"], writes=["lns"])
                    em.op(ACT, lambda e, ssq2=ssq2, rstd2=rstd2: e.activation(
                        rstd2, ssq2, AF.Ln, bias=eps2),
                        reads=["lns", "kcol"], writes=["lns"])
                    em.op(ACT, lambda e, rstd2=rstd2: e.activation(
                        rstd2, rstd2, AF.Exp, scale=-0.5),
                        reads=["lns"], writes=["lns"])
                    em.op(DVE, lambda e, rstd2=rstd2: e.tensor_scalar(
                        s_(2), s_(2), rstd2, None, ALU.mult),
                        reads=["scr2", "lns"], writes=["scr2"])
                    em.op(DVE, lambda e: e.tensor_tensor(
                        s_(2), s_(2), nwb[:], ALU.mult),
                        reads=["scr2", "nwb"], writes=["scr2"])
                    transpose_in(2, tbuf, "tbuf", i)

                for m in range(32):
                    sl = m % 2
                    wsl = wslab[:, sl * 8 * P:(sl + 1) * 8 * P]
                    dma(wsl, win_bf[m, :, :], reads=[f"wsl{sl}", "win_bf"],
                        writes=[f"wsl{sl}"])
                    for k in range(NKD):
                        em.op(PE, lambda e, k=k, wsl=wsl, sl=sl: e.matmul(
                            bank(sl), wsl[:, k * P:(k + 1) * P],
                            tbuf[:, k * CH:(k + 1) * CH],
                            start=(k == 0), stop=(k == NKD - 1),
                            skip_group_check=True),
                            reads=[f"wsl{sl}", "tbuf"], writes=[f"pb{sl}"])
                    if m >= ND2:
                        em.op(ACT, lambda e, sl=sl: e.activation(
                            ussl[:], bank(sl), AF.Silu),
                            reads=[f"pb{sl}"], writes=["ussl"])
                        dma(sz_sp[m - ND2, :, T0:T0 + CH], ussl[:],
                            reads=["ussl"], writes=[f"szsp{m - ND2}"])
                        continue
                    em.op(DVE, lambda e, m=m: e.tensor_copy(
                        uext[:, 0:3], tails[:, m * 4:m * 4 + 3]),
                        reads=["tails"], writes=["uext"])
                    em.op(DVE, lambda e, sl=sl: e.tensor_copy(
                        uext[:, 3:3 + CH], bank(sl)),
                        reads=[f"pb{sl}"], writes=["uext"])
                    em.op(DVE, lambda e, m=m: e.tensor_copy(
                        tails[:, m * 4:m * 4 + 3], uext[:, CH:CH + 3]),
                        reads=["uext"], writes=["tails"])
                    em.op(DVE, lambda e, m=m: e.tensor_scalar(
                        cvt[:], uext[:, 0:CH], convw[:, m * 4:m * 4 + 1], None,
                        ALU.mult), reads=["uext", "convw"], writes=["cvt"])
                    for j in range(1, 4):
                        em.op(DVE, lambda e, m=m, j=j: e.scalar_tensor_tensor(
                            cvt[:], uext[:, j:j + CH],
                            convw[:, m * 4 + j:m * 4 + j + 1],
                            cvt[:], ALU.mult, ALU.add),
                            reads=["uext", "convw", "cvt"], writes=["cvt"])
                    em.op(ACT, lambda e, m=m: e.activation(
                        ussl[:], cvt[:], AF.Silu, bias=convb[:, m:m + 1]),
                        reads=["cvt", "convb"], writes=["ussl"])
                    dma(us_sp[m, :, T0:T0 + CH], ussl[:], reads=["ussl"],
                        writes=[f"ussp{m}"])
                    em.op(DVE, lambda e: e.tensor_copy(usbf[:], ussl[:]),
                          reads=["ussl"], writes=["usbf"])
                    em.op(PE, lambda e, m=m: e.matmul(
                        bank(2), xpT[:, m * P:(m + 1) * P], usbf[:],
                        start=(m == 0), stop=(m == ND2 - 1),
                        skip_group_check=True),
                        reads=["usbf", "xpT"], writes=["pb2"])

                em.op(DVE, lambda e: e.tensor_copy(spl[:, 0:CH], bank(2)),
                      reads=["pb2", "spl"], writes=["spl"])
                dma(b_sp[:, T0:T0 + CH], spl[0:NST, 0:CH], reads=["spl"],
                    writes=[f"bsp{c}"])
                dma(c_sp[:, T0:T0 + CH], spl[NST:2 * NST, 0:CH],
                    reads=["spl"], writes=[f"csp{c}"])
                dma(ussl[0:RK, :], spl[RK:P, 0:CH], reads=["spl", "ussl"],
                    writes=["ussl"])
                em.op(DVE, lambda e: e.tensor_copy(dtr_bf[:], ussl[0:RK, :]),
                      reads=["ussl"], writes=["dtrb"])

                for m in range(ND2):
                    sl = m % 2
                    em.op(PE, lambda e, m=m, sl=sl: e.matmul(
                        bank(sl), dtpT[:, m * P:(m + 1) * P], dtr_bf[:],
                        start=True, stop=True, skip_group_check=True),
                        reads=["dtrb", "dtpT"], writes=[f"pb{sl}"])
                    aa = spl[:, 0:CH]
                    ll = spl[:, CH:2 * CH]
                    em.op(ACT, lambda e, m=m, sl=sl: e.activation(
                        aa, bank(sl), AF.Abs, bias=dtb[:, m:m + 1]),
                        reads=[f"pb{sl}", "dtb"], writes=["spl"])
                    em.op(ACT, lambda e, aa=aa, ll=ll: e.activation(
                        ll, aa, AF.Exp, scale=-1.0), reads=["spl"], writes=["spl"])
                    em.op(ACT, lambda e, ll=ll: e.activation(
                        ll, ll, AF.Ln, bias=onec),
                        reads=["spl", "kcol"], writes=["spl"])
                    em.op(DVE, lambda e, m=m, sl=sl, aa=aa: e.tensor_scalar(
                        aa, bank(sl), dtb[:, m:m + 1], 0.0, ALU.add, ALU.max),
                        reads=[f"pb{sl}", "dtb"], writes=["spl"])
                    em.op(DVE, lambda e, aa=aa, ll=ll: e.tensor_tensor(
                        aa, aa, ll, ALU.add), reads=["spl"], writes=["spl"])
                    dma(dt_sp[m, :, T0:T0 + CH], aa, reads=["spl"], writes=[f"dtsp{m}"])
                    dma(ussl[:], us_sp[m, :, T0:T0 + CH],
                        reads=["ussl", f"ussp{m}"], writes=["ussl"])
                    em.op(DVE, lambda e, aa=aa: e.tensor_tensor(
                        cvt[:], aa, ussl[:], ALU.mult),
                        reads=["spl", "ussl"], writes=["cvt"])
                    dma(dtu_sp[m, :, T0:T0 + CH], cvt[:], reads=["cvt"],
                        writes=[f"dtusp{m}"])

            import os as _os
            _ks = _os.environ.get("KSERIAL", "0")
            if _ks == "1":
                em.serial = True
            elif _ks == "0":
                em.serial = False
            elif _ks.startswith("except:"):
                em.serial = _ks
            else:
                em.serial = _ks.split(",")
            W0 = wi * WIN
            blocks = [(m0, 2) for m0 in range(0, 16, 2)]
            for bi, (m0, bsz) in enumerate(blocks):
                yb = (bi % 2) * 4
                for j in range(bsz):
                    m = m0 + j
                    dma(dtsl[:, j * WIN:(j + 1) * WIN], dt_sp[m, :, W0:W0 + WIN],
                        reads=[f"dtsl{j}", f"dtsp{m}"], writes=[f"dtsl{j}"],
                        ser_tag="sblk")
                    dma(dtusl[:, j * WIN:(j + 1) * WIN],
                        dtu_sp[m, :, W0:W0 + WIN],
                        reads=[f"dtusl{j}", f"dtusp{m}"],
                        writes=[f"dtusl{j}"], ser_tag="sblk")
                fences = []
                for n in range(NST):
                    fences.append({e: em.n[e] - 1 for e in ENGS if em.n[e]})
                    _fd = int(_os.environ.get("KFENCE", "0"))
                    if _fd > 0 and len(fences) >= _fd:
                        em.barrier = fences[-_fd]
                    em.op(SP, lambda e, n=n: e.dma_start(
                        out=bbt[:], in_=b_sp[n:n + 1, W0:W0 + WIN]
                        .broadcast_to([P, WIN])),
                        reads=["bbt", f"bsp{2 * wi}", f"bsp{2 * wi + 1}"],
                        writes=["bbt"], ser_tag="srow")
                    em.op(SP, lambda e, n=n: e.dma_start(
                        out=cbt[:], in_=c_sp[n:n + 1, W0:W0 + WIN]
                        .broadcast_to([P, WIN])),
                        reads=["cbt", f"csp{2 * wi}", f"csp{2 * wi + 1}"],
                        writes=["cbt"], ser_tag="srow")
                    for j in range(bsz):
                        m = m0 + j
                        ii = (n * bsz + j) % 2
                        da = dab[:, ii * WIN:(ii + 1) * WIN]
                        xx = xbb[:, ii * WIN:(ii + 1) * WIN]
                        hh = hbb[:, ii * WIN:(ii + 1) * WIN]
                        hc = hcb[:, j * WIN:(j + 1) * WIN]
                        scol = m * NST + n
                        em.op(ACT, lambda e, j=j, m=m, n=n, da=da: e.activation(
                            da, dtsl[:, j * WIN:(j + 1) * WIN], AF.Exp,
                            scale=a_t[:, m * NST + n:m * NST + n + 1]),
                            reads=[f"dtsl{j}", "a_t"], writes=[f"da{ii}"],
                            ser_tag="sda")
                        xeng = DVE if j == 0 else POOL
                        em.op(xeng, lambda e, j=j, xx=xx: e.tensor_tensor(
                            xx, dtusl[:, j * WIN:(j + 1) * WIN], bbt[:], ALU.mult),
                            reads=[f"dtusl{j}", "bbt"], writes=[f"xb{ii}"],
                            ser_tag="sxb")
                        em.op(DVE, lambda e, da=da, xx=xx, hh=hh, scol=scol:
                              e.tensor_tensor_scan(
                                  hh, da, xx, hstate[:, scol:scol + 1],
                                  ALU.mult, ALU.add),
                              reads=[f"da{ii}", f"xb{ii}", "hstate"],
                              writes=[f"hb{ii}"], ser_tag="sscan")
                        em.op(DVE, lambda e, hh=hh, scol=scol: e.tensor_copy(
                            hstate[:, scol:scol + 1], hh[:, WIN - 1:WIN]),
                            reads=[f"hb{ii}"], writes=["hstate"],
                            ser_tag="shend")
                        hc_eng = DVE if j == 0 else POOL
                        em.op(hc_eng, lambda e, hh=hh, hc=hc: e.tensor_tensor(
                            hc, hh, cbt[:], ALU.mult),
                            reads=[f"hb{ii}", "cbt"], writes=[f"hc{j}"],
                            ser_tag="shc")
                        for h2 in range(2):
                            em.op(PE, lambda e, j=j, hc=hc, h2=h2, n=n, yb=yb:
                                  e.matmul(
                                      ps[:, (yb + 2 * j + h2) * 512:
                                         (yb + 2 * j + h2 + 1) * 512],
                                      identb[:], hc[:, h2 * 512:(h2 + 1) * 512],
                                      start=(n == 0), stop=(n == NST - 1),
                                      skip_group_check=True),
                                  reads=[f"hc{j}", "identb"],
                                  writes=[f"pb{yb + 2 * j + h2}"], ser_tag="symm")
                em.barrier = None
                for j in range(bsz):
                    m = m0 + j
                    uD = posts[:, 0:WIN]
                    sz = posts[:, WIN:2 * WIN]
                    dma(uD, us_sp[m, :, W0:W0 + WIN],
                        reads=["posts", f"ussp{m}"], writes=["posts"],
                        ser_tag="spost")
                    dma(sz, sz_sp[m, :, W0:W0 + WIN],
                        reads=["posts2", f"szsp{m}"], writes=["posts2"],
                        ser_tag="spost")
                    em.op(DVE, lambda e, uD=uD, m=m: e.tensor_scalar(
                        uD, uD, dpar[:, m:m + 1], None, ALU.mult),
                        reads=["posts", "dpar"], writes=["posts"],
                        ser_tag="spost")
                    em.op(DVE, lambda e, j=j, uD=uD, yb=yb: e.tensor_tensor(
                        uD, ps[:, (yb + 2 * j) * 512:(yb + 2 * j + 2) * 512],
                        uD, ALU.add),
                        reads=pb(yb + 2 * j, yb + 2 * j + 1) + ["posts"],
                        writes=["posts"], ser_tag="spost")
                    em.op(DVE, lambda e, uD=uD, sz=sz: e.tensor_tensor(
                        y3b[:], uD, sz, ALU.mult),
                        reads=["posts", "posts2"], writes=["y3b"],
                        ser_tag="spost")
                    dma(y3_sp[m, :, W0:W0 + WIN], y3b[:], reads=["y3b"],
                        writes=[f"y3sp{m}"], ser_tag="spost")

            em.serial = False
            if _os.environ.get("KSTAGEB", "0") == "1":
                em.barrier = {e: em.n[e] - 1 for e in ENGS if em.n[e]}
            for ti in range(8):
                ci, i = divmod(ti, 4)
                tsl = 0
                yti = y3ti[:, :]
                dma(yti.rearrange("p (m c) -> p m c", c=P),
                    y3_sp[:, :, W0 + ti * P:W0 + (ti + 1) * P].rearrange(
                        "m p c -> p m c"),
                    reads=[f"y3ti{tsl}"] + [f"y3sp{mm}" for mm in range(ND2)],
                    writes=[f"y3ti{tsl}"])
                for k in range(16):
                    sl = k % 2
                    wsl_ap = woslab[:, sl * D:(sl + 1) * D]
                    dma(wsl_ap, wout_bf[k, :, :], reads=[f"wo{sl}", "wout_bf"],
                        writes=[f"wo{sl}"])
                    for h2 in range(2):
                        em.op(PE, lambda e, k=k, h2=h2, yti=yti, wsl_ap=wsl_ap,
                              tsl=tsl: e.matmul(
                                  bank(3 + h2),
                                  yti[:, k * P:(k + 1) * P],
                                  wsl_ap[:, h2 * 512:(h2 + 1) * 512],
                                  start=(k == 0), stop=(k == 15),
                                  skip_group_check=True),
                              reads=[f"y3ti{tsl}", f"wo{sl}"],
                              writes=[f"pb{3 + h2}"])
                xw_ap = xch[:, ti * D:(ti + 1) * D]
                xwt = f"xch{ci}_{i}"
                layernorm(xwt, xw_ap, s_(2), "scr2", eps1)
                em.op(DVE, lambda e, xw_ap=xw_ap: e.tensor_tensor(
                    xw_ap, s_(2), xw_ap, ALU.add),
                    reads=["scr2", xwt], writes=[xwt])
                em.op(DVE, lambda e, xw_ap=xw_ap: e.tensor_tensor(
                    xw_ap, ps[:, 3 * 512:5 * 512], xw_ap, ALU.add),
                    reads=pb(3, 4) + [xwt], writes=[xwt])

            for ci in range(CPW):
                ffn_block(ci, store_out=True, wi=wi)


        em.finalize()
        streams = {"sp": [], "pe": [], "act": [], "dve": [], "pool": []}
        for eng, fn, waits, seq in em.ops:
            stream = "pool" if eng in ("pool", "pq") else eng
            streams[stream].append((eng, fn, waits, seq))

        def play(stream_name, handle):
            waited = {}
            for eng, fn, waits, seq in streams[stream_name]:
                for we, ws in waits.items():
                    val = em.rank[we][ws]
                    if we in ("sp", "pq"):
                        val *= 16
                    if val > 0 and waited.get(we, -1) < val:
                        handle.wait_ge(sems[we], val)
                        waited[we] = val
                inst = fn(handle)
                if seq in em.needed[eng]:
                    inst.then_inc(sems[eng], 16 if eng in ("sp", "pq") else 1)
            if stream_name == "sp":
                handle.wait_ge(sems["sp"], 16 * em.ninc["sp"])
                if em.ninc["pq"]:
                    handle.wait_ge(sems["pq"], 16 * em.ninc["pq"])

        @block.sync
        def _(sync):
            play("sp", sync)

        @block.tensor
        def _(t):
            play("pe", t)

        @block.scalar
        def _(s):
            play("act", s)

        @block.vector
        def _(v):
            play("dve", v)

        @block.gpsimd
        def _(g):
            play("pool", g)

    return nc


_NC_CACHE = {}


def kernel(L0_attn_x, L0_attn_norm_w, L0_attn_in_proj_w, L0_attn_conv_w,
           L0_attn_conv_b, L0_attn_x_proj_w, L0_attn_dt_proj_w, L0_attn_dt_proj_b,
           L0_attn_A_log, L0_attn_D_param, L0_attn_out_proj_w, L0_ffn_weight,
           L0_ffn_bias):
    B = L0_attn_x.shape[0]
    f32 = np.float32

    def lay(a):
        return np.ascontiguousarray(np.asarray(a), dtype=f32)

    w_in_g = lay(np.asarray(L0_attn_in_proj_w).T
                 .reshape(8, 128, 32, 128).transpose(2, 0, 1, 3))
    w_out_g = lay(np.asarray(L0_attn_out_proj_w).T.reshape(16, 128, 1024))
    perm = list(range(64, 96)) + list(range(96, 128)) + list(range(0, 64))
    w_xp_g = lay(np.asarray(L0_attn_x_proj_w)[perm].T.reshape(16, 128, 128))
    w_dtp = lay(np.asarray(L0_attn_dt_proj_w).T)
    w_ffn = lay(L0_ffn_weight)
    b_ffn = lay(np.asarray(L0_ffn_bias).reshape(1, 1024))
    nrmw = lay(np.asarray(L0_attn_norm_w).reshape(1, 1024))
    conv_w_l = lay(np.asarray(L0_attn_conv_w).reshape(16, 128, 4)
                   .transpose(1, 0, 2).reshape(128, 64))
    conv_b_l = lay(np.asarray(L0_attn_conv_b).reshape(16, 128).T)
    dtb_l = lay(np.asarray(L0_attn_dt_proj_b).reshape(16, 128).T)
    d_l = lay(np.asarray(L0_attn_D_param).reshape(16, 128).T)
    alog_l = lay(np.asarray(L0_attn_A_log).reshape(16, 128, 32)
                 .transpose(1, 0, 2).reshape(128, 512))

    if "nc" not in _NC_CACHE:
        _NC_CACHE["nc"] = build_core_program()
    nc = _NC_CACHE["nc"]

    shared = {
        "w_in_g": w_in_g, "w_out_g": w_out_g, "w_xp_g": w_xp_g, "w_dtp": w_dtp,
        "w_ffn": w_ffn, "b_ffn": b_ffn, "nrmw": nrmw, "conv_w_l": conv_w_l,
        "conv_b_l": conv_b_l, "dtb_l": dtb_l, "d_l": d_l, "alog_l": alog_l,
    }
    in_maps = []
    for b in range(B):
        m = dict(shared)
        m["x"] = lay(L0_attn_x[b])
        in_maps.append(m)

    res = run_bass_kernel_spmd(nc, in_maps, core_ids=list(range(B)))
    out = np.stack([res.results[b]["out"] for b in range(B)], axis=0)
    return out.astype(np.float32)


# version 15
# speedup vs baseline: 2.0531x; 1.0470x over previous
import numpy as np
import concourse.bass as bass
import concourse.mybir as mybir
from concourse.bass_utils import run_bass_kernel_spmd

F32 = mybir.dt.float32
BF16 = mybir.dt.bfloat16
AF = mybir.ActivationFunctionType
ALU = mybir.AluOpType
AX = mybir.AxisListType

P = 128
T = 4096
D = 1024
D2 = 2048
NST = 32
RK = 64
CH = 512
NCH = T // CH
WIN = 1024
NWIN = T // WIN
CPW = WIN // CH
ND2 = D2 // P
NKD = D // P
NQB = D // NST

ENGS = ("sp", "pq", "pe", "act", "dve", "pool")


class Em:
    def __init__(self):
        self.ops = []
        self.n = {e: 0 for e in ENGS}
        self.writer = {}
        self.readers = {}
        self.serial = False
        self.barrier = None

    def op(self, eng, fn, reads=(), writes=(), ser_tag=None):
        waits = {}
        if self.barrier:
            for e2, s2 in self.barrier.items():
                if e2 != eng and waits.get(e2, -1) < s2:
                    waits[e2] = s2
                if e2 == eng and eng != "pe" and waits.get(e2, -1) < s2:
                    waits[e2] = s2
        ser = False
        if self.serial:
            if self.serial is True:
                ser = True
            elif isinstance(self.serial, str) and self.serial.startswith("except:"):
                ser = (ser_tag or "") not in self.serial[7:].split(",")
            elif eng in self.serial or (ser_tag and ser_tag in self.serial):
                ser = True
        if ser:
            for e2 in ENGS:
                ns = self.n[e2] - (1 if e2 == eng else 0)
                if ns > 0:
                    waits[e2] = ns - 1

        def need(dep):
            if dep is None:
                return
            e, s = dep
            if e == eng:
                if e == "pe":
                    return
            if e == "pool" and eng == "pq":
                return
            if waits.get(e, -1) < s:
                waits[e] = s

        for r in reads:
            need(self.writer.get(r))
        for w in writes:
            need(self.writer.get(w))
            for rd in self.readers.get(w, ()):
                need(rd)
        seq = self.n[eng]
        if eng in ("sp", "pq") and seq > 0 and waits.get(eng, -1) < seq - 1:
            waits[eng] = seq - 1
        self.ops.append((eng, fn, waits, seq))
        self.n[eng] += 1
        me = (eng, seq)
        for w in writes:
            self.writer[w] = me
            self.readers[w] = []
        for r in reads:
            self.readers.setdefault(r, []).append(me)

    def finalize(self):
        needed = {e: set() for e in ENGS}
        for eng, fn, waits, seq in self.ops:
            for e, s in waits.items():
                needed[e].add(s)
        needed["sp"] = set(range(self.n["sp"]))
        needed["pq"] = set(range(self.n["pq"]))
        self.rank = {}
        self.ninc = {}
        for e in ENGS:
            cnt = 0
            rm = {}
            for s in range(self.n[e]):
                if s in needed[e]:
                    cnt += 1
                rm[s] = cnt
            self.rank[e] = rm
            self.ninc[e] = cnt
        self.needed = needed


def build_core_program(debug=False):
    nc = bass.Bass("TRN2", target_bir_lowering=False)
    spk = "ExternalOutput" if debug else "Internal"
    dumps = (debug is True)

    def din(name, shape, dt=F32):
        return nc.dram_tensor(name, list(shape), dt, kind="ExternalInput").ap()

    x_in = din("x", (T, D))
    w_in_g = din("w_in_g", (32, 8, P, P))
    w_out_g = din("w_out_g", (16, P, D))
    w_xp_g = din("w_xp_g", (16, P, P))
    w_dtp = din("w_dtp", (RK, D2))
    w_ffn = din("w_ffn", (D, D))
    b_ffn = din("b_ffn", (1, D))
    nrmw = din("nrmw", (1, D))
    conv_w_l = din("conv_w_l", (P, ND2 * 4))
    conv_b_l = din("conv_b_l", (P, ND2))
    dtb_l = din("dtb_l", (P, ND2))
    d_l = din("d_l", (P, ND2))
    alog_l = din("alog_l", (P, ND2 * NST))

    out_d = nc.dram_tensor("out", [T, D], F32, kind="ExternalOutput").ap()
    x1_sp = nc.dram_tensor("x1_sp", [T, D], F32, kind=spk).ap() if dumps else None
    xn1_sp = nc.dram_tensor("xn1_sp", [T, D], F32, kind=spk).ap() if dumps else None
    xd_sp = nc.dram_tensor("xd_sp", [T, D], F32, kind=spk).ap() if dumps else None
    wdt_sp = nc.dram_tensor("wdt_sp", [P, 8 * D], F32, kind=spk).ap() if dumps else None
    tb_sp = nc.dram_tensor("tb_sp", [P, 8 * CH], F32, kind=spk).ap() if dumps else None

    win_bf = nc.dram_tensor("win_bf", [32, P, 8 * P], BF16, kind="Internal").ap()
    wout_bf = nc.dram_tensor("wout_bf", [16, P, D], BF16, kind="Internal").ap()
    dt_sp = nc.dram_tensor("dt_sp", [ND2, P, T], F32, kind=spk).ap()
    dtu_sp = nc.dram_tensor("dtu_sp", [ND2, P, T], F32, kind=spk).ap()
    us_sp = nc.dram_tensor("us_sp", [ND2, P, T], F32, kind=spk).ap()
    sz_sp = nc.dram_tensor("sz_sp", [ND2, P, T], F32, kind=spk).ap()
    y3_sp = nc.dram_tensor("y3_sp", [ND2, P, T], BF16, kind=spk).ap()
    b_sp = nc.dram_tensor("b_sp", [NST, T], F32, kind=spk).ap()
    c_sp = nc.dram_tensor("c_sp", [NST, T], BF16, kind=spk).ap()

    em = Em()
    SP, PQ, PE, ACT, DVE, POOL = ENGS

    import contextlib
    with contextlib.ExitStack() as _st:
        ident = _st.enter_context(nc.sbuf_tensor("ident", [P, P], F32))
        identb = _st.enter_context(nc.sbuf_tensor("identb", [P, P], BF16))
        ones1 = _st.enter_context(nc.sbuf_tensor("ones1", [1, P], F32))
        wdT = _st.enter_context(nc.sbuf_tensor("wdT", [P, 8 * D], BF16))
        xpT = _st.enter_context(nc.sbuf_tensor("xpT", [P, 16 * P], BF16))
        dtpT = _st.enter_context(nc.sbuf_tensor("dtpT", [RK, D2], BF16))
        a_t = _st.enter_context(nc.sbuf_tensor("a_t", [P, ND2 * NST], F32))
        convw = _st.enter_context(nc.sbuf_tensor("convw", [P, ND2 * 4], F32))
        convb = _st.enter_context(nc.sbuf_tensor("convb", [P, ND2], F32))
        dtb = _st.enter_context(nc.sbuf_tensor("dtb", [P, ND2], F32))
        dpar = _st.enter_context(nc.sbuf_tensor("dpar", [P, ND2], F32))
        nwb = _st.enter_context(nc.sbuf_tensor("nwb", [P, D], F32))
        bfb = _st.enter_context(nc.sbuf_tensor("bfb", [P, D], F32))
        kcol = _st.enter_context(nc.sbuf_tensor("kcol", [P, 4], F32))
        lns = _st.enter_context(nc.sbuf_tensor("lns", [P, 8], F32))
        qs = _st.enter_context(nc.sbuf_tensor("qs", [P, 3 * NQB], F32))
        hstate = _st.enter_context(nc.sbuf_tensor("hstate", [P, ND2 * NST], F32))
        tails = _st.enter_context(nc.sbuf_tensor("tails", [P, ND2 * 4], F32))
        wslab = _st.enter_context(nc.sbuf_tensor("wslab", [P, 2 * 8 * P], BF16))
        woslab = _st.enter_context(nc.sbuf_tensor("woslab", [P, 2 * D], BF16))
        xch = _st.enter_context(nc.sbuf_tensor("xch", [P, CPW * 4 * D], F32))
        scr = _st.enter_context(nc.sbuf_tensor("scr", [P, 5 * D], F32))
        tbuf = _st.enter_context(nc.sbuf_tensor("tbuf", [P, 8 * CH], BF16))
        uext = _st.enter_context(nc.sbuf_tensor("uext", [P, CH + 3], F32))
        cvt = _st.enter_context(nc.sbuf_tensor("cvt", [P, CH], F32))
        usbf = _st.enter_context(nc.sbuf_tensor("usbf", [P, CH], BF16))
        ussl = _st.enter_context(nc.sbuf_tensor("ussl", [P, CH], F32))
        dtr_bf = _st.enter_context(nc.sbuf_tensor("dtr_bf", [RK, CH], BF16))
        spl = _st.enter_context(nc.sbuf_tensor("spl", [P, 2 * CH], F32))
        cbf = _st.enter_context(nc.sbuf_tensor("cbf", [P, CH], BF16))
        dtsl = _st.enter_context(nc.sbuf_tensor("dtsl", [P, 2 * WIN], F32))
        dtusl = _st.enter_context(nc.sbuf_tensor("dtusl", [P, 2 * WIN], F32))
        dab = _st.enter_context(nc.sbuf_tensor("dab", [P, 2 * WIN], F32))
        xbb = _st.enter_context(nc.sbuf_tensor("xbb", [P, 2 * WIN], F32))
        hbb = _st.enter_context(nc.sbuf_tensor("hbb", [P, 2 * WIN], BF16))
        hcb = _st.enter_context(nc.sbuf_tensor("hcb", [P, 2 * WIN], BF16))
        bbt = _st.enter_context(nc.sbuf_tensor("bbt", [P, WIN], F32))
        cbt = _st.enter_context(nc.sbuf_tensor("cbt", [P, WIN], BF16))
        y3b = _st.enter_context(nc.sbuf_tensor("y3b", [P, WIN], BF16))
        y3ti = _st.enter_context(nc.sbuf_tensor("y3ti", [P, ND2 * P], BF16))
        posts = _st.enter_context(nc.sbuf_tensor("posts", [P, 2 * WIN], F32))
        ps = _st.enter_context(nc.psum_tensor("ps", [P, 4096], F32))
        s_sp = _st.enter_context(nc.semaphore("s_sp"))
        s_pq = _st.enter_context(nc.semaphore("s_pq"))
        s_pe = _st.enter_context(nc.semaphore("s_pe"))
        s_act = _st.enter_context(nc.semaphore("s_act"))
        s_dve = _st.enter_context(nc.semaphore("s_dve"))
        s_pool = _st.enter_context(nc.semaphore("s_pool"))
        block = _st.enter_context(nc.Block())
        sems = {"sp": s_sp, "pq": s_pq, "pe": s_pe, "act": s_act,
                "dve": s_dve, "pool": s_pool}

        def bank(i, w=512):
            return ps[:, i * 512:i * 512 + w]

        def pb(*idx):
            return [f"pb{i}" for i in idx]

        def dma(dst_ap, src_ap, reads=(), writes=(), q=SP, ser_tag=None):
            em.op(q, lambda e, o=dst_ap, i=src_ap: e.dma_start(out=o, in_=i),
                  reads=reads, writes=writes, ser_tag=ser_tag)

        def s_(i, w=D):
            return scr[:, i * D:i * D + w]

        em.op(POOL, lambda e: e.memset(ident[:], 0.0), writes=["ident"])
        em.op(POOL, lambda e: e.affine_select(
            out=ident[:], in_=ident[:], compare_op=ALU.not_equal, fill=1.0,
            base=0, pattern=[[-1, P]], channel_multiplier=1),
            reads=["ident"], writes=["ident"])
        em.op(DVE, lambda e: e.tensor_copy(identb[:], ident[:]),
              reads=["ident"], writes=["identb"])
        em.op(DVE, lambda e: e.memset(ones1[:], 1.0), writes=["ones1"])
        em.op(DVE, lambda e: e.memset(hstate[:], 0.0), writes=["hstate"])
        em.op(DVE, lambda e: e.memset(tails[:], 0.0), writes=["tails"])
        em.op(DVE, lambda e: e.memset(kcol[:], 0.0), writes=["kcol"])
        for j, val in ((0, 1e-5), (1, 1e-6), (2, 1.0)):
            em.op(DVE, lambda e, j=j, val=val: e.tensor_scalar(
                kcol[:, j:j + 1], kcol[:, j:j + 1], val, None, ALU.add),
                reads=["kcol"], writes=["kcol"])
        eps1, eps2, onec = kcol[:, 0:1], kcol[:, 1:2], kcol[:, 2:3]

        dma(convw[:], conv_w_l[:, :], writes=["convw"])
        dma(convb[:], conv_b_l[:, :], writes=["convb"])
        dma(dtb[:], dtb_l[:, :], writes=["dtb"])
        dma(dpar[:], d_l[:, :], writes=["dpar"])
        dma(a_t[:], alog_l[:, :], writes=["a_t"])
        em.op(ACT, lambda e: e.activation(a_t[:], a_t[:], AF.Exp),
              reads=["a_t"], writes=["a_t"])
        em.op(DVE, lambda e: e.tensor_scalar(a_t[:], a_t[:], -1.0, None, ALU.mult),
              reads=["a_t"], writes=["a_t"])

        def row_bcast(dst, src_row, tag):
            dma(dst[0:1, :], src_row, writes=[tag])
            p = 1
            while p < P:
                q = min(p, P - p)
                dma(dst[p:p + q, :], dst[0:q, :], reads=[tag], writes=[tag])
                p += q
        row_bcast(nwb, nrmw[:, :], "nwb")
        row_bcast(bfb, b_ffn[:, :], "bfb")

        for k in range(16):
            dma(s_(0, P), w_xp_g[k, :, :], reads=["scr0"], writes=["scr0"])
            em.op(DVE, lambda e, k=k: e.tensor_copy(
                xpT[:, k * P:(k + 1) * P], s_(0, P)),
                reads=["scr0"], writes=["xpT"])
        for h in range(2):
            dma(spl[0:RK, 0:D2 // 2], w_dtp[:, h * D2 // 2:(h + 1) * D2 // 2],
                reads=["spl"], writes=["spl"])
            em.op(DVE, lambda e, h=h: e.tensor_copy(
                dtpT[:, h * D2 // 2:(h + 1) * D2 // 2], spl[0:RK, 0:D2 // 2]),
                reads=["spl"], writes=["dtpT"])

        for m in range(32):
            dma(s_(0).rearrange("p (k c) -> p k c", c=P),
                w_in_g[m].rearrange("k p c -> p k c"),
                reads=["scr0"], writes=["scr0"])
            em.op(DVE, lambda e: e.tensor_copy(tbuf[:, 0:D], s_(0)),
                  reads=["scr0"], writes=["tbuf"])
            dma(win_bf[m, :, :], tbuf[:, 0:D], reads=["tbuf"], writes=["win_bf"])
        for k in range(16):
            dma(s_(0), w_out_g[k, :, :], reads=["scr0"], writes=["scr0"])
            em.op(DVE, lambda e: e.tensor_copy(tbuf[:, 0:D], s_(0)),
                  reads=["scr0"], writes=["tbuf"])
            dma(wout_bf[k, :, :], tbuf[:, 0:D], reads=["tbuf"], writes=["wout_bf"])

        def quantize(src_i, dst_i):
            sv, sa, sm, sw0, swt = (x for x in (src_i, 0, 1, 4, 2))
            v = s_(sv)
            aa = s_(sa)
            m1 = s_(sm)
            sgn = s_(dst_i)
            w0t = s_(sw0)
            wt = s_(swt)
            amax = qs[:, 0:NQB]
            sc = qs[:, NQB:2 * NQB]
            sinv = qs[:, 2 * NQB:3 * NQB]
            rd_src = [f"scr{src_i}"]
            em.op(DVE, lambda e: e.tensor_reduce(
                amax, v.rearrange("p (b k) -> p b k", k=NST), AX.X, ALU.max,
                apply_absolute_value=True), reads=rd_src, writes=["qs"])
            em.op(DVE, lambda e: e.tensor_scalar(
                sc, amax, 1.0 / 7.5, 1e-30, ALU.mult, ALU.max),
                reads=["qs"], writes=["qs"])
            em.op(DVE, lambda e: e.reciprocal(sinv, sc), reads=["qs"], writes=["qs"])
            em.op(DVE, lambda e: e.tensor_tensor(
                v.rearrange("p (b k) -> p b k", k=NST),
                v.rearrange("p (b k) -> p b k", k=NST),
                sinv.unsqueeze(-1).broadcast_to([P, NQB, NST]), ALU.mult),
                reads=rd_src + ["qs"], writes=rd_src)
            em.op(ACT, lambda e: e.activation(sgn, v, AF.Sign),
                  reads=rd_src, writes=[f"scr{dst_i}"])
            em.op(ACT, lambda e: e.activation(aa, v, AF.Abs),
                  reads=rd_src, writes=["scr0"])
            em.op(DVE, lambda e: e.tensor_scalar(
                m1, aa, 1.9375, -4.0, ALU.is_gt, ALU.mult),
                reads=["scr0"], writes=["scr1"])
            em.op(DVE, lambda e: e.tensor_scalar(
                v, aa, 3.875, -2.0, ALU.is_gt, ALU.mult),
                reads=["scr0"], writes=rd_src)
            em.op(DVE, lambda e: e.scalar_tensor_tensor(
                w0t, m1, -0.25, v, ALU.mult, ALU.subtract),
                reads=["scr1"] + rd_src, writes=["scr4"])
            em.op(DVE, lambda e: e.tensor_scalar(
                w0t, w0t, 0.125, 0.125, ALU.mult, ALU.add),
                reads=["scr4"], writes=["scr4"])
            em.op(DVE, lambda e: e.tensor_tensor(
                w0t.rearrange("p (b k) -> p b k", k=NST),
                w0t.rearrange("p (b k) -> p b k", k=NST),
                sc.unsqueeze(-1).broadcast_to([P, NQB, NST]), ALU.mult),
                reads=["scr4", "qs"], writes=["scr4"])
            em.op(DVE, lambda e: e.tensor_tensor(m1, m1, v, ALU.add),
                  reads=["scr1"] + rd_src, writes=["scr1"])
            em.op(DVE, lambda e: e.tensor_scalar(
                m1, m1, 0.125, 1.0, ALU.mult, ALU.add),
                reads=["scr1"], writes=["scr1"])
            em.op(DVE, lambda e: e.tensor_tensor(wt, aa, m1, ALU.mult),
                  reads=["scr0", "scr1"], writes=["scr2"])
            em.op(DVE, lambda e: e.tensor_scalar(
                wt, wt, 8.0, 0.5, ALU.mult, ALU.add),
                reads=["scr2"], writes=["scr2"])
            em.op(DVE, lambda e: e.tensor_copy(aa, wt),
                  reads=["scr2"], writes=["scr0"])
            for K in (32.0, 16.0, 8.0, 4.0, 2.0, 1.0):
                em.op(DVE, lambda e, K=K: e.tensor_scalar(
                    m1, aa, K, K, ALU.is_ge, ALU.mult),
                    reads=["scr0"], writes=["scr1"])
                em.op(DVE, lambda e: e.tensor_tensor(aa, aa, m1, ALU.subtract),
                      reads=["scr0", "scr1"], writes=["scr0"])
            em.op(DVE, lambda e: e.tensor_tensor(wt, wt, aa, ALU.subtract),
                  reads=["scr2", "scr0"], writes=["scr2"])
            em.op(DVE, lambda e: e.tensor_tensor(wt, wt, w0t, ALU.mult),
                  reads=["scr2", "scr4"], writes=["scr2"])
            em.op(DVE, lambda e: e.tensor_tensor(s_(dst_i), wt, sgn, ALU.mult),
                  reads=["scr2", f"scr{dst_i}"], writes=[f"scr{dst_i}"])

        for r in range(8):
            dma(s_(2), w_ffn[r * P:(r + 1) * P, :], reads=["scr2"], writes=["scr2"])
            quantize(2, 3)
            for kb in range(8):
                em.op(PE, lambda e, kb=kb: e.transpose(
                    bank(5, P), s_(3)[:, kb * P:(kb + 1) * P], ident[:]),
                    reads=["scr3", "ident"], writes=["pb5"])
                em.op(DVE, lambda e, kb=kb, r=r: e.tensor_copy(
                    wdT[:, kb * D + r * P:kb * D + (r + 1) * P], bank(5, P)),
                    reads=["pb5"], writes=["wdT"])

        if dumps:
            for kb in range(8):
                em.op(DVE, lambda e, kb=kb: e.tensor_copy(
                    s_(0), wdT[:, kb * D:(kb + 1) * D]),
                    reads=["wdT"], writes=["scr0"])
                dma(wdt_sp[:, kb * D:(kb + 1) * D], s_(0), reads=["scr0"])

        def layernorm(src_tag, src_ap, dst_ap, dst_tag, eps_col):
            sm, ssq, mcol, rstd = (lns[:, j:j + 1] for j in range(4))
            em.op(DVE, lambda e: e.reduce_sum(sm, src_ap, AX.X),
                  reads=[src_tag], writes=["lns"])
            em.op(ACT, lambda e: e.activation(
                s_(4), src_ap, AF.Square),
                reads=[src_tag], writes=["scr4"])
            em.op(DVE, lambda e: e.reduce_sum(ssq, s_(4), AX.X),
                  reads=["scr4"], writes=["lns"])
            em.op(DVE, lambda e: e.tensor_scalar(
                mcol, sm, 1.0 / D, None, ALU.mult), reads=["lns"], writes=["lns"])
            em.op(DVE, lambda e: e.scalar_tensor_tensor(
                sm, mcol, 1.0, mcol, ALU.mult, ALU.mult),
                reads=["lns"], writes=["lns"])
            em.op(DVE, lambda e: e.tensor_scalar(
                ssq, ssq, 1.0 / D, None, ALU.mult), reads=["lns"], writes=["lns"])
            em.op(DVE, lambda e: e.tensor_tensor(ssq, ssq, sm, ALU.subtract),
                  reads=["lns"], writes=["lns"])
            em.op(ACT, lambda e: e.activation(rstd, ssq, AF.Ln, bias=eps_col),
                  reads=["lns", "kcol"], writes=["lns"])
            em.op(ACT, lambda e: e.activation(rstd, rstd, AF.Exp, scale=-0.5),
                  reads=["lns"], writes=["lns"])
            em.op(DVE, lambda e: e.tensor_scalar(
                dst_ap, src_ap, mcol, rstd, ALU.subtract, ALU.mult),
                reads=[src_tag, "lns"], writes=[dst_tag])

        def transpose_in(src_i, dst, dst_tag, i, width=CH):
            for k in range(NKD):
                slot = k % 4
                em.op(PE, lambda e, k=k, slot=slot: e.transpose(
                    bank(5)[:, slot * P:(slot + 1) * P],
                    s_(src_i)[:, k * P:(k + 1) * P], ident[:]),
                    reads=[f"scr{src_i}", "ident"], writes=["pb5"])
                em.op(DVE, lambda e, k=k, slot=slot: e.tensor_copy(
                    dst[:, k * width + i * P:k * width + (i + 1) * P],
                    bank(5)[:, slot * P:(slot + 1) * P]),
                    reads=["pb5"], writes=[dst_tag])

        def ffn_block(ci, store_out=None, wi=0):
            for i in range(4):
                xw_ap = xch[:, (ci * 4 + i) * D:(ci * 4 + i + 1) * D]
                xwt = f"xch{ci}_{i}"
                layernorm(xwt, xw_ap, s_(2), "scr2", eps1)
                if dumps and store_out is None:
                    T0d = (wi * CPW + ci) * CH
                    dma(xn1_sp[T0d + i * P:T0d + (i + 1) * P, :], s_(2),
                        reads=["scr2"])
                quantize(2, 3)
                if dumps and store_out is None:
                    dma(xd_sp[T0d + i * P:T0d + (i + 1) * P, :], s_(3),
                        reads=["scr3"])
                transpose_in(3, tbuf, "tbuf", i)
            for i in range(4):
                xw_ap = xch[:, (ci * 4 + i) * D:(ci * 4 + i + 1) * D]
                xwt = f"xch{ci}_{i}"
                for k in range(NKD):
                    for h2 in range(2):
                        em.op(PE, lambda e, k=k, i=i, h2=h2: e.matmul(
                            bank(3 + h2),
                            tbuf[:, k * CH + i * P:k * CH + (i + 1) * P],
                            wdT[:, k * D + h2 * 512:k * D + (h2 + 1) * 512],
                            start=(k == 0), stop=(k == NKD - 1),
                            skip_group_check=True),
                            reads=["tbuf", "wdT"], writes=[f"pb{3 + h2}"])
                em.op(DVE, lambda e, xw_ap=xw_ap: e.scalar_tensor_tensor(
                    xw_ap, ps[:, 3 * 512:5 * 512], 0.5, xw_ap, ALU.mult, ALU.add),
                    reads=pb(3, 4) + [xwt], writes=[xwt])
                em.op(DVE, lambda e, xw_ap=xw_ap: e.scalar_tensor_tensor(
                    xw_ap, bfb[:], 0.5, xw_ap, ALU.mult, ALU.add),
                    reads=["bfb", xwt], writes=[xwt])
                if store_out is not None:
                    T0 = (wi * CPW + ci) * CH
                    dma(out_d[T0 + i * P:T0 + (i + 1) * P, :], xw_ap, reads=[xwt])

        for wi in range(NWIN):
            for ci in range(CPW):
                c = wi * CPW + ci
                T0 = c * CH

                for i in range(4):
                    xw_ap = xch[:, (ci * 4 + i) * D:(ci * 4 + i + 1) * D]
                    dma(xw_ap, x_in[T0 + i * P:T0 + (i + 1) * P, :],
                        reads=[f"xch{ci}_{i}"], writes=[f"xch{ci}_{i}"])

                ffn_block(ci)
                if dumps and c == 0:
                    for kk in range(8):
                        em.op(DVE, lambda e, kk=kk: e.tensor_copy(
                            s_(0, CH), tbuf[:, kk * CH:(kk + 1) * CH]),
                            reads=["tbuf"], writes=["scr0"])
                        dma(tb_sp[:, kk * CH:(kk + 1) * CH], s_(0, CH),
                            reads=["scr0"])
                if dumps:
                    for i in range(4):
                        dma(x1_sp[T0 + i * P:T0 + (i + 1) * P, :],
                            xch[:, (ci * 4 + i) * D:(ci * 4 + i + 1) * D],
                            reads=[f"xch{ci}_{i}"])

                for i in range(4):
                    xw_ap = xch[:, (ci * 4 + i) * D:(ci * 4 + i + 1) * D]
                    xwt = f"xch{ci}_{i}"
                    layernorm(xwt, xw_ap, s_(2), "scr2", eps1)
                    ssq2 = lns[:, 4:5]
                    rstd2 = lns[:, 5:6]
                    em.op(ACT, lambda e: e.activation(
                        s_(4), s_(2), AF.Square),
                        reads=["scr2"], writes=["scr4"])
                    em.op(DVE, lambda e, ssq2=ssq2: e.reduce_sum(
                        ssq2, s_(4), AX.X),
                        reads=["scr4"], writes=["lns"])
                    em.op(DVE, lambda e, ssq2=ssq2: e.tensor_scalar(
                        ssq2, ssq2, 1.0 / D, None, ALU.mult),
                        reads=["lns"], writes=["lns"])
                    em.op(ACT, lambda e, ssq2=ssq2, rstd2=rstd2: e.activation(
                        rstd2, ssq2, AF.Ln, bias=eps2),
                        reads=["lns", "kcol"], writes=["lns"])
                    em.op(ACT, lambda e, rstd2=rstd2: e.activation(
                        rstd2, rstd2, AF.Exp, scale=-0.5),
                        reads=["lns"], writes=["lns"])
                    em.op(DVE, lambda e, rstd2=rstd2: e.tensor_scalar(
                        s_(2), s_(2), rstd2, None, ALU.mult),
                        reads=["scr2", "lns"], writes=["scr2"])
                    em.op(DVE, lambda e: e.tensor_tensor(
                        s_(2), s_(2), nwb[:], ALU.mult),
                        reads=["scr2", "nwb"], writes=["scr2"])
                    transpose_in(2, tbuf, "tbuf", i)

                for m in range(32):
                    sl = m % 2
                    wsl = wslab[:, sl * 8 * P:(sl + 1) * 8 * P]
                    dma(wsl, win_bf[m, :, :], reads=[f"wsl{sl}", "win_bf"],
                        writes=[f"wsl{sl}"])
                    for k in range(NKD):
                        em.op(PE, lambda e, k=k, wsl=wsl, sl=sl: e.matmul(
                            bank(sl), wsl[:, k * P:(k + 1) * P],
                            tbuf[:, k * CH:(k + 1) * CH],
                            start=(k == 0), stop=(k == NKD - 1),
                            skip_group_check=True),
                            reads=[f"wsl{sl}", "tbuf"], writes=[f"pb{sl}"])
                    if m >= ND2:
                        em.op(ACT, lambda e, sl=sl: e.activation(
                            ussl[:], bank(sl), AF.Silu),
                            reads=[f"pb{sl}"], writes=["ussl"])
                        dma(sz_sp[m - ND2, :, T0:T0 + CH], ussl[:],
                            reads=["ussl"], writes=[f"szsp{m - ND2}"])
                        continue
                    em.op(DVE, lambda e, m=m: e.tensor_copy(
                        uext[:, 0:3], tails[:, m * 4:m * 4 + 3]),
                        reads=["tails"], writes=["uext"])
                    em.op(DVE, lambda e, sl=sl: e.tensor_copy(
                        uext[:, 3:3 + CH], bank(sl)),
                        reads=[f"pb{sl}"], writes=["uext"])
                    em.op(DVE, lambda e, m=m: e.tensor_copy(
                        tails[:, m * 4:m * 4 + 3], uext[:, CH:CH + 3]),
                        reads=["uext"], writes=["tails"])
                    em.op(DVE, lambda e, m=m: e.tensor_scalar(
                        cvt[:], uext[:, 0:CH], convw[:, m * 4:m * 4 + 1], None,
                        ALU.mult), reads=["uext", "convw"], writes=["cvt"])
                    for j in range(1, 4):
                        em.op(DVE, lambda e, m=m, j=j: e.scalar_tensor_tensor(
                            cvt[:], uext[:, j:j + CH],
                            convw[:, m * 4 + j:m * 4 + j + 1],
                            cvt[:], ALU.mult, ALU.add),
                            reads=["uext", "convw", "cvt"], writes=["cvt"])
                    em.op(ACT, lambda e, m=m: e.activation(
                        ussl[:], cvt[:], AF.Silu, bias=convb[:, m:m + 1]),
                        reads=["cvt", "convb"], writes=["ussl"])
                    dma(us_sp[m, :, T0:T0 + CH], ussl[:], reads=["ussl"],
                        writes=[f"ussp{m}"])
                    em.op(DVE, lambda e: e.tensor_copy(usbf[:], ussl[:]),
                          reads=["ussl"], writes=["usbf"])
                    em.op(PE, lambda e, m=m: e.matmul(
                        bank(2), xpT[:, m * P:(m + 1) * P], usbf[:],
                        start=(m == 0), stop=(m == ND2 - 1),
                        skip_group_check=True),
                        reads=["usbf", "xpT"], writes=["pb2"])

                em.op(DVE, lambda e: e.tensor_copy(spl[:, 0:CH], bank(2)),
                      reads=["pb2", "spl"], writes=["spl"])
                dma(b_sp[:, T0:T0 + CH], spl[0:NST, 0:CH], reads=["spl"],
                    writes=[f"bsp{c}"])
                em.op(DVE, lambda e: e.tensor_copy(
                    cbf[NST:2 * NST, :], spl[NST:2 * NST, 0:CH]),
                    reads=["spl"], writes=["cbf"])
                dma(c_sp[:, T0:T0 + CH], cbf[NST:2 * NST, 0:CH],
                    reads=["cbf"], writes=[f"csp{c}"])
                dma(ussl[0:RK, :], spl[RK:P, 0:CH], reads=["spl", "ussl"],
                    writes=["ussl"])
                em.op(DVE, lambda e: e.tensor_copy(dtr_bf[:], ussl[0:RK, :]),
                      reads=["ussl"], writes=["dtrb"])

                for m in range(ND2):
                    sl = m % 2
                    em.op(PE, lambda e, m=m, sl=sl: e.matmul(
                        bank(sl), dtpT[:, m * P:(m + 1) * P], dtr_bf[:],
                        start=True, stop=True, skip_group_check=True),
                        reads=["dtrb", "dtpT"], writes=[f"pb{sl}"])
                    aa = spl[:, 0:CH]
                    ll = spl[:, CH:2 * CH]
                    em.op(ACT, lambda e, m=m, sl=sl: e.activation(
                        aa, bank(sl), AF.Abs, bias=dtb[:, m:m + 1]),
                        reads=[f"pb{sl}", "dtb"], writes=["spl"])
                    em.op(ACT, lambda e, aa=aa, ll=ll: e.activation(
                        ll, aa, AF.Exp, scale=-1.0), reads=["spl"], writes=["spl"])
                    em.op(ACT, lambda e, ll=ll: e.activation(
                        ll, ll, AF.Ln, bias=onec),
                        reads=["spl", "kcol"], writes=["spl"])
                    em.op(DVE, lambda e, m=m, sl=sl, aa=aa: e.tensor_scalar(
                        aa, bank(sl), dtb[:, m:m + 1], 0.0, ALU.add, ALU.max),
                        reads=[f"pb{sl}", "dtb"], writes=["spl"])
                    em.op(DVE, lambda e, aa=aa, ll=ll: e.tensor_tensor(
                        aa, aa, ll, ALU.add), reads=["spl"], writes=["spl"])
                    dma(dt_sp[m, :, T0:T0 + CH], aa, reads=["spl"], writes=[f"dtsp{m}"])
                    dma(ussl[:], us_sp[m, :, T0:T0 + CH],
                        reads=["ussl", f"ussp{m}"], writes=["ussl"])
                    em.op(DVE, lambda e, aa=aa: e.tensor_tensor(
                        cvt[:], aa, ussl[:], ALU.mult),
                        reads=["spl", "ussl"], writes=["cvt"])
                    dma(dtu_sp[m, :, T0:T0 + CH], cvt[:], reads=["cvt"],
                        writes=[f"dtusp{m}"])

            import os as _os
            _ks = _os.environ.get("KSERIAL", "0")
            if _ks == "1":
                em.serial = True
            elif _ks == "0":
                em.serial = False
            elif _ks.startswith("except:"):
                em.serial = _ks
            else:
                em.serial = _ks.split(",")
            W0 = wi * WIN
            blocks = [(m0, 2) for m0 in range(0, 16, 2)]
            for bi, (m0, bsz) in enumerate(blocks):
                yb = (bi % 2) * 4
                for j in range(bsz):
                    m = m0 + j
                    dma(dtsl[:, j * WIN:(j + 1) * WIN], dt_sp[m, :, W0:W0 + WIN],
                        reads=[f"dtsl{j}", f"dtsp{m}"], writes=[f"dtsl{j}"],
                        ser_tag="sblk")
                    dma(dtusl[:, j * WIN:(j + 1) * WIN],
                        dtu_sp[m, :, W0:W0 + WIN],
                        reads=[f"dtusl{j}", f"dtusp{m}"],
                        writes=[f"dtusl{j}"], ser_tag="sblk")
                fences = []
                for n in range(NST):
                    fences.append({e: em.n[e] - 1 for e in ENGS if em.n[e]})
                    _fd = int(_os.environ.get("KFENCE", "3"))
                    if _fd > 0 and len(fences) >= _fd:
                        em.barrier = fences[-_fd]
                    em.op(SP, lambda e, n=n: e.dma_start(
                        out=bbt[:], in_=b_sp[n:n + 1, W0:W0 + WIN]
                        .broadcast_to([P, WIN])),
                        reads=["bbt", f"bsp{2 * wi}", f"bsp{2 * wi + 1}"],
                        writes=["bbt"], ser_tag="srow")
                    em.op(SP, lambda e, n=n: e.dma_start(
                        out=cbt[:], in_=c_sp[n:n + 1, W0:W0 + WIN]
                        .broadcast_to([P, WIN])),
                        reads=["cbt", f"csp{2 * wi}", f"csp{2 * wi + 1}"],
                        writes=["cbt"], ser_tag="srow")
                    for j in range(bsz):
                        m = m0 + j
                        ii = (n * bsz + j) % 2
                        da = dab[:, ii * WIN:(ii + 1) * WIN]
                        xx = xbb[:, ii * WIN:(ii + 1) * WIN]
                        hh = hbb[:, ii * WIN:(ii + 1) * WIN]
                        hc = hcb[:, j * WIN:(j + 1) * WIN]
                        scol = m * NST + n
                        em.op(ACT, lambda e, j=j, m=m, n=n, da=da: e.activation(
                            da, dtsl[:, j * WIN:(j + 1) * WIN], AF.Exp,
                            scale=a_t[:, m * NST + n:m * NST + n + 1]),
                            reads=[f"dtsl{j}", "a_t"], writes=[f"da{ii}"],
                            ser_tag="sda")
                        em.op(POOL, lambda e, j=j, xx=xx: e.tensor_tensor(
                            xx, dtusl[:, j * WIN:(j + 1) * WIN], bbt[:], ALU.mult),
                            reads=[f"dtusl{j}", "bbt"], writes=[f"xb{ii}"],
                            ser_tag="sxb")
                        em.op(DVE, lambda e, da=da, xx=xx, hh=hh, scol=scol:
                              e.tensor_tensor_scan(
                                  hh, da, xx, hstate[:, scol:scol + 1],
                                  ALU.mult, ALU.add),
                              reads=[f"da{ii}", f"xb{ii}", "hstate"],
                              writes=[f"hb{ii}"], ser_tag="sscan")
                        em.op(DVE, lambda e, hh=hh, scol=scol: e.tensor_copy(
                            hstate[:, scol:scol + 1], hh[:, WIN - 1:WIN]),
                            reads=[f"hb{ii}"], writes=["hstate"],
                            ser_tag="shend")
                        em.op(DVE, lambda e, hh=hh, hc=hc: e.tensor_tensor(
                            hc, hh, cbt[:], ALU.mult),
                            reads=[f"hb{ii}", "cbt"], writes=[f"hc{j}"],
                            ser_tag="shc")
                        for h2 in range(2):
                            em.op(PE, lambda e, j=j, hc=hc, h2=h2, n=n, yb=yb:
                                  e.matmul(
                                      ps[:, (yb + 2 * j + h2) * 512:
                                         (yb + 2 * j + h2 + 1) * 512],
                                      identb[:], hc[:, h2 * 512:(h2 + 1) * 512],
                                      start=(n == 0), stop=(n == NST - 1),
                                      skip_group_check=True),
                                  reads=[f"hc{j}", "identb"],
                                  writes=[f"pb{yb + 2 * j + h2}"], ser_tag="symm")
                em.barrier = None
                for j in range(bsz):
                    m = m0 + j
                    uD = posts[:, 0:WIN]
                    sz = posts[:, WIN:2 * WIN]
                    dma(uD, us_sp[m, :, W0:W0 + WIN],
                        reads=["posts", f"ussp{m}"], writes=["posts"],
                        ser_tag="spost")
                    dma(sz, sz_sp[m, :, W0:W0 + WIN],
                        reads=["posts2", f"szsp{m}"], writes=["posts2"],
                        ser_tag="spost")
                    em.op(DVE, lambda e, uD=uD, m=m: e.tensor_scalar(
                        uD, uD, dpar[:, m:m + 1], None, ALU.mult),
                        reads=["posts", "dpar"], writes=["posts"],
                        ser_tag="spost")
                    em.op(DVE, lambda e, j=j, uD=uD, yb=yb: e.tensor_tensor(
                        uD, ps[:, (yb + 2 * j) * 512:(yb + 2 * j + 2) * 512],
                        uD, ALU.add),
                        reads=pb(yb + 2 * j, yb + 2 * j + 1) + ["posts"],
                        writes=["posts"], ser_tag="spost")
                    em.op(DVE, lambda e, uD=uD, sz=sz: e.tensor_tensor(
                        y3b[:], uD, sz, ALU.mult),
                        reads=["posts", "posts2"], writes=["y3b"],
                        ser_tag="spost")
                    dma(y3_sp[m, :, W0:W0 + WIN], y3b[:], reads=["y3b"],
                        writes=[f"y3sp{m}"], ser_tag="spost")

            em.serial = False
            if _os.environ.get("KSTAGEB", "0") == "1":
                em.barrier = {e: em.n[e] - 1 for e in ENGS if em.n[e]}
            for ti in range(8):
                ci, i = divmod(ti, 4)
                tsl = 0
                yti = y3ti[:, :]
                dma(yti.rearrange("p (m c) -> p m c", c=P),
                    y3_sp[:, :, W0 + ti * P:W0 + (ti + 1) * P].rearrange(
                        "m p c -> p m c"),
                    reads=[f"y3ti{tsl}"] + [f"y3sp{mm}" for mm in range(ND2)],
                    writes=[f"y3ti{tsl}"])
                for k in range(16):
                    sl = k % 2
                    wsl_ap = woslab[:, sl * D:(sl + 1) * D]
                    dma(wsl_ap, wout_bf[k, :, :], reads=[f"wo{sl}", "wout_bf"],
                        writes=[f"wo{sl}"])
                    for h2 in range(2):
                        em.op(PE, lambda e, k=k, h2=h2, yti=yti, wsl_ap=wsl_ap,
                              tsl=tsl: e.matmul(
                                  bank(3 + h2),
                                  yti[:, k * P:(k + 1) * P],
                                  wsl_ap[:, h2 * 512:(h2 + 1) * 512],
                                  start=(k == 0), stop=(k == 15),
                                  skip_group_check=True),
                              reads=[f"y3ti{tsl}", f"wo{sl}"],
                              writes=[f"pb{3 + h2}"])
                xw_ap = xch[:, ti * D:(ti + 1) * D]
                xwt = f"xch{ci}_{i}"
                layernorm(xwt, xw_ap, s_(2), "scr2", eps1)
                em.op(DVE, lambda e, xw_ap=xw_ap: e.tensor_tensor(
                    xw_ap, s_(2), xw_ap, ALU.add),
                    reads=["scr2", xwt], writes=[xwt])
                em.op(DVE, lambda e, xw_ap=xw_ap: e.tensor_tensor(
                    xw_ap, ps[:, 3 * 512:5 * 512], xw_ap, ALU.add),
                    reads=pb(3, 4) + [xwt], writes=[xwt])

            for ci in range(CPW):
                ffn_block(ci, store_out=True, wi=wi)


        em.finalize()
        streams = {"sp": [], "pe": [], "act": [], "dve": [], "pool": []}
        for eng, fn, waits, seq in em.ops:
            stream = "pool" if eng in ("pool", "pq") else eng
            streams[stream].append((eng, fn, waits, seq))

        def play(stream_name, handle):
            waited = {}
            for eng, fn, waits, seq in streams[stream_name]:
                for we, ws in waits.items():
                    val = em.rank[we][ws]
                    if we in ("sp", "pq"):
                        val *= 16
                    if val > 0 and waited.get(we, -1) < val:
                        handle.wait_ge(sems[we], val)
                        waited[we] = val
                inst = fn(handle)
                if seq in em.needed[eng]:
                    inst.then_inc(sems[eng], 16 if eng in ("sp", "pq") else 1)
            if stream_name == "sp":
                handle.wait_ge(sems["sp"], 16 * em.ninc["sp"])
                if em.ninc["pq"]:
                    handle.wait_ge(sems["pq"], 16 * em.ninc["pq"])

        @block.sync
        def _(sync):
            play("sp", sync)

        @block.tensor
        def _(t):
            play("pe", t)

        @block.scalar
        def _(s):
            play("act", s)

        @block.vector
        def _(v):
            play("dve", v)

        @block.gpsimd
        def _(g):
            play("pool", g)

    return nc


_NC_CACHE = {}


def kernel(L0_attn_x, L0_attn_norm_w, L0_attn_in_proj_w, L0_attn_conv_w,
           L0_attn_conv_b, L0_attn_x_proj_w, L0_attn_dt_proj_w, L0_attn_dt_proj_b,
           L0_attn_A_log, L0_attn_D_param, L0_attn_out_proj_w, L0_ffn_weight,
           L0_ffn_bias):
    B = L0_attn_x.shape[0]
    f32 = np.float32

    def lay(a):
        return np.ascontiguousarray(np.asarray(a), dtype=f32)

    w_in_g = lay(np.asarray(L0_attn_in_proj_w).T
                 .reshape(8, 128, 32, 128).transpose(2, 0, 1, 3))
    w_out_g = lay(np.asarray(L0_attn_out_proj_w).T.reshape(16, 128, 1024))
    perm = list(range(64, 96)) + list(range(96, 128)) + list(range(0, 64))
    w_xp_g = lay(np.asarray(L0_attn_x_proj_w)[perm].T.reshape(16, 128, 128))
    w_dtp = lay(np.asarray(L0_attn_dt_proj_w).T)
    w_ffn = lay(L0_ffn_weight)
    b_ffn = lay(np.asarray(L0_ffn_bias).reshape(1, 1024))
    nrmw = lay(np.asarray(L0_attn_norm_w).reshape(1, 1024))
    conv_w_l = lay(np.asarray(L0_attn_conv_w).reshape(16, 128, 4)
                   .transpose(1, 0, 2).reshape(128, 64))
    conv_b_l = lay(np.asarray(L0_attn_conv_b).reshape(16, 128).T)
    dtb_l = lay(np.asarray(L0_attn_dt_proj_b).reshape(16, 128).T)
    d_l = lay(np.asarray(L0_attn_D_param).reshape(16, 128).T)
    alog_l = lay(np.asarray(L0_attn_A_log).reshape(16, 128, 32)
                 .transpose(1, 0, 2).reshape(128, 512))

    if "nc" not in _NC_CACHE:
        _NC_CACHE["nc"] = build_core_program()
    nc = _NC_CACHE["nc"]

    shared = {
        "w_in_g": w_in_g, "w_out_g": w_out_g, "w_xp_g": w_xp_g, "w_dtp": w_dtp,
        "w_ffn": w_ffn, "b_ffn": b_ffn, "nrmw": nrmw, "conv_w_l": conv_w_l,
        "conv_b_l": conv_b_l, "dtb_l": dtb_l, "d_l": d_l, "alog_l": alog_l,
    }
    in_maps = []
    for b in range(B):
        m = dict(shared)
        m["x"] = lay(L0_attn_x[b])
        in_maps.append(m)

    res = run_bass_kernel_spmd(nc, in_maps, core_ids=list(range(B)))
    out = np.stack([res.results[b]["out"] for b in range(B)], axis=0)
    return out.astype(np.float32)
